# version 04
# baseline (speedup 1.0000x reference)
import numpy as np
import ml_dtypes
import concourse.bass as bass
import concourse.mybir as mybir
from concourse.bass_utils import run_bass_kernel_spmd
from concourse.bass import IndirectOffsetOnAxis
from contextlib import ExitStack
import os

_PHASES = int(os.environ.get("KPHASES", "99"))
_DEBUG = int(os.environ.get("KDEBUG", "0"))

F32 = mybir.dt.float32
BF16 = mybir.dt.bfloat16
AF = mybir.ActivationFunctionType
ALU = mybir.AluOpType
AX = mybir.AxisListType

NCORES = 8
S = 8192
D = 2048
NKT = D // 128
TOK = S // NCORES
HPG = 4
NE, TOPK, FE, FSH = 32, 6, 768, 1536
EPS = 1e-5
QW = 512
NQC = 16
SKT = S // 128
TCAP = 384


class Q:

    def __init__(self, eng, name, sem, inc):
        self.eng, self.name, self.sem, self.inc = eng, name, sem, inc
        self.n = 0
        self.waited = {}

    def wait(self, token):
        if token is None:
            return
        name, cnt, sem = token
        if self.waited.get(name, -1) >= cnt:
            return
        self.eng.wait_ge(sem, cnt)
        self.waited[name] = cnt

    def self_wait(self):
        self.wait((self.name, self.n, self.sem))

    def emit(self, method, *args, deps=(), inc=None, **kw):
        for d in deps:
            self.wait(d)
        instr = getattr(self.eng, method)(*args, **kw)
        step = self.inc if inc is None else inc
        instr.then_inc(self.sem, step)
        self.n += step
        return (self.name, self.n, self.sem)

    def tok(self):
        return (self.name, self.n, self.sem)


class Buf:

    def __init__(self):
        self.w = None
        self.r = []

    def wdeps(self):
        return ([self.w] if self.w else []) + self.r

    def rdeps(self):
        return [self.w] if self.w else []

    def wr(self, tok):
        self.w = tok
        self.r = []

    def rd(self, tok):
        self.r.append(tok)


def build_program():
    nc = bass.Bass("TRN2", target_bir_lowering=False)
    dt = nc.dram_tensor
    xT_own = dt("xT_own", [D, TOK], F32, kind="ExternalInput").ap()
    wqp = dt("wqp", [D, 128], F32, kind="ExternalInput").ap()
    wkp = dt("wkp", [D, 128], F32, kind="ExternalInput").ap()
    wv = dt("wv", [D, 32], F32, kind="ExternalInput").ap()
    vemb0p = dt("vemb0p", [128, 1], F32, kind="ExternalInput").ap()
    vemb1p = dt("vemb1p", [128, 1], F32, kind="ExternalInput").ap()
    msel = dt("msel", [128, NCORES], F32, kind="ExternalInput").ap()
    wopad = dt("wopad", [16, 8 * 128, 128], F32, kind="ExternalInput").ap()
    wgate = dt("wgate", [16, D, 128], F32, kind="ExternalInput").ap()
    gate_w = dt("gate_w", [D, NE], F32, kind="ExternalInput").ap()
    wgm = dt("wgm", [NE, 6, D, 128], F32, kind="ExternalInput").ap()
    wum = dt("wum", [NE, 6, D, 128], F32, kind="ExternalInput").ap()
    wdp = dt("wdp", [NE, FE, D], F32, kind="ExternalInput").ap()
    sgm = dt("sgm", [12, D, 128], F32, kind="ExternalInput").ap()
    sum_m = dt("sum_m", [12, D, 128], F32, kind="ExternalInput").ap()
    sdm = dt("sdm", [16, FSH, 128], F32, kind="ExternalInput").ap()
    ident = dt("ident", [128, 128], F32, kind="ExternalInput").ap()
    ones = dt("ones", [128, 128], F32, kind="ExternalInput").ap()
    sel = dt("sel", [128, 128], F32, kind="ExternalInput").ap()
    bd = dt("bd", [128, HPG], BF16, kind="ExternalInput").ap()
    padmask = dt("padmask", [128, 1], F32, kind="ExternalInput").ap()
    cbsel = dt("cbsel", [NE, NE * 128], BF16, kind="ExternalInput").ap()
    masks = dt("masks", [128, 4 * HPG * QW], BF16, kind="ExternalInput").ap()
    tokid = dt("tokid", [128, 8], mybir.dt.int32, kind="ExternalInput").ap()
    eoff = dt("eoff", [NE, 1], F32, kind="ExternalInput").ap()
    ln1_own_d = dt("ln1_own_d", [D, TOK], BF16, kind="Internal").ap()
    ln1_all = dt("ln1_all", [NCORES * D, TOK], BF16, kind="Internal",
                 addr_space="Shared").ap()
    attn_own_d = dt("attn_own_d", [128, S], BF16, kind="Internal").ap()
    attn_all = dt("attn_all", [NCORES * 128, S], BF16, kind="Internal",
                  addr_space="Shared").ap()
    ln2tok_d = dt("ln2tok_d", [TOK, D], BF16, kind="Internal").ap()
    comb_tok_d = dt("comb_tok_d", [TOK, NE], F32, kind="Internal").ap()
    idx_d = dt("idx_d", [NE, TCAP], mybir.dt.int32, kind="Internal").ap()
    stage_d = dt("stage_d", [2, TCAP, D], BF16, kind="Internal").ap()
    racc = [dt(f"racc{i}", [TOK, D], F32, kind="Internal").ap() for i in range(4)]
    y = dt("y", [TOK, D], F32, kind="ExternalOutput").ap()
    if _DEBUG:
        d_ln1 = dt("d_ln1", [128, NKT * TOK], F32, kind="ExternalOutput").ap()
        d_attn = dt("d_attn", [128, S], F32, kind="ExternalOutput").ap()
        d_xmid = dt("d_xmid", [128, NKT * TOK], F32, kind="ExternalOutput").ap()
        d_ln2 = dt("d_ln2", [128, NKT * TOK], F32, kind="ExternalOutput").ap()
        d_comb = dt("d_comb", [NE, TOK], F32, kind="ExternalOutput").ap()
        d_atn = dt("d_atn", [128, 8 * TOK], F32, kind="ExternalOutput").ap()
        d_idx = dt("d_idx", [NE, TCAP], mybir.dt.int32, kind="ExternalOutput").ap()
        d_xg = dt("d_xg", [128, NKT * TCAP], F32, kind="ExternalOutput").ap()

    with ExitStack() as ctx:
        EE = ctx.enter_context
        TE = Q(nc.tensor, "te", EE(nc.semaphore("s_te")), 1)
        SC = Q(nc.scalar, "sc", EE(nc.semaphore("s_sc")), 1)
        VE = Q(nc.vector, "ve", EE(nc.semaphore("s_ve")), 1)
        SY = Q(nc.sync, "sy", EE(nc.semaphore("s_sy")), 16)
        GP = Q(nc.gpsimd, "gp", EE(nc.semaphore("s_gp")), 16)
        sem_ag1 = EE(nc.semaphore("s_ag1"))
        sem_ag2 = EE(nc.semaphore("s_ag2"))
        r_btcap = nc.gpsimd.to_reg(NE * TCAP)
        r_btok = nc.gpsimd.to_reg(TOK)
        r_b2tok = nc.gpsimd.to_reg(2 * TOK)
        ALL = (TE, SC, VE, SY, GP)

        def barrier():
            toks = [q.tok() for q in ALL]
            for q in ALL:
                for t in toks:
                    q.wait(t)

        xT = EE(nc.sbuf_tensor("xT", [128, NKT * TOK], F32))
        id_sb = EE(nc.sbuf_tensor("id_sb", [128, 128], F32))
        on_sb = EE(nc.sbuf_tensor("on_sb", [128, 128], F32))
        sel_sb = EE(nc.sbuf_tensor("sel_sb", [128, 128], F32))
        bd_sb = EE(nc.sbuf_tensor("bd_sb", [128, HPG], BF16))
        pm_sb = EE(nc.sbuf_tensor("pm_sb", [128, 1], F32))
        ve0_sb = EE(nc.sbuf_tensor("ve0_sb", [128, 1], F32))
        ve1_sb = EE(nc.sbuf_tensor("ve1_sb", [128, 1], F32))
        vd_sb = EE(nc.sbuf_tensor("vd_sb", [128, 1], F32))
        ms_sb = EE(nc.sbuf_tensor("ms_sb", [128, NCORES], F32))

        t_x = SY.emit("dma_start", out=xT.rearrange("p (k t) -> p k t", k=NKT),
                      in_=xT_own.rearrange("(k p) t -> p k t", p=128))
        SY.emit("dma_start", out=id_sb[:], in_=ident)
        SY.emit("dma_start", out=on_sb[:], in_=ones)
        SY.emit("dma_start", out=sel_sb[:], in_=sel)
        SY.emit("dma_start", out=bd_sb[:], in_=bd)
        SY.emit("dma_start", out=pm_sb[:], in_=padmask)
        SY.emit("dma_start", out=ve0_sb[:], in_=vemb0p)
        t_ve = SY.emit("dma_start", out=ve1_sb[:], in_=vemb1p)
        t_msel = SY.emit("dma_start", out=ms_sb[:], in_=msel)
        t_vd = VE.emit("tensor_sub", vd_sb[:], ve1_sb[:], ve0_sb[:], deps=[t_ve])
        t_const = SY.tok()

        def layernorm(dst_bf16, fp32_consumer=None, name=""):
            with ExitStack() as c2:
                E2 = c2.enter_context
                sq = E2(nc.sbuf_tensor(f"sq{name}", [128, 2 * TOK], F32))
                stat = E2(nc.sbuf_tensor(f"stat{name}", [1, 4 * TOK], F32))
                lnf = E2(nc.sbuf_tensor(f"lnf{name}", [128, 2 * TOK], F32))
                mb_sb = E2(nc.sbuf_tensor(f"mb{name}", [128, TOK], F32))
                rb_sb = E2(nc.sbuf_tensor(f"rb{name}", [128, TOK], F32))
                with ExitStack() as c3:
                    E3 = c3.enter_context
                    ps_sum = E3(nc.psum_tensor(f"pssum{name}", [1, TOK], F32))
                    ps_ssq = E3(nc.psum_tensor(f"psssq{name}", [1, TOK], F32))
                    t_sq = [None, None]
                    t_ssqmm = None
                    for kt in range(NKT):
                        xkt = xT[:, TOK * kt:TOK * (kt + 1)]
                        sqb = sq[:, TOK * (kt % 2):TOK * (kt % 2 + 1)]
                        t_sq[kt % 2] = SC.emit("activation", sqb, xkt, AF.Square,
                                               deps=[t_x] + ([t_ssqmm] if kt >= 2 else []))
                        for h in range(2):
                            TE.emit("matmul", ps_sum[:, 512 * h:512 * (h + 1)],
                                    on_sb[:, 0:1], xkt[:, 512 * h:512 * (h + 1)],
                                    start=(kt == 0), stop=(kt == NKT - 1),
                                    deps=[t_x, t_const])
                        for h in range(2):
                            t_ssqmm = TE.emit("matmul", ps_ssq[:, 512 * h:512 * (h + 1)],
                                              on_sb[:, 0:1], sqb[:, 512 * h:512 * (h + 1)],
                                              start=(kt == 0), stop=(kt == NKT - 1),
                                              deps=[t_sq[kt % 2]])
                    t_summ = TE.tok()
                    VE.emit("tensor_scalar_mul", stat[:, 0:TOK], ps_sum[:], 1.0 / D,
                            deps=[t_summ])
                    VE.self_wait()
                    VE.emit("tensor_tensor", stat[:, TOK:2 * TOK], stat[:, 0:TOK],
                            stat[:, 0:TOK], ALU.mult)
                    VE.emit("tensor_scalar_mul", stat[:, 2 * TOK:3 * TOK], ps_ssq[:],
                            1.0 / D, deps=[t_summ])
                    VE.self_wait()
                    VE.emit("tensor_sub", stat[:, 2 * TOK:3 * TOK],
                            stat[:, 2 * TOK:3 * TOK], stat[:, TOK:2 * TOK])
                    VE.self_wait()
                    t_var = VE.emit("tensor_scalar_add", stat[:, 2 * TOK:3 * TOK],
                                    stat[:, 2 * TOK:3 * TOK], EPS)
                    VE.self_wait()
                    t_ln = SC.emit("activation", stat[:, 3 * TOK:4 * TOK],
                                   stat[:, 2 * TOK:3 * TOK], AF.Ln, deps=[t_var])
                    t_half = VE.emit("tensor_scalar_mul", stat[:, 3 * TOK:4 * TOK],
                                     stat[:, 3 * TOK:4 * TOK], -0.5, deps=[t_ln])
                    VE.self_wait()
                    t_rs = SC.emit("activation", stat[:, 3 * TOK:4 * TOK],
                                   stat[:, 3 * TOK:4 * TOK], AF.Exp, deps=[t_half])
                with ExitStack() as c3:
                    E3 = c3.enter_context
                    ps_mb = E3(nc.psum_tensor(f"psmb{name}", [128, TOK], F32))
                    ps_rb = E3(nc.psum_tensor(f"psrb{name}", [128, TOK], F32))
                    TE.self_wait()
                    for h in range(2):
                        TE.emit("matmul", ps_mb[:, 512 * h:512 * (h + 1)], on_sb[0:1, :],
                                stat[0:1, 512 * h:512 * (h + 1)], start=True, stop=True,
                                deps=[t_rs])
                    for h in range(2):
                        t_bc = TE.emit("matmul", ps_rb[:, 512 * h:512 * (h + 1)],
                                       on_sb[0:1, :],
                                       stat[0:1, 3 * TOK + 512 * h:3 * TOK + 512 * (h + 1)],
                                       start=True, stop=True)
                    VE.emit("tensor_copy", mb_sb[:], ps_mb[:], deps=[t_bc])
                    t_rbc = VE.emit("tensor_copy", rb_sb[:], ps_rb[:])
                    VE.self_wait()
                prev = [None, None]
                for kt in range(NKT):
                    xkt = xT[:, TOK * kt:TOK * (kt + 1)]
                    lbuf = lnf[:, TOK * (kt % 2):TOK * (kt % 2 + 1)]
                    dep_reuse = list(prev[kt % 2] or ())
                    VE.emit("tensor_sub", lbuf, xkt, mb_sb[:],
                            deps=[t_rbc] + [d for d in dep_reuse if d])
                    VE.self_wait()
                    t_n = VE.emit("tensor_tensor", lbuf, lbuf, rb_sb[:], ALU.mult)
                    VE.self_wait()
                    t_cast = SC.emit("activation", dst_bf16[:, TOK * kt:TOK * (kt + 1)],
                                     lbuf, AF.Copy, deps=[t_n])
                    t_cons = fp32_consumer(kt, lbuf, t_n) if fp32_consumer else None
                    prev[kt % 2] = (t_cast, t_cons)
                barrier()

        with ExitStack() as cA:
            EA = cA.enter_context
            ln1o = EA(nc.sbuf_tensor("ln1o", [128, NKT * TOK], BF16))

            layernorm(ln1o, name="l1")
            t_st1 = GP.emit("dma_start",
                            out=ln1_own_d.rearrange("(k p) t -> p k t", p=128),
                            in_=ln1o.rearrange("p (k t) -> p k t", k=NKT))
            GP.wait(t_st1)
            nc.gpsimd.collective_compute("AllGather", ALU.bypass,
                                         [list(range(NCORES))], [ln1_own_d],
                                         [ln1_all]).then_inc(sem_ag1, 1)
            t_ag1 = ("ag1", 1, sem_ag1)
            if _DEBUG:
                GP.emit("dma_start", out=d_ln1, in_=ln1o[:], deps=[t_st1])

            with ExitStack() as cB:
                EB = cB.enter_context
                msk_sb = EB(nc.sbuf_tensor("msk_sb", [128, 4 * HPG * QW], BF16))
                pq_pad = EB(nc.sbuf_tensor("pq_pad", [128, S], BF16))
                pk_pad = EB(nc.sbuf_tensor("pk_pad", [128, S], BF16))
                pv_pad = EB(nc.sbuf_tensor("pv_pad", [128, SKT * HPG * 9], BF16))
                t_mskl = SY.emit("dma_start", out=msk_sb[:], in_=masks)

                if _PHASES <= 0:
                    return nc
                with ExitStack() as c2:
                    E2 = c2.enter_context
                    lns = E2(nc.sbuf_tensor("lns", [128, NKT * 512], BF16))
                    wq_sb = E2(nc.sbuf_tensor("wq_sb", [128, NKT * 128], BF16))
                    wk_sb = E2(nc.sbuf_tensor("wk_sb", [128, NKT * 128], BF16))
                    wv_sb = E2(nc.sbuf_tensor("wv_sb", [128, NKT * 32], BF16))
                    prq = E2(nc.sbuf_tensor("prq", [128, 512], BF16))
                    esk = E2(nc.sbuf_tensor("esk", [128, HPG], F32))
                    ps_q = [E2(nc.psum_tensor(f"ps_q{i}", [128, 512], F32)) for i in range(2)]
                    ps_k = [E2(nc.psum_tensor(f"ps_k{i}", [128, 512], F32)) for i in range(2)]
                    ps_v = [E2(nc.psum_tensor(f"ps_v{i}", [128, 32], F32)) for i in range(2)]
                    ps_sk = E2(nc.psum_tensor("ps_sk", [128, HPG], F32))
                    bq = [Buf(), Buf()]
                    bk = [Buf(), Buf()]
                    bv = [Buf(), Buf()]
                    bsk, blns, bprq, besk = Buf(), Buf(), Buf(), Buf()

                    t_wq = GP.emit("dma_start",
                                   out=wq_sb.rearrange("p (k c) -> p k c", k=NKT),
                                   in_=wqp.rearrange("(k p) c -> p k c", p=128))
                    t_wk = GP.emit("dma_start",
                                   out=wk_sb.rearrange("p (k c) -> p k c", k=NKT),
                                   in_=wkp.rearrange("(k p) c -> p k c", p=128))
                    t_wv = GP.emit("dma_start",
                                   out=wv_sb.rearrange("p (k c) -> p k c", k=NKT),
                                   in_=wv.rearrange("(k p) c -> p k c", p=128))
                    t_pv1 = VE.emit(
                        "memset",
                        pv_pad.rearrange("p (k h n) -> p k h n", k=SKT, h=HPG)[:, :, :, 8:9],
                        1.0)
                    ln1v = ln1_all.rearrange("(r k p) t -> r k p t", r=NCORES, k=NKT)
                    for ch in range(16):
                        r, s0 = ch // 2, 512 * (ch % 2)
                        t_ld = SY.emit("dma_start",
                                       out=lns.rearrange("p (k t) -> p k t", k=NKT),
                                       in_=ln1v[r, :, :, s0:s0 + 512].rearrange(
                                           "k p t -> p k t"),
                                       deps=[t_ag1] + blns.wdeps())
                        blns.wr(t_ld)
                        pq_, bq_ = ps_q[ch % 2], bq[ch % 2]
                        for kt in range(NKT):
                            t_mm = TE.emit("matmul", pq_[:],
                                           wq_sb[:, 128 * kt:128 * (kt + 1)],
                                           lns[:, 512 * kt:512 * (kt + 1)],
                                           start=(kt == 0), stop=(kt == NKT - 1),
                                           deps=[t_ld, t_wq] + (bq_.wdeps() if kt == 0 else []))
                        bq_.wr(t_mm)
                        blns.rd(t_mm)
                        t_sg = SC.emit("activation", prq[:], pq_[:], AF.Sigmoid,
                                       deps=[t_mm] + bprq.wdeps())
                        bq_.rd(t_sg)
                        bprq.wr(t_sg)
                        t_pq = VE.emit("tensor_scalar_mul",
                                       pq_pad[:, 512 * ch:512 * (ch + 1)], prq[:],
                                       pm_sb[:, 0:1], deps=[t_sg, t_const])
                        bprq.rd(t_pq)
                        pk_, bk_ = ps_k[ch % 2], bk[ch % 2]
                        for kt in range(NKT):
                            t_mm = TE.emit("matmul", pk_[:],
                                           wk_sb[:, 128 * kt:128 * (kt + 1)],
                                           lns[:, 512 * kt:512 * (kt + 1)],
                                           start=(kt == 0), stop=(kt == NKT - 1),
                                           deps=[t_wk] + (bk_.wdeps() if kt == 0 else []))
                        bk_.wr(t_mm)
                        blns.rd(t_mm)
                        t_pk = SC.emit("activation", pk_pad[:, 512 * ch:512 * (ch + 1)],
                                       pk_[:], AF.Sigmoid, deps=[t_mm])
                        bk_.rd(t_pk)
                        for tt in range(4):
                            kk = 4 * ch + tt
                            pv_, bv_ = ps_v[tt % 2], bv[tt % 2]
                            for kt in range(NKT):
                                t_mm = TE.emit(
                                    "matmul", pv_[:],
                                    lns[:, 512 * kt + 128 * tt:512 * kt + 128 * (tt + 1)],
                                    wv_sb[:, 32 * kt:32 * (kt + 1)],
                                    start=(kt == 0), stop=(kt == NKT - 1),
                                    deps=[t_wv] + (bv_.wdeps() if kt == 0 else []))
                            bv_.wr(t_mm)
                            blns.rd(t_mm)
                            pvv = pv_pad.rearrange("p (k h n) -> p k h n", k=SKT, h=HPG)
                            t_ev = SC.emit("activation", pvv[:, kk, :, 0:8],
                                           pv_.ap().rearrange("p (h n) -> p h n", h=HPG),
                                           AF.Sigmoid, deps=[t_mm, t_pv1])
                            bv_.rd(t_ev)
                            TE.self_wait()
                            t_skm = TE.emit("matmul", ps_sk[:],
                                            pk_pad[:, 128 * kk:128 * (kk + 1)], bd_sb[:],
                                            start=True, stop=True,
                                            deps=[t_pk] + bsk.wdeps())
                            bsk.wr(t_skm)
                            t_esk = SC.emit("activation", esk[:], ps_sk[:], AF.Exp,
                                            scale=-1.0, deps=[t_skm] + besk.wdeps())
                            bsk.rd(t_esk)
                            besk.wr(t_esk)
                            for h in range(HPG):
                                pvh = pvv[:, kk, h, 0:9]
                                t_sc9 = VE.emit("tensor_scalar_mul", pvh, pvh,
                                                esk[:, h:h + 1], deps=[t_esk, t_ev])
                                besk.rd(t_sc9)
                            VE.self_wait()
                barrier()

                if _PHASES <= 1:
                    return nc
                with ExitStack() as c2:
                    E2 = c2.enter_context
                    ps_dot = E2(nc.psum_tensor("ps_dot", [128, HPG * QW], F32))
                    ps_nd = E2(nc.psum_tensor("ps_nd", [128, QW], F32))
                    E_sb = E2(nc.sbuf_tensor("E_sb", [128, 2 * HPG * QW], BF16))
                    num_sb = E2(nc.sbuf_tensor("num_sb", [128, QW], F32))
                    rec_sb = E2(nc.sbuf_tensor("rec_sb", [128, QW], F32))
                    att_sb = E2(nc.sbuf_tensor("att_sb", [128, QW], F32))
                    atb_sb = E2(nc.sbuf_tensor("atb_sb", [128, QW], BF16))
                    bdh = [Buf(), Buf()]
                    bE = [Buf(), Buf()]
                    bnd, bnsb, brec, batt, batb = (Buf() for _ in range(5))

                    t_z = VE.emit("memset", ps_nd[:], 0.0)
                    bnd.wr(t_z)
                    t_stc = None
                    git = 0
                    for j in range(NQC):
                        q0 = QW * j
                        Kj = 4 * (j + 1)
                        pend = None

                        def emit_av(pkk, pp, pw, last):
                            TE.self_wait()
                            for h in range(HPG):
                                t_av = TE.emit(
                                    "matmul", ps_nd[32 * h:32 * h + 9, :],
                                    pv_pad[:, (HPG * pkk + h) * 9:(HPG * pkk + h) * 9 + 9],
                                    E_sb[:, pp * HPG * QW + QW * h:pp * HPG * QW + QW * h + QW],
                                    start=(pkk == 0), stop=last,
                                    tile_position=(0, 32 * h), skip_group_check=True,
                                    deps=([pw] + (bnd.wdeps() if pkk == 0 else []))
                                    if h == 0 else [])
                            bnd.wr(t_av)
                            bE[pp].rd(t_av)
                            TE.self_wait()
                            return t_av

                        for kk in range(Kj):
                            par = git % 2
                            for half in range(2):
                                for h in (2 * half, 2 * half + 1):
                                    t_dot = TE.emit(
                                        "matmul", ps_dot[:, QW * h:QW * (h + 1)],
                                        pk_pad[32 * h:32 * h + 32, 128 * kk:128 * (kk + 1)],
                                        pq_pad[32 * h:32 * h + 32, q0:q0 + QW],
                                        start=True, stop=True, tile_position=(32 * h, 0),
                                        deps=(bdh[half].wdeps() if h == 2 * half else []))
                                bdh[half].wr(t_dot)
                            t_exp = None
                            for half in range(2):
                                Ebh = E_sb[:, par * HPG * QW + 2 * QW * half:
                                           par * HPG * QW + 2 * QW * (half + 1)]
                                t_exp = SC.emit(
                                    "activation", Ebh,
                                    ps_dot[:, 2 * QW * half:2 * QW * (half + 1)],
                                    AF.Exp, scale=2.0,
                                    deps=[bdh[half].w] + (bE[par].wdeps() if half == 0 else []))
                                bdh[half].rd(t_exp)
                                if half == 0:
                                    bE[par].wr(t_exp)
                            bE[par].wr(t_exp)
                            dpat = kk - 4 * j
                            if 0 <= dpat <= 3:
                                moff = dpat * HPG * QW
                                t_m = VE.emit("tensor_tensor",
                                              E_sb[:, par * HPG * QW:(par + 1) * HPG * QW],
                                              E_sb[:, par * HPG * QW:(par + 1) * HPG * QW],
                                              msk_sb[:, moff:moff + HPG * QW], ALU.mult,
                                              deps=[t_exp, t_mskl])
                                bE[par].wr(t_m)
                            if pend is not None:
                                emit_av(pend[0], pend[1], pend[2], False)
                            pend = (kk, par, bE[par].w)
                            git += 1
                        t_av = emit_av(pend[0], pend[1], pend[2], True)
                        t_evn = VE.emit("tensor_copy", num_sb[:], ps_nd[:],
                                        deps=[t_av] + bnsb.wdeps())
                        bnd.rd(t_evn)
                        bnsb.wr(t_evn)
                        VE.self_wait()
                        TE.self_wait()
                        t_den = TE.emit("matmul", ps_nd[:], sel_sb[:], num_sb[:],
                                        start=True, stop=True,
                                        deps=[t_evn] + bnd.wdeps())
                        bnsb.rd(t_den)
                        bnd.wr(t_den)
                        TE.self_wait()
                        t_rc = VE.emit("reciprocal", rec_sb[:], ps_nd[:],
                                       deps=[t_den] + brec.wdeps())
                        bnd.rd(t_rc)
                        brec.wr(t_rc)
                        VE.self_wait()
                        t_at = VE.emit("tensor_tensor", att_sb[:], num_sb[:],
                                       rec_sb[:], ALU.mult, deps=batt.wdeps())
                        bnsb.rd(t_at)
                        brec.rd(t_at)
                        batt.wr(t_at)
                        VE.self_wait()
                        t_ab = VE.emit("tensor_scalar", atb_sb[:], att_sb[:],
                                       vd_sb[:, 0:1], ve0_sb[:, 0:1], ALU.mult, ALU.add,
                                       deps=[t_vd] + batb.wdeps())
                        batt.rd(t_ab)
                        batb.wr(t_ab)
                        VE.self_wait()
                        t_stc = GP.emit("dma_start", out=attn_own_d[:, q0:q0 + QW],
                                        in_=atb_sb[:], deps=[t_ab])
                        batb.rd(t_stc)
                    GP.wait(t_stc)
                    nc.gpsimd.collective_compute("AllGather", ALU.bypass,
                                                 [list(range(NCORES))], [attn_own_d],
                                                 [attn_all]).then_inc(sem_ag2, 1)
                    t_ag2 = ("ag2", 1, sem_ag2)
                    if _DEBUG:
                        GP.emit("dma_start", out=d_attn, in_=attn_own_d, deps=[t_stc])
                barrier()
            barrier()

            if _PHASES <= 2:
                return nc
            with ExitStack() as c2:
                E2 = c2.enter_context
                atf = E2(nc.sbuf_tensor("atf", [128, S], BF16))
                atn = E2(nc.sbuf_tensor("atn", [128, 8 * TOK], BF16))
                wo_sb = E2(nc.sbuf_tensor("wo_sb", [128, 2 * 8 * 128], BF16))
                wgt_sb = E2(nc.sbuf_tensor("wgt_sb", [128, NKT * 128], BF16))
                gat_sb = E2(nc.sbuf_tensor("gat_sb", [128, TOK], BF16))
                tmp_sb = E2(nc.sbuf_tensor("tmp_sb", [128, TOK], F32))
                ps_wo = [E2(nc.psum_tensor(f"ps_wo{i}", [128, TOK], F32)) for i in range(2)]
                ps_gt = [E2(nc.psum_tensor(f"ps_gt{i}", [128, TOK], F32)) for i in range(2)]
                bwo = [Buf(), Buf()]
                bgt = [Buf(), Buf()]
                bwos = [Buf(), Buf()]
                bwgs, bgat, batf = Buf(), Buf(), Buf()
                batn = Buf()
                av = attn_all.rearrange("(g p) t -> g p t", p=128)
                for g in range(8):
                    t_af = SY.emit("dma_start", out=atf[:], in_=av[g],
                                   deps=[t_ag2] + batf.wdeps())
                    batf.wr(t_af)
                    dst = atn[:, TOK * g:TOK * (g + 1)]
                    t_a0 = VE.emit("tensor_scalar_mul", dst, atf[:, 0:TOK],
                                   ms_sb[:, 0:1], deps=[t_af, t_msel] + batn.wdeps())
                    for b in range(1, NCORES):
                        VE.self_wait()
                        t_a0 = VE.emit("scalar_tensor_tensor", dst,
                                       atf[:, TOK * b:TOK * (b + 1)], ms_sb[:, b:b + 1],
                                       dst, ALU.mult, ALU.add)
                    VE.self_wait()
                    batn.wr(t_a0)
                    batf.rd(t_a0)
                t_atn = VE.tok()
                for dm in range(NKT):
                    p = dm % 2
                    t_wol = GP.emit("dma_start",
                                    out=wo_sb[:, p * 1024:(p + 1) * 1024].rearrange(
                                        "p (g c) -> p g c", g=8),
                                    in_=wopad[dm].rearrange("(g p) c -> p g c", p=128),
                                    deps=bwos[p].wdeps())
                    bwos[p].wr(t_wol)
                    t_wgl = GP.emit("dma_start",
                                    out=wgt_sb.rearrange("p (k c) -> p k c", k=NKT),
                                    in_=wgate[dm].rearrange("(k p) c -> p k c", p=128),
                                    deps=bwgs.wdeps())
                    bwgs.wr(t_wgl)
                    for h in range(2):
                        for g in range(8):
                            t_womm = TE.emit(
                                "matmul", ps_wo[p][:, 512 * h:512 * (h + 1)],
                                wo_sb[:, p * 1024 + 128 * g:p * 1024 + 128 * (g + 1)],
                                atn[:, TOK * g + 512 * h:TOK * g + 512 * (h + 1)],
                                start=(g == 0), stop=(g == 7),
                                deps=[t_wol, t_atn] +
                                     (bwo[p].wdeps() if g == 0 and h == 0 else []))
                    bwo[p].wr(t_womm)
                    bwos[p].rd(t_womm)
                    for h in range(2):
                        for kt in range(NKT):
                            t_gtm = TE.emit(
                                "matmul", ps_gt[p][:, 512 * h:512 * (h + 1)],
                                wgt_sb[:, 128 * kt:128 * (kt + 1)],
                                ln1o[:, TOK * kt + 512 * h:TOK * kt + 512 * (h + 1)],
                                start=(kt == 0), stop=(kt == NKT - 1),
                                deps=[t_wgl] +
                                     (bgt[p].wdeps() if kt == 0 and h == 0 else []))
                    bgt[p].wr(t_gtm)
                    bwgs.rd(t_gtm)
                    t_sg = SC.emit("activation", gat_sb[:], ps_gt[p][:], AF.Sigmoid,
                                   deps=[t_gtm] + bgat.wdeps())
                    bgt[p].rd(t_sg)
                    bgat.wr(t_sg)
                    t_t = VE.emit("tensor_tensor", tmp_sb[:], ps_wo[p][:], gat_sb[:],
                                  ALU.mult, deps=[t_sg, t_womm])
                    bwo[p].rd(t_t)
                    bgat.rd(t_t)
                    VE.self_wait()
                    VE.emit("tensor_add", xT[:, TOK * dm:TOK * (dm + 1)],
                            xT[:, TOK * dm:TOK * (dm + 1)], tmp_sb[:])
                    VE.self_wait()
                if _DEBUG:
                    GP.emit("dma_start", out=d_xmid, in_=xT[:], deps=[VE.tok()])
                    GP.emit("dma_start", out=d_atn, in_=atn[:], deps=[t_atn])
            barrier()
        barrier()

        if _PHASES <= 3:
            return nc
        with ExitStack() as cC:
            EC = cC.enter_context
            idx_sb = EC(nc.sbuf_tensor("idx_sb", [128, NE * TCAP // 128],
                                       mybir.dt.int32))
            idx2_sb = EC(nc.sbuf_tensor("idx2_sb", [128, 2 * NE * TCAP // 128],
                                        mybir.dt.int32))
            cD = EC(ExitStack())
            ln2o = cD.enter_context(nc.sbuf_tensor("ln2o", [128, NKT * TOK], BF16))
            combT = cD.enter_context(nc.sbuf_tensor("combT", [NE, TOK], BF16))

            with ExitStack() as c2:
                E2 = c2.enter_context
                gw_sb = E2(nc.sbuf_tensor("gw_sb", [128, NKT * NE], F32))
                ps_sc = E2(nc.psum_tensor("ps_sc", [NE, TOK], F32))
                sc_sb = E2(nc.sbuf_tensor("sc_sb", [NE, TOK], F32))
                t_gwl = SY.emit("dma_start",
                                out=gw_sb.rearrange("p (k c) -> p k c", k=NKT),
                                in_=gate_w.rearrange("(k p) c -> p k c", p=128))
                t_scm = [None]

                def consume(kt, lbuf, t_n):
                    for h in range(2):
                        t_scm[0] = TE.emit("matmul", ps_sc[:, 512 * h:512 * (h + 1)],
                                           gw_sb[:, NE * kt:NE * (kt + 1)],
                                           lbuf[:, 512 * h:512 * (h + 1)],
                                           start=(kt == 0), stop=(kt == NKT - 1),
                                           deps=[t_n, t_gwl])
                    return t_scm[0]

                layernorm(ln2o, fp32_consumer=consume, name="l2")
                t_sig = SC.emit("activation", sc_sb[:], ps_sc[:], AF.Sigmoid,
                                deps=[t_scm[0]])
                sct = E2(nc.sbuf_tensor("sct", [128, NE], F32))
                work = E2(nc.sbuf_tensor("work", [128, NE], F32))
                tmq = E2(nc.sbuf_tensor("tmq", [128, NE], F32))
                msum = E2(nc.sbuf_tensor("msum", [128, 8], F32))
                comb = E2(nc.sbuf_tensor("comb", [128, NE], F32))
                ps_t1 = E2(nc.psum_tensor("ps_t1", [128, NE], F32))
                ps_t2 = E2(nc.psum_tensor("ps_t2", [NE, 128], F32))
                bt1, bt2 = Buf(), Buf()
                t_cstore = None
                for tt in range(8):
                    TE.self_wait()
                    t_tr = TE.emit("transpose", ps_t1[:],
                                   sc_sb[:, 128 * tt:128 * (tt + 1)],
                                   id_sb[0:NE, 0:NE], deps=[t_sig] + bt1.wdeps())
                    bt1.wr(t_tr)
                    t_sct = VE.emit("tensor_copy", sct[:], ps_t1[:], deps=[t_tr])
                    bt1.rd(t_sct)
                    VE.self_wait()
                    VE.emit("tensor_copy", work[:], sct[:])
                    VE.emit("memset", msum[:, 0:1], 0.0)
                    VE.self_wait()
                    for it in range(TOPK):
                        VE.emit("reduce_max", msum[:, 1:2], work[:], AX.X)
                        VE.self_wait()
                        VE.emit("tensor_add", msum[:, 0:1], msum[:, 0:1], msum[:, 1:2])
                        if it < TOPK - 1:
                            VE.emit("tensor_scalar", tmq[:], work[:], msum[:, 1:2],
                                    None, ALU.is_equal)
                            VE.self_wait()
                            VE.emit("scalar_tensor_tensor", work[:], tmq[:], -100.0,
                                    work[:], ALU.mult, ALU.add)
                            VE.self_wait()
                    VE.self_wait()
                    VE.emit("tensor_scalar", tmq[:], sct[:], msum[:, 1:2], None,
                            ALU.is_ge)
                    VE.emit("tensor_scalar_add", msum[:, 2:3], msum[:, 0:1], 1e-20)
                    VE.self_wait()
                    VE.emit("reciprocal", msum[:, 3:4], msum[:, 2:3])
                    VE.self_wait()
                    VE.emit("tensor_scalar", comb[:], sct[:], msum[:, 3:4], None,
                            ALU.mult, deps=([t_cstore] if tt > 0 else []))
                    VE.self_wait()
                    t_cb = VE.emit("tensor_tensor", comb[:], comb[:], tmq[:], ALU.mult)
                    VE.self_wait()
                    t_cstore = GP.emit("dma_start",
                                       out=comb_tok_d[128 * tt:128 * (tt + 1), :],
                                       in_=comb[:], deps=[t_cb])
                    bcombrow.rd(t_cstore) if False else None
                    TE.self_wait()
                    t_tr2 = TE.emit("transpose", ps_t2[:], comb[:], id_sb[:],
                                    deps=[t_cb] + bt2.wdeps())
                    bt2.wr(t_tr2)
                    t_cT = VE.emit("tensor_copy", combT[0:NE, 128 * tt:128 * (tt + 1)],
                                   ps_t2[:], deps=[t_tr2])
                    bt2.rd(t_cT)
                    VE.self_wait()
                if _DEBUG:
                    GP.emit("dma_start", out=d_ln2, in_=ln2o[:], deps=[VE.tok()])
                    GP.emit("dma_start", out=d_comb, in_=combT[:], deps=[VE.tok()])
                zero_sb = E2(nc.sbuf_tensor("zero_sb", [128, D], F32))
                t_zz0 = VE.emit("memset", zero_sb[:], 0.0)
                VE.self_wait()
                t_rz = None
                for i in range(4):
                    for tt in range(8):
                        t_rz = GP.emit("dma_start",
                                       out=racc[i][128 * tt:128 * (tt + 1), :],
                                       in_=zero_sb[:], deps=[t_zz0])
                ps_tb = E2(nc.psum_tensor("ps_tb", [128, 128], BF16))
                lntt = E2(nc.sbuf_tensor("lntt", [128, D], BF16))
                idb2 = E2(nc.sbuf_tensor("idb2", [128, 128], BF16))
                t_idb = VE.emit("tensor_copy", idb2[:], id_sb[:])
                VE.self_wait()
                btb, blntt = Buf(), Buf()
                for tt in range(8):
                    for kt in range(NKT):
                        TE.self_wait()
                        t_tr = TE.emit("transpose", ps_tb[:],
                                       ln2o[:, TOK * kt + 128 * tt:TOK * kt + 128 * (tt + 1)],
                                       idb2[:], deps=[t_idb] + btb.wdeps())
                        btb.wr(t_tr)
                        t_ev = VE.emit("tensor_copy", lntt[:, 128 * kt:128 * (kt + 1)],
                                       ps_tb[:], deps=[t_tr] + (blntt.wdeps() if kt == 0 else []))
                        btb.rd(t_ev)
                        VE.self_wait()
                    t_lnst = GP.emit("dma_start", out=ln2tok_d[128 * tt:128 * (tt + 1), :],
                                     in_=lntt[:], deps=[VE.tok()])
                    blntt.wr(t_lnst) if False else None
                    blntt.rd(t_lnst)
                t_ln2tok = t_lnst
                msk = E2(nc.sbuf_tensor("msk", [NE, TOK], F32))
                zsc = E2(nc.sbuf_tensor("zsc", [NE, TOK], F32))
                posx = E2(nc.sbuf_tensor("posx", [NE, TOK], F32))
                post_i = E2(nc.sbuf_tensor("post_i", [128, 8 * NE], mybir.dt.int32))
                tokid_sb = E2(nc.sbuf_tensor("tokid_sb", [128, 8], mybir.dt.int32))
                idxf = E2(nc.sbuf_tensor("idxf", [96, 128], mybir.dt.int32))
                t_tid = SY.emit("dma_start", out=tokid_sb[:], in_=tokid)
                eoff_sb = E2(nc.sbuf_tensor("eoff_sb", [NE, 1], F32))
                t_eo = SY.emit("dma_start", out=eoff_sb[:], in_=eoff)
                VE.emit("tensor_scalar", msk[:], combT[:], 0.0, None, ALU.is_gt,
                        deps=[t_cT])
                VE.emit("memset", zsc[:], 0.0)
                VE.self_wait()
                VE.emit("tensor_tensor_scan", posx[:], msk[:], zsc[:], 0.0,
                        ALU.add, ALU.add)
                VE.self_wait()
                VE.emit("tensor_scalar_add", posx[:], posx[:], -1.0)
                VE.self_wait()
                VE.emit("tensor_scalar", zsc[:], msk[:], -1.0, 1.0, ALU.mult, ALU.add)
                VE.self_wait()
                VE.emit("scalar_tensor_tensor", posx[:], zsc[:], 1.0e6,
                        posx[:], ALU.mult, ALU.add)
                VE.self_wait()
                t_posx = VE.emit("tensor_scalar_add", posx[:], posx[:],
                                 eoff_sb[:, 0:1], deps=[t_eo])
                VE.self_wait()
                t_if = VE.emit("memset", idxf[:], 100000)
                VE.self_wait()
                t_ifd = GP.emit("dma_start",
                                out=idx_d.rearrange("e (a p) -> (e a) p", p=128),
                                in_=idxf[:], deps=[t_if])
                t_scat = None
                for tt in range(8):
                    TE.self_wait()
                    t_trp = TE.emit("transpose", ps_t1[:],
                                    posx[:, 128 * tt:128 * (tt + 1)],
                                    id_sb[0:NE, 0:NE], deps=[t_posx] + bt1.wdeps())
                    bt1.wr(t_trp)
                    t_pi = VE.emit("tensor_copy", post_i[:, NE * tt:NE * (tt + 1)],
                                   ps_t1[:], deps=[t_trp])
                    bt1.rd(t_pi)
                    VE.self_wait()
                    for e in range(NE):
                        t_scat = GP.emit("indirect_dma_start",
                                         idx_d.rearrange("e (t o) -> (e t) o", o=1),
                                         IndirectOffsetOnAxis(
                                             post_i[:, NE * tt + e:NE * tt + e + 1], 0),
                                         tokid_sb[:, tt:tt + 1], None,
                                         bounds_check=r_btcap, oob_is_err=False,
                                         deps=[t_pi, t_tid, t_ifd])
                for e in range(NE):
                    for a in range(3):
                        t_idxload = GP.emit(
                            "dma_start", out=idx_sb[:, 3 * e + a:3 * e + a + 1],
                            in_=idx_d[e:e + 1, 128 * a:128 * (a + 1)].rearrange(
                                "e p -> p e"),
                            deps=[t_scat])
                if _DEBUG:
                    GP.emit("dma_start", out=d_idx, in_=idx_d, deps=[t_scat])
                for dmh in range(2):
                    VE.emit("tensor_scalar", idx2_sb[:, 96 * dmh:96 * (dmh + 1)],
                            idx_sb[:], 2, dmh, ALU.mult, ALU.add, deps=[t_idxload])
                VE.self_wait()
                t_idx2 = VE.tok()
            barrier()

            if _PHASES <= 4:
                return nc
            if _PHASES <= 5:
                return nc
            with ExitStack() as c2:
                E2 = c2.enter_context
                shact = E2(nc.sbuf_tensor("shact", [128, 12 * TOK], BF16))
                wsh_sb = E2(nc.sbuf_tensor("wsh_sb", [128, 4 * NKT * 128], BF16))
                wsd_sb = E2(nc.sbuf_tensor("wsd_sb", [128, 2 * 12 * 128], BF16))
                silu2 = E2(nc.sbuf_tensor("silu2", [128, TOK], BF16))
                ps_a = E2(nc.psum_tensor("ps_a2", [128, TOK], F32))
                ps_b = E2(nc.psum_tensor("ps_b2", [128, TOK], F32))
                ps_c = E2(nc.psum_tensor("ps_c2", [128, TOK], F32))
                ps_d = E2(nc.psum_tensor("ps_d2", [128, TOK], F32))
                bpa, bpb, bpc, bpd = Buf(), Buf(), Buf(), Buf()
                bwsh = [Buf(), Buf()]
                bwsd = [Buf(), Buf()]
                bsilu, bsh = Buf(), Buf()
                bxt = [Buf() for _ in range(NKT)]
                for m in range(12):
                    pbuf = m % 2
                    woff = pbuf * 2 * NKT * 128
                    t_lg = GP.emit("dma_start",
                                   out=wsh_sb[:, woff:woff + NKT * 128].rearrange(
                                       "p (k c) -> p k c", k=NKT),
                                   in_=sgm[m].rearrange("(k p) c -> p k c", p=128),
                                   deps=bwsh[pbuf].wdeps())
                    t_lu = GP.emit("dma_start",
                                   out=wsh_sb[:, woff + NKT * 128:woff + 2 * NKT * 128]
                                   .rearrange("p (k c) -> p k c", k=NKT),
                                   in_=sum_m[m].rearrange("(k p) c -> p k c", p=128))
                    bwsh[pbuf].wr(t_lu)
                    pg, bg = (ps_a, bpa) if m % 2 == 0 else (ps_b, bpb)
                    pu, bu = (ps_c, bpc) if m % 2 == 0 else (ps_d, bpd)
                    for h in range(2):
                        for kt in range(NKT):
                            t_g = TE.emit(
                                "matmul", pg[:, 512 * h:512 * (h + 1)],
                                wsh_sb[:, woff + 128 * kt:woff + 128 * (kt + 1)],
                                ln2o[:, TOK * kt + 512 * h:TOK * kt + 512 * (h + 1)],
                                start=(kt == 0), stop=(kt == NKT - 1),
                                deps=[t_lg] + (bg.wdeps() if kt == 0 and h == 0 else []))
                    bg.wr(t_g)
                    bwsh[pbuf].rd(t_g)
                    for h in range(2):
                        for kt in range(NKT):
                            t_u = TE.emit(
                                "matmul", pu[:, 512 * h:512 * (h + 1)],
                                wsh_sb[:, woff + NKT * 128 + 128 * kt:
                                       woff + NKT * 128 + 128 * (kt + 1)],
                                ln2o[:, TOK * kt + 512 * h:TOK * kt + 512 * (h + 1)],
                                start=(kt == 0), stop=(kt == NKT - 1),
                                deps=[t_lu] + (bu.wdeps() if kt == 0 and h == 0 else []))
                    bu.wr(t_u)
                    bwsh[pbuf].rd(t_u)
                    t_si = SC.emit("activation", silu2[:], pg[:], AF.Silu,
                                   deps=[t_g] + bsilu.wdeps())
                    bg.rd(t_si)
                    bsilu.wr(t_si)
                    a = shact[:, TOK * m:TOK * (m + 1)]
                    t_a1 = VE.emit("tensor_tensor", a, silu2[:], pu[:], ALU.mult,
                                   deps=[t_si, t_u])
                    bu.rd(t_a1)
                    bsilu.rd(t_a1)
                    bsh.wr(t_a1)
                    VE.self_wait()
                for dm in range(NKT):
                    pbuf = dm % 2
                    woff = pbuf * 12 * 128
                    t_lw = GP.emit("dma_start",
                                   out=wsd_sb[:, woff:woff + 12 * 128].rearrange(
                                       "p (f c) -> p f c", f=12),
                                   in_=sdm[dm].rearrange("(f p) c -> p f c", p=128),
                                   deps=bwsd[pbuf].wdeps())
                    bwsd[pbuf].wr(t_lw)
                    pr, br = (ps_a, bpa) if dm % 2 == 0 else (ps_b, bpb)
                    for h in range(2):
                        for f in range(12):
                            t_r = TE.emit(
                                "matmul", pr[:, 512 * h:512 * (h + 1)],
                                wsd_sb[:, woff + 128 * f:woff + 128 * (f + 1)],
                                shact[:, TOK * f + 512 * h:TOK * f + 512 * (h + 1)],
                                start=(f == 0), stop=(f == 11),
                                deps=[t_lw, bsh.w] +
                                     (br.wdeps() if f == 0 and h == 0 else []))
                        br.wr(t_r)
                    bwsd[pbuf].rd(t_r)
                    bsh.rd(t_r)
                    t_add = VE.emit("tensor_add", xT[:, TOK * dm:TOK * (dm + 1)],
                                    xT[:, TOK * dm:TOK * (dm + 1)], pr[:],
                                    deps=[t_r] + bxt[dm].wdeps())
                    br.rd(t_add)
                    bxt[dm].wr(t_add)
                    VE.self_wait()
            barrier()

            cD.close()
            barrier()
            if _PHASES <= 6:
                return nc
            with ExitStack() as c2:
                E2 = c2.enter_context
                xg = [E2(nc.sbuf_tensor(f"xg{i}", [128, NKT * TCAP], BF16))
                      for i in range(2)]
                actT = [E2(nc.sbuf_tensor(f"actT{i}", [128, 6 * TCAP], BF16))
                        for i in range(2)]
                gst = E2(nc.sbuf_tensor("gst", [128, D], BF16))
                cbg = [E2(nc.sbuf_tensor(f"cbg{i}", [128, 3 * NE], F32))
                       for i in range(2)]
                wgu_sb = E2(nc.sbuf_tensor("wgu_sb", [128, 4 * NKT * 128], BF16))
                wdd_sb = E2(nc.sbuf_tensor("wdd_sb", [128, 6 * D], BF16))
                silu_sb = E2(nc.sbuf_tensor("silu_sb", [128, TCAP], BF16))
                rout = [E2(nc.sbuf_tensor(f"rout{i}", [128, 1024], F32))
                        for i in range(2)]
                ps_g = [E2(nc.psum_tensor(f"ps_g{i}", [128, TCAP], F32)) for i in range(2)]
                ps_u = [E2(nc.psum_tensor(f"ps_u{i}", [128, TCAP], F32)) for i in range(2)]
                ps_r = [E2(nc.psum_tensor(f"ps_r{i}", [128, 1024], F32)) for i in range(2)]
                bxg = [Buf(), Buf()]
                bact = [Buf(), Buf()]
                bgst = Buf()
                bstg = [Buf(), Buf()]
                bcbg = [Buf(), Buf()]
                bwgu = [Buf(), Buf()]
                bwdd = [Buf(), Buf()]
                bpg = [Buf(), Buf()]
                bpu = [Buf(), Buf()]
                bpr = [Buf(), Buf()]
                brt = [Buf(), Buf()]
                bracc = [Buf(), Buf(), Buf(), Buf()]
                bsilu = Buf()
                for i in range(4):
                    bracc[i].wr(t_rz)
                for e in range(NE):
                    buf = e % 2
                    t_stg = None
                    for a in range(3):
                        t_ga = GP.emit("indirect_dma_start", gst[:], None, ln2tok_d,
                                       IndirectOffsetOnAxis(
                                           idx_sb[:, 3 * e + a:3 * e + a + 1], 0),
                                       bounds_check=r_btok, oob_is_err=False,
                                       deps=[t_idxload, t_ln2tok] + bgst.wdeps())
                        bgst.wr(t_ga)
                        t_stg = GP.emit("dma_start",
                                        out=stage_d[buf, 128 * a:128 * (a + 1), :],
                                        in_=gst[:], deps=[t_ga] +
                                        (bstg[buf].wdeps() if a == 0 else []))
                        bgst.rd(t_stg)
                    bstg[buf].wr(t_stg)
                    t_cg = None
                    for a in range(3):
                        t_cg = GP.emit("indirect_dma_start",
                                       cbg[buf][:, NE * a:NE * (a + 1)], None,
                                       comb_tok_d,
                                       IndirectOffsetOnAxis(
                                           idx_sb[:, 3 * e + a:3 * e + a + 1], 0),
                                       bounds_check=r_btok, oob_is_err=False,
                                       deps=[t_cstore] + (bcbg[buf].wdeps() if a == 0 else []))
                    bcbg[buf].wr(t_cg)
                    t_xg = None
                    for kt in range(NKT):
                        QQ = SY
                        t_xg = QQ.emit("dma_start",
                                       out=xg[buf][:, TCAP * kt:TCAP * (kt + 1)],
                                       in_=stage_d[buf, :, 128 * kt:128 * (kt + 1)],
                                       transpose=True, inc=16,
                                       deps=[t_stg] + (bxg[buf].wdeps() if kt == 0 else []))
                    t_xg_sy = SY.tok()
                    t_xg_sc = SC.tok()
                    bxg[buf].wr(t_xg_sy)
                    bxg[buf].r.append(t_xg_sc)
                    bstg[buf].rd(t_xg_sy)
                    bstg[buf].rd(t_xg_sc)
                    if _DEBUG and e == 0:
                        GP.emit("dma_start", out=d_xg, in_=xg[0][:],
                                deps=[t_xg_sy, t_xg_sc])
                    for m in range(6):
                        pb = m % 2
                        woff = pb * 2 * NKT * 128
                        t_lg = GP.emit("dma_start",
                                       out=wgu_sb[:, woff:woff + NKT * 128].rearrange(
                                           "p (k c) -> p k c", k=NKT),
                                       in_=wgm[e, m].rearrange("(k p) c -> p k c", p=128),
                                       deps=bwgu[pb].wdeps())
                        t_lu = GP.emit("dma_start",
                                       out=wgu_sb[:, woff + NKT * 128:woff + 2 * NKT * 128]
                                       .rearrange("p (k c) -> p k c", k=NKT),
                                       in_=wum[e, m].rearrange("(k p) c -> p k c", p=128))
                        bwgu[pb].wr(t_lu)
                        for kt in range(NKT):
                            t_g = TE.emit(
                                "matmul", ps_g[pb][:],
                                wgu_sb[:, woff + 128 * kt:woff + 128 * (kt + 1)],
                                xg[buf][:, TCAP * kt:TCAP * (kt + 1)],
                                start=(kt == 0), stop=(kt == NKT - 1),
                                deps=[t_lg, t_xg_sy, t_xg_sc] +
                                     (bpg[pb].wdeps() if kt == 0 else []))
                        bpg[pb].wr(t_g)
                        bwgu[pb].rd(t_g)
                        bxg[buf].rd(t_g)
                        for kt in range(NKT):
                            t_u = TE.emit(
                                "matmul", ps_u[pb][:],
                                wgu_sb[:, woff + NKT * 128 + 128 * kt:
                                       woff + NKT * 128 + 128 * (kt + 1)],
                                xg[buf][:, TCAP * kt:TCAP * (kt + 1)],
                                start=(kt == 0), stop=(kt == NKT - 1),
                                deps=[t_lu] + (bpu[pb].wdeps() if kt == 0 else []))
                        bpu[pb].wr(t_u)
                        bwgu[pb].rd(t_u)
                        bxg[buf].rd(t_u)
                        t_si = SC.emit("activation", silu_sb[:], ps_g[pb][:], AF.Silu,
                                       deps=[t_g] + bsilu.wdeps())
                        bpg[pb].rd(t_si)
                        bsilu.wr(t_si)
                        a_ = actT[buf][:, TCAP * m:TCAP * (m + 1)]
                        t_a1 = VE.emit("tensor_tensor", a_, silu_sb[:], ps_u[pb][:],
                                       ALU.mult,
                                       deps=[t_si, t_u] + (bact[buf].wdeps() if m == 0 else []))
                        bpu[pb].rd(t_a1)
                        bsilu.rd(t_a1)
                        VE.self_wait()
                    bact[buf].wr(VE.tok())
                    t_lw = None
                    for m in range(6):
                        t_lw = GP.emit("dma_start",
                                       out=wdd_sb[:, D * m:D * (m + 1)],
                                       in_=wdp[e, 128 * m:128 * (m + 1), :],
                                       deps=(bwdd[0].wdeps() if m == 0 else []))
                    bwdd[0].wr(t_lw)
                    for a in range(3):
                        for dmh in range(2):
                            pb = (a * 2 + dmh) % 2
                            pr = ps_r[pb]
                            first = True
                            for m in range(6):
                                for ns in range(2):
                                    t_r = TE.emit(
                                        "matmul", pr[:, 512 * ns:512 * (ns + 1)],
                                        actT[buf][:, TCAP * m + 128 * a:
                                                  TCAP * m + 128 * (a + 1)],
                                        wdd_sb[:, D * m + 1024 * dmh + 512 * ns:
                                               D * m + 1024 * dmh + 512 * (ns + 1)],
                                        start=(m == 0), stop=(m == 5),
                                        skip_group_check=True,
                                        deps=[t_lw, bact[buf].w] +
                                             (bpr[pb].wdeps() if first else []))
                                    first = False
                            bpr[pb].wr(t_r)
                            bwdd[0].rd(t_r)
                            bact[buf].rd(t_r)
                            rb = (a * 2 + dmh) % 2
                            t_ev = VE.emit("tensor_scalar", rout[rb][:], pr[:],
                                           cbg[buf][:, NE * a + e:NE * a + e + 1],
                                           None, ALU.mult,
                                           deps=[t_r, t_cg] + brt[rb].wdeps())
                            bpr[pb].rd(t_ev)
                            bcbg[buf].rd(t_ev)
                            brt[rb].wr(t_ev)
                            VE.self_wait()
                            acc = e % 4
                            t_sc2 = GP.emit(
                                "indirect_dma_start",
                                racc[acc].rearrange("t (h c) -> (t h) c", h=2),
                                IndirectOffsetOnAxis(
                                    idx2_sb[:, 96 * dmh + 3 * e + a:
                                            96 * dmh + 3 * e + a + 1], 0),
                                rout[rb][:], None,
                                bounds_check=r_b2tok, oob_is_err=False,
                                compute_op=ALU.add,
                                deps=[t_ev, t_idx2] + bracc[acc].wdeps())
                            bracc[acc].wr(t_sc2)
                            brt[rb].rd(t_sc2)
            barrier()
        barrier()

        if _PHASES <= 6:
            return nc
        with ExitStack() as c2:
            E2 = c2.enter_context
            out_sb = E2(nc.sbuf_tensor("out_sb", [128, D], F32))
            rc_sb = [E2(nc.sbuf_tensor(f"rc_sb{i}", [128, D], F32)) for i in range(4)]
            ps_t = [E2(nc.psum_tensor(f"ps_to{i}", [128, 128], F32)) for i in range(2)]
            bt = [Buf(), Buf()]
            bout = Buf()
            brc = [Buf() for _ in range(4)]
            for tt in range(8):
                t_rl = []
                for i in range(4):
                    t_rl.append(SY.emit("dma_start", out=rc_sb[i][:],
                                        in_=racc[i][128 * tt:128 * (tt + 1), :],
                                        deps=brc[i].wdeps()))
                for kt in range(NKT):
                    p = kt % 2
                    TE.self_wait()
                    t_tr = TE.emit("transpose", ps_t[p][:],
                                   xT[:, TOK * kt + 128 * tt:TOK * kt + 128 * (tt + 1)],
                                   id_sb[:], deps=bt[p].wdeps())
                    bt[p].wr(t_tr)
                    t_cp = VE.emit("tensor_copy", out_sb[:, 128 * kt:128 * (kt + 1)],
                                   ps_t[p][:],
                                   deps=[t_tr] + (bout.wdeps() if kt == 0 else []))
                    bt[p].rd(t_cp)
                    VE.self_wait()
                for i in range(4):
                    VE.emit("tensor_add", out_sb[:], out_sb[:], rc_sb[i][:],
                            deps=[t_rl[i]])
                    VE.self_wait()
                    brc[i].rd(VE.tok())
                t_all = VE.tok()
                t_st = GP.emit("dma_start", out=y[128 * tt:128 * (tt + 1), :],
                               in_=out_sb[:], deps=[t_all])
                bout.wr(t_st)
            GP.self_wait()
        barrier()
    return nc


def _pad_heads(w, hg):
    out = np.zeros((w.shape[0], 128), np.float32)
    for j in range(HPG):
        out[:, 32 * j:32 * j + 8] = w[:, 8 * (HPG * hg + j):8 * (HPG * hg + j) + 8]
    return out


def kernel(L0_attn_hidden_states, L0_attn_Wq, L0_attn_Wk, L0_attn_Wv, L0_attn_Wo,
           L0_attn_Wg, L0_attn_v_emb0, L0_attn_v_emb1, L0_ffn_gate_w, L0_ffn_wg,
           L0_ffn_wu, L0_ffn_wd, L0_ffn_sg, L0_ffn_su, L0_ffn_sd):
    f32 = np.float32
    bf = ml_dtypes.bfloat16
    x = np.asarray(L0_attn_hidden_states, f32)[0]
    Wq = np.asarray(L0_attn_Wq, f32)
    Wk = np.asarray(L0_attn_Wk, f32)
    Wv = np.asarray(L0_attn_Wv, f32)
    Wo = np.asarray(L0_attn_Wo, f32)
    Wg = np.asarray(L0_attn_Wg, f32)
    ve0 = np.asarray(L0_attn_v_emb0, f32)
    ve1 = np.asarray(L0_attn_v_emb1, f32)
    gate_w = np.asarray(L0_ffn_gate_w, f32)
    wg = np.asarray(L0_ffn_wg, f32)
    wu = np.asarray(L0_ffn_wu, f32)
    wd = np.asarray(L0_ffn_wd, f32)
    sg = np.asarray(L0_ffn_sg, f32)
    su = np.asarray(L0_ffn_su, f32)
    sd = np.asarray(L0_ffn_sd, f32)

    xT = np.ascontiguousarray(x.T)
    wopad = np.zeros((16, 8 * 128, 128), f32)
    for g in range(8):
        for j in range(HPG):
            rows = slice(128 * g + 32 * j, 128 * g + 32 * j + 8)
            src = Wo[8 * (HPG * g + j):8 * (HPG * g + j) + 8]
            for dm in range(16):
                wopad[dm, rows, :] = src[:, 128 * dm:128 * (dm + 1)]
    wgate = np.ascontiguousarray(Wg.reshape(D, 16, 128).transpose(1, 0, 2))
    wgm = np.ascontiguousarray(wg.reshape(NE, D, 6, 128).transpose(0, 2, 1, 3))
    wum = np.ascontiguousarray(wu.reshape(NE, D, 6, 128).transpose(0, 2, 1, 3))
    tokid_h = (np.arange(128)[:, None] + 128 * np.arange(8)[None, :]).astype(np.int32)
    eoff_h = (TCAP * np.arange(NE)[:, None]).astype(np.float32)
    sgm = np.ascontiguousarray(sg.reshape(D, 12, 128).transpose(1, 0, 2))
    sum_m = np.ascontiguousarray(su.reshape(D, 12, 128).transpose(1, 0, 2))
    sdm = np.ascontiguousarray(sd.reshape(FSH, 16, 128).transpose(1, 0, 2))
    ident = np.eye(128, dtype=f32)
    ones = np.ones((128, 128), f32)
    sel = np.zeros((128, 128), f32)
    bdm = np.zeros((128, HPG), f32)
    padmask = np.zeros((128, 1), f32)
    for j in range(HPG):
        sel[32 * j + 8, 32 * j:32 * j + 32] = 1.0
        bdm[32 * j:32 * j + 8, j] = 1.0
        padmask[32 * j:32 * j + 8, 0] = 1.0
    cbsel = np.zeros((NE, NE * 128), f32)
    for e in range(NE):
        cbsel[e, 128 * e:128 * (e + 1)] = 1.0
    masks = np.zeros((128, 4 * HPG * QW), f32)
    qr = np.arange(QW)
    kr = np.arange(128)[:, None]
    for p in range(4):
        m = (128 * p + kr <= qr[None, :]).astype(f32)
        for h in range(HPG):
            masks[:, p * HPG * QW + QW * h:p * HPG * QW + QW * (h + 1)] = m

    shared = {
        "wopad": wopad, "wgate": wgate, "gate_w": gate_w, "wgm": wgm, "wum": wum,
        "wdp": wd, "sgm": sgm, "sum_m": sum_m, "sdm": sdm, "ident": ident,
        "tokid": tokid_h, "eoff": eoff_h,
        "ones": ones, "sel": sel, "bd": bdm.astype(bf), "padmask": padmask,
        "cbsel": cbsel.astype(bf), "masks": masks.astype(bf),
    }
    in_maps = []
    for c in range(NCORES):
        ve0p = np.zeros((128, 1), f32)
        ve1p = np.zeros((128, 1), f32)
        for j in range(HPG):
            ve0p[32 * j:32 * j + 8, 0] = ve0[8 * (HPG * c + j):8 * (HPG * c + j) + 8]
            ve1p[32 * j:32 * j + 8, 0] = ve1[8 * (HPG * c + j):8 * (HPG * c + j) + 8]
        mselc = np.zeros((128, NCORES), f32)
        mselc[:, c] = 1.0
        m = dict(shared)
        m.update({
            "xT_own": np.ascontiguousarray(xT[:, TOK * c:TOK * (c + 1)]),
            "wqp": _pad_heads(Wq, c), "wkp": _pad_heads(Wk, c),
            "wv": np.ascontiguousarray(Wv[:, 32 * c:32 * (c + 1)]),
            "vemb0p": ve0p, "vemb1p": ve1p, "msel": mselc,
        })
        in_maps.append(m)

    nc = build_program()
    res = run_bass_kernel_spmd(nc, in_maps, core_ids=list(range(NCORES)))
    out = np.concatenate([res.results[c]["y"] for c in range(NCORES)], axis=0)
    return out.reshape(1, S, D).astype(np.float32)


# version 05
# speedup vs baseline: 1.0212x; 1.0212x over previous
import numpy as np
import ml_dtypes
import concourse.bass as bass
import concourse.mybir as mybir
from concourse.bass_utils import run_bass_kernel_spmd
from concourse.bass import IndirectOffsetOnAxis
from contextlib import ExitStack
import os

_PHASES = int(os.environ.get("KPHASES", "99"))
_DEBUG = int(os.environ.get("KDEBUG", "0"))

F32 = mybir.dt.float32
BF16 = mybir.dt.bfloat16
AF = mybir.ActivationFunctionType
ALU = mybir.AluOpType
AX = mybir.AxisListType

NCORES = 8
S = 8192
D = 2048
NKT = D // 128
TOK = S // NCORES
HPG = 4
NE, TOPK, FE, FSH = 32, 6, 768, 1536
EPS = 1e-5
QW = 512
NQC = 16
SKT = S // 128
TCAP = 384


class Q:

    def __init__(self, eng, name, sem, inc):
        self.eng, self.name, self.sem, self.inc = eng, name, sem, inc
        self.n = 0
        self.waited = {}

    def wait(self, token):
        if token is None:
            return
        name, cnt, sem = token
        if self.waited.get(name, -1) >= cnt:
            return
        self.eng.wait_ge(sem, cnt)
        self.waited[name] = cnt

    def self_wait(self):
        self.wait((self.name, self.n, self.sem))

    def emit(self, method, *args, deps=(), inc=None, **kw):
        for d in deps:
            self.wait(d)
        instr = getattr(self.eng, method)(*args, **kw)
        step = self.inc if inc is None else inc
        instr.then_inc(self.sem, step)
        self.n += step
        return (self.name, self.n, self.sem)

    def tok(self):
        return (self.name, self.n, self.sem)


class Buf:

    def __init__(self):
        self.w = None
        self.r = []

    def wdeps(self):
        return ([self.w] if self.w else []) + self.r

    def rdeps(self):
        return [self.w] if self.w else []

    def wr(self, tok):
        self.w = tok
        self.r = []

    def rd(self, tok):
        self.r.append(tok)


def build_program():
    nc = bass.Bass("TRN2", target_bir_lowering=False)
    dt = nc.dram_tensor
    xT_own = dt("xT_own", [D, TOK], F32, kind="ExternalInput").ap()
    wqp = dt("wqp", [D, 128], F32, kind="ExternalInput").ap()
    wkp = dt("wkp", [D, 128], F32, kind="ExternalInput").ap()
    wv = dt("wv", [D, 32], F32, kind="ExternalInput").ap()
    vemb0p = dt("vemb0p", [128, 1], F32, kind="ExternalInput").ap()
    vemb1p = dt("vemb1p", [128, 1], F32, kind="ExternalInput").ap()
    msel = dt("msel", [128, NCORES], F32, kind="ExternalInput").ap()
    wopad = dt("wopad", [16, 8 * 128, 128], F32, kind="ExternalInput").ap()
    wgate = dt("wgate", [16, D, 128], F32, kind="ExternalInput").ap()
    gate_w = dt("gate_w", [D, NE], F32, kind="ExternalInput").ap()
    wgm = dt("wgm", [NE, 6, D, 128], F32, kind="ExternalInput").ap()
    wum = dt("wum", [NE, 6, D, 128], F32, kind="ExternalInput").ap()
    wdp = dt("wdp", [NE, FE, D], F32, kind="ExternalInput").ap()
    sgm = dt("sgm", [12, D, 128], F32, kind="ExternalInput").ap()
    sum_m = dt("sum_m", [12, D, 128], F32, kind="ExternalInput").ap()
    sdm = dt("sdm", [16, FSH, 128], F32, kind="ExternalInput").ap()
    ident = dt("ident", [128, 128], F32, kind="ExternalInput").ap()
    ones = dt("ones", [128, 128], F32, kind="ExternalInput").ap()
    sel = dt("sel", [128, 128], F32, kind="ExternalInput").ap()
    bd = dt("bd", [128, HPG], BF16, kind="ExternalInput").ap()
    padmask = dt("padmask", [128, 1], F32, kind="ExternalInput").ap()
    cbsel = dt("cbsel", [NE, NE * 128], BF16, kind="ExternalInput").ap()
    masks = dt("masks", [128, 4 * HPG * QW], BF16, kind="ExternalInput").ap()
    tokid = dt("tokid", [128, 8], mybir.dt.int32, kind="ExternalInput").ap()
    eoff = dt("eoff", [NE, 1], F32, kind="ExternalInput").ap()
    ln1_own_d = dt("ln1_own_d", [D, TOK], BF16, kind="Internal").ap()
    ln1_all = dt("ln1_all", [NCORES * D, TOK], BF16, kind="Internal",
                 addr_space="Shared").ap()
    attn_own_d = dt("attn_own_d", [128, S], BF16, kind="Internal").ap()
    attn_all = dt("attn_all", [NCORES * 128, S], BF16, kind="Internal",
                  addr_space="Shared").ap()
    ln2tok_d = dt("ln2tok_d", [TOK, D], BF16, kind="Internal").ap()
    comb_tok_d = dt("comb_tok_d", [TOK, NE], F32, kind="Internal").ap()
    idx_d = dt("idx_d", [NE, TCAP], mybir.dt.int32, kind="Internal").ap()
    stage_d = dt("stage_d", [2, TCAP, D], BF16, kind="Internal").ap()
    racc = [dt(f"racc{i}", [TOK, D], F32, kind="Internal").ap() for i in range(4)]
    y = dt("y", [TOK, D], F32, kind="ExternalOutput").ap()
    if _DEBUG:
        d_ln1 = dt("d_ln1", [128, NKT * TOK], F32, kind="ExternalOutput").ap()
        d_attn = dt("d_attn", [128, S], F32, kind="ExternalOutput").ap()
        d_xmid = dt("d_xmid", [128, NKT * TOK], F32, kind="ExternalOutput").ap()
        d_ln2 = dt("d_ln2", [128, NKT * TOK], F32, kind="ExternalOutput").ap()
        d_comb = dt("d_comb", [NE, TOK], F32, kind="ExternalOutput").ap()
        d_atn = dt("d_atn", [128, 8 * TOK], F32, kind="ExternalOutput").ap()
        d_idx = dt("d_idx", [NE, TCAP], mybir.dt.int32, kind="ExternalOutput").ap()
        d_xg = dt("d_xg", [128, NKT * TCAP], F32, kind="ExternalOutput").ap()

    with ExitStack() as ctx:
        EE = ctx.enter_context
        TE = Q(nc.tensor, "te", EE(nc.semaphore("s_te")), 1)
        SC = Q(nc.scalar, "sc", EE(nc.semaphore("s_sc")), 1)
        VE = Q(nc.vector, "ve", EE(nc.semaphore("s_ve")), 1)
        SY = Q(nc.sync, "sy", EE(nc.semaphore("s_sy")), 16)
        GP = Q(nc.gpsimd, "gp", EE(nc.semaphore("s_gp")), 16)
        sem_ag1 = EE(nc.semaphore("s_ag1"))
        sem_ag2 = EE(nc.semaphore("s_ag2"))
        r_btcap = nc.gpsimd.to_reg(NE * TCAP)
        r_btok = nc.gpsimd.to_reg(TOK)
        r_b2tok = nc.gpsimd.to_reg(2 * TOK)
        ALL = (TE, SC, VE, SY, GP)

        def barrier():
            toks = [q.tok() for q in ALL]
            for q in ALL:
                for t in toks:
                    q.wait(t)

        xT = EE(nc.sbuf_tensor("xT", [128, NKT * TOK], F32))
        id_sb = EE(nc.sbuf_tensor("id_sb", [128, 128], F32))
        on_sb = EE(nc.sbuf_tensor("on_sb", [128, 128], F32))
        sel_sb = EE(nc.sbuf_tensor("sel_sb", [128, 128], F32))
        bd_sb = EE(nc.sbuf_tensor("bd_sb", [128, HPG], BF16))
        pm_sb = EE(nc.sbuf_tensor("pm_sb", [128, 1], F32))
        ve0_sb = EE(nc.sbuf_tensor("ve0_sb", [128, 1], F32))
        ve1_sb = EE(nc.sbuf_tensor("ve1_sb", [128, 1], F32))
        vd_sb = EE(nc.sbuf_tensor("vd_sb", [128, 1], F32))
        ms_sb = EE(nc.sbuf_tensor("ms_sb", [128, NCORES], F32))

        t_x = SY.emit("dma_start", out=xT.rearrange("p (k t) -> p k t", k=NKT),
                      in_=xT_own.rearrange("(k p) t -> p k t", p=128))
        SY.emit("dma_start", out=id_sb[:], in_=ident)
        SY.emit("dma_start", out=on_sb[:], in_=ones)
        SY.emit("dma_start", out=sel_sb[:], in_=sel)
        SY.emit("dma_start", out=bd_sb[:], in_=bd)
        SY.emit("dma_start", out=pm_sb[:], in_=padmask)
        SY.emit("dma_start", out=ve0_sb[:], in_=vemb0p)
        t_ve = SY.emit("dma_start", out=ve1_sb[:], in_=vemb1p)
        t_msel = SY.emit("dma_start", out=ms_sb[:], in_=msel)
        t_vd = VE.emit("tensor_sub", vd_sb[:], ve1_sb[:], ve0_sb[:], deps=[t_ve])
        t_const = SY.tok()

        def layernorm(dst_bf16, fp32_consumer=None, name=""):
            with ExitStack() as c2:
                E2 = c2.enter_context
                sq = E2(nc.sbuf_tensor(f"sq{name}", [128, 2 * TOK], F32))
                stat = E2(nc.sbuf_tensor(f"stat{name}", [1, 4 * TOK], F32))
                lnf = E2(nc.sbuf_tensor(f"lnf{name}", [128, 2 * TOK], F32))
                mb_sb = E2(nc.sbuf_tensor(f"mb{name}", [128, TOK], F32))
                rb_sb = E2(nc.sbuf_tensor(f"rb{name}", [128, TOK], F32))
                with ExitStack() as c3:
                    E3 = c3.enter_context
                    ps_sum = E3(nc.psum_tensor(f"pssum{name}", [1, TOK], F32))
                    ps_ssq = E3(nc.psum_tensor(f"psssq{name}", [1, TOK], F32))
                    t_sq = [None, None]
                    t_ssqmm = None
                    for kt in range(NKT):
                        xkt = xT[:, TOK * kt:TOK * (kt + 1)]
                        sqb = sq[:, TOK * (kt % 2):TOK * (kt % 2 + 1)]
                        t_sq[kt % 2] = SC.emit("activation", sqb, xkt, AF.Square,
                                               deps=[t_x] + ([t_ssqmm] if kt >= 2 else []))
                        for h in range(2):
                            TE.emit("matmul", ps_sum[:, 512 * h:512 * (h + 1)],
                                    on_sb[:, 0:1], xkt[:, 512 * h:512 * (h + 1)],
                                    start=(kt == 0), stop=(kt == NKT - 1),
                                    deps=[t_x, t_const])
                        for h in range(2):
                            t_ssqmm = TE.emit("matmul", ps_ssq[:, 512 * h:512 * (h + 1)],
                                              on_sb[:, 0:1], sqb[:, 512 * h:512 * (h + 1)],
                                              start=(kt == 0), stop=(kt == NKT - 1),
                                              deps=[t_sq[kt % 2]])
                    t_summ = TE.tok()
                    VE.emit("tensor_scalar_mul", stat[:, 0:TOK], ps_sum[:], 1.0 / D,
                            deps=[t_summ])
                    VE.self_wait()
                    VE.emit("tensor_tensor", stat[:, TOK:2 * TOK], stat[:, 0:TOK],
                            stat[:, 0:TOK], ALU.mult)
                    VE.emit("tensor_scalar_mul", stat[:, 2 * TOK:3 * TOK], ps_ssq[:],
                            1.0 / D, deps=[t_summ])
                    VE.self_wait()
                    VE.emit("tensor_sub", stat[:, 2 * TOK:3 * TOK],
                            stat[:, 2 * TOK:3 * TOK], stat[:, TOK:2 * TOK])
                    VE.self_wait()
                    t_var = VE.emit("tensor_scalar_add", stat[:, 2 * TOK:3 * TOK],
                                    stat[:, 2 * TOK:3 * TOK], EPS)
                    VE.self_wait()
                    t_ln = SC.emit("activation", stat[:, 3 * TOK:4 * TOK],
                                   stat[:, 2 * TOK:3 * TOK], AF.Ln, deps=[t_var])
                    t_half = VE.emit("tensor_scalar_mul", stat[:, 3 * TOK:4 * TOK],
                                     stat[:, 3 * TOK:4 * TOK], -0.5, deps=[t_ln])
                    VE.self_wait()
                    t_rs = SC.emit("activation", stat[:, 3 * TOK:4 * TOK],
                                   stat[:, 3 * TOK:4 * TOK], AF.Exp, deps=[t_half])
                with ExitStack() as c3:
                    E3 = c3.enter_context
                    ps_mb = E3(nc.psum_tensor(f"psmb{name}", [128, TOK], F32))
                    ps_rb = E3(nc.psum_tensor(f"psrb{name}", [128, TOK], F32))
                    TE.self_wait()
                    for h in range(2):
                        TE.emit("matmul", ps_mb[:, 512 * h:512 * (h + 1)], on_sb[0:1, :],
                                stat[0:1, 512 * h:512 * (h + 1)], start=True, stop=True,
                                deps=[t_rs])
                    for h in range(2):
                        t_bc = TE.emit("matmul", ps_rb[:, 512 * h:512 * (h + 1)],
                                       on_sb[0:1, :],
                                       stat[0:1, 3 * TOK + 512 * h:3 * TOK + 512 * (h + 1)],
                                       start=True, stop=True)
                    VE.emit("tensor_copy", mb_sb[:], ps_mb[:], deps=[t_bc])
                    t_rbc = VE.emit("tensor_copy", rb_sb[:], ps_rb[:])
                    VE.self_wait()
                prev = [None, None]
                for kt in range(NKT):
                    xkt = xT[:, TOK * kt:TOK * (kt + 1)]
                    lbuf = lnf[:, TOK * (kt % 2):TOK * (kt % 2 + 1)]
                    dep_reuse = list(prev[kt % 2] or ())
                    VE.emit("tensor_sub", lbuf, xkt, mb_sb[:],
                            deps=[t_rbc] + [d for d in dep_reuse if d])
                    VE.self_wait()
                    t_n = VE.emit("tensor_tensor", lbuf, lbuf, rb_sb[:], ALU.mult)
                    VE.self_wait()
                    t_cast = SC.emit("activation", dst_bf16[:, TOK * kt:TOK * (kt + 1)],
                                     lbuf, AF.Copy, deps=[t_n])
                    t_cons = fp32_consumer(kt, lbuf, t_n) if fp32_consumer else None
                    prev[kt % 2] = (t_cast, t_cons)
                barrier()

        with ExitStack() as cA:
            EA = cA.enter_context
            ln1o = EA(nc.sbuf_tensor("ln1o", [128, NKT * TOK], BF16))

            layernorm(ln1o, name="l1")
            t_st1 = GP.emit("dma_start",
                            out=ln1_own_d.rearrange("(k p) t -> p k t", p=128),
                            in_=ln1o.rearrange("p (k t) -> p k t", k=NKT))
            GP.wait(t_st1)
            nc.gpsimd.collective_compute("AllGather", ALU.bypass,
                                         [list(range(NCORES))], [ln1_own_d],
                                         [ln1_all]).then_inc(sem_ag1, 1)
            t_ag1 = ("ag1", 1, sem_ag1)
            if _DEBUG:
                GP.emit("dma_start", out=d_ln1, in_=ln1o[:], deps=[t_st1])

            with ExitStack() as cB:
                EB = cB.enter_context
                msk_sb = EB(nc.sbuf_tensor("msk_sb", [128, 4 * HPG * QW], BF16))
                pq_pad = EB(nc.sbuf_tensor("pq_pad", [128, S], BF16))
                pk_pad = EB(nc.sbuf_tensor("pk_pad", [128, S], BF16))
                pv_pad = EB(nc.sbuf_tensor("pv_pad", [128, SKT * HPG * 9], BF16))
                t_mskl = SY.emit("dma_start", out=msk_sb[:], in_=masks)

                if _PHASES <= 0:
                    return nc
                with ExitStack() as c2:
                    E2 = c2.enter_context
                    lns = E2(nc.sbuf_tensor("lns", [128, NKT * 512], BF16))
                    wq_sb = E2(nc.sbuf_tensor("wq_sb", [128, NKT * 128], BF16))
                    wk_sb = E2(nc.sbuf_tensor("wk_sb", [128, NKT * 128], BF16))
                    wv_sb = E2(nc.sbuf_tensor("wv_sb", [128, NKT * 32], BF16))
                    prq = E2(nc.sbuf_tensor("prq", [128, 512], BF16))
                    esk = E2(nc.sbuf_tensor("esk", [128, HPG], F32))
                    ps_q = [E2(nc.psum_tensor(f"ps_q{i}", [128, 512], F32)) for i in range(2)]
                    ps_k = [E2(nc.psum_tensor(f"ps_k{i}", [128, 512], F32)) for i in range(2)]
                    ps_v = [E2(nc.psum_tensor(f"ps_v{i}", [128, 32], F32)) for i in range(2)]
                    ps_sk = E2(nc.psum_tensor("ps_sk", [128, HPG], F32))
                    bq = [Buf(), Buf()]
                    bk = [Buf(), Buf()]
                    bv = [Buf(), Buf()]
                    bsk, blns, bprq, besk = Buf(), Buf(), Buf(), Buf()

                    t_wq = GP.emit("dma_start",
                                   out=wq_sb.rearrange("p (k c) -> p k c", k=NKT),
                                   in_=wqp.rearrange("(k p) c -> p k c", p=128))
                    t_wk = GP.emit("dma_start",
                                   out=wk_sb.rearrange("p (k c) -> p k c", k=NKT),
                                   in_=wkp.rearrange("(k p) c -> p k c", p=128))
                    t_wv = GP.emit("dma_start",
                                   out=wv_sb.rearrange("p (k c) -> p k c", k=NKT),
                                   in_=wv.rearrange("(k p) c -> p k c", p=128))
                    t_pv1 = VE.emit(
                        "memset",
                        pv_pad.rearrange("p (k h n) -> p k h n", k=SKT, h=HPG)[:, :, :, 8:9],
                        1.0)
                    ln1v = ln1_all.rearrange("(r k p) t -> r k p t", r=NCORES, k=NKT)
                    for ch in range(16):
                        r, s0 = ch // 2, 512 * (ch % 2)
                        t_ld = SY.emit("dma_start",
                                       out=lns.rearrange("p (k t) -> p k t", k=NKT),
                                       in_=ln1v[r, :, :, s0:s0 + 512].rearrange(
                                           "k p t -> p k t"),
                                       deps=[t_ag1] + blns.wdeps())
                        blns.wr(t_ld)
                        pq_, bq_ = ps_q[ch % 2], bq[ch % 2]
                        for kt in range(NKT):
                            t_mm = TE.emit("matmul", pq_[:],
                                           wq_sb[:, 128 * kt:128 * (kt + 1)],
                                           lns[:, 512 * kt:512 * (kt + 1)],
                                           start=(kt == 0), stop=(kt == NKT - 1),
                                           deps=[t_ld, t_wq] + (bq_.wdeps() if kt == 0 else []))
                        bq_.wr(t_mm)
                        blns.rd(t_mm)
                        t_sg = SC.emit("activation", prq[:], pq_[:], AF.Sigmoid,
                                       deps=[t_mm] + bprq.wdeps())
                        bq_.rd(t_sg)
                        bprq.wr(t_sg)
                        t_pq = VE.emit("tensor_scalar_mul",
                                       pq_pad[:, 512 * ch:512 * (ch + 1)], prq[:],
                                       pm_sb[:, 0:1], deps=[t_sg, t_const])
                        bprq.rd(t_pq)
                        pk_, bk_ = ps_k[ch % 2], bk[ch % 2]
                        for kt in range(NKT):
                            t_mm = TE.emit("matmul", pk_[:],
                                           wk_sb[:, 128 * kt:128 * (kt + 1)],
                                           lns[:, 512 * kt:512 * (kt + 1)],
                                           start=(kt == 0), stop=(kt == NKT - 1),
                                           deps=[t_wk] + (bk_.wdeps() if kt == 0 else []))
                        bk_.wr(t_mm)
                        blns.rd(t_mm)
                        t_pk = SC.emit("activation", pk_pad[:, 512 * ch:512 * (ch + 1)],
                                       pk_[:], AF.Sigmoid, deps=[t_mm])
                        bk_.rd(t_pk)
                        for tt in range(4):
                            kk = 4 * ch + tt
                            pv_, bv_ = ps_v[tt % 2], bv[tt % 2]
                            for kt in range(NKT):
                                t_mm = TE.emit(
                                    "matmul", pv_[:],
                                    lns[:, 512 * kt + 128 * tt:512 * kt + 128 * (tt + 1)],
                                    wv_sb[:, 32 * kt:32 * (kt + 1)],
                                    start=(kt == 0), stop=(kt == NKT - 1),
                                    deps=[t_wv] + (bv_.wdeps() if kt == 0 else []))
                            bv_.wr(t_mm)
                            blns.rd(t_mm)
                            pvv = pv_pad.rearrange("p (k h n) -> p k h n", k=SKT, h=HPG)
                            t_ev = SC.emit("activation", pvv[:, kk, :, 0:8],
                                           pv_.ap().rearrange("p (h n) -> p h n", h=HPG),
                                           AF.Sigmoid, deps=[t_mm, t_pv1])
                            bv_.rd(t_ev)
                            TE.self_wait()
                            t_skm = TE.emit("matmul", ps_sk[:],
                                            pk_pad[:, 128 * kk:128 * (kk + 1)], bd_sb[:],
                                            start=True, stop=True,
                                            deps=[t_pk] + bsk.wdeps())
                            bsk.wr(t_skm)
                            t_esk = SC.emit("activation", esk[:], ps_sk[:], AF.Exp,
                                            scale=-1.0, deps=[t_skm] + besk.wdeps())
                            bsk.rd(t_esk)
                            besk.wr(t_esk)
                            for h in range(HPG):
                                pvh = pvv[:, kk, h, 0:9]
                                t_sc9 = VE.emit("tensor_scalar_mul", pvh, pvh,
                                                esk[:, h:h + 1], deps=[t_esk, t_ev])
                                besk.rd(t_sc9)
                            VE.self_wait()
                barrier()

                if _PHASES <= 1:
                    return nc
                with ExitStack() as c2:
                    E2 = c2.enter_context
                    ps_dot = E2(nc.psum_tensor("ps_dot", [128, HPG * QW], F32))
                    ps_nd = E2(nc.psum_tensor("ps_nd", [128, QW], F32))
                    E_sb = E2(nc.sbuf_tensor("E_sb", [128, 2 * HPG * QW], BF16))
                    num_sb = E2(nc.sbuf_tensor("num_sb", [128, QW], F32))
                    rec_sb = E2(nc.sbuf_tensor("rec_sb", [128, QW], F32))
                    att_sb = E2(nc.sbuf_tensor("att_sb", [128, QW], F32))
                    atb_sb = E2(nc.sbuf_tensor("atb_sb", [128, QW], BF16))
                    bdh = [Buf(), Buf()]
                    bE = [Buf(), Buf()]
                    bnd, bnsb, brec, batt, batb = (Buf() for _ in range(5))

                    t_z = VE.emit("memset", ps_nd[:], 0.0)
                    bnd.wr(t_z)
                    t_stc = None
                    git = 0
                    for j in range(NQC):
                        q0 = QW * j
                        Kj = 4 * (j + 1)
                        pend = None

                        def emit_av(pkk, pp, pw, last):
                            TE.self_wait()
                            for h in range(HPG):
                                t_av = TE.emit(
                                    "matmul", ps_nd[32 * h:32 * h + 9, :],
                                    pv_pad[:, (HPG * pkk + h) * 9:(HPG * pkk + h) * 9 + 9],
                                    E_sb[:, pp * HPG * QW + QW * h:pp * HPG * QW + QW * h + QW],
                                    start=(pkk == 0), stop=last,
                                    tile_position=(0, 32 * h), skip_group_check=True,
                                    deps=([pw] + (bnd.wdeps() if pkk == 0 else []))
                                    if h == 0 else [])
                            bnd.wr(t_av)
                            bE[pp].rd(t_av)
                            TE.self_wait()
                            return t_av

                        for kk in range(Kj):
                            par = git % 2
                            for half in range(2):
                                for h in (2 * half, 2 * half + 1):
                                    t_dot = TE.emit(
                                        "matmul", ps_dot[:, QW * h:QW * (h + 1)],
                                        pk_pad[32 * h:32 * h + 32, 128 * kk:128 * (kk + 1)],
                                        pq_pad[32 * h:32 * h + 32, q0:q0 + QW],
                                        start=True, stop=True, tile_position=(32 * h, 0),
                                        deps=(bdh[half].wdeps() if h == 2 * half else []))
                                bdh[half].wr(t_dot)
                            t_exp = None
                            for half in range(2):
                                Ebh = E_sb[:, par * HPG * QW + 2 * QW * half:
                                           par * HPG * QW + 2 * QW * (half + 1)]
                                t_exp = SC.emit(
                                    "activation", Ebh,
                                    ps_dot[:, 2 * QW * half:2 * QW * (half + 1)],
                                    AF.Exp, scale=2.0,
                                    deps=[bdh[half].w] + (bE[par].wdeps() if half == 0 else []))
                                bdh[half].rd(t_exp)
                                if half == 0:
                                    bE[par].wr(t_exp)
                            bE[par].wr(t_exp)
                            dpat = kk - 4 * j
                            if 0 <= dpat <= 3:
                                moff = dpat * HPG * QW
                                t_m = VE.emit("tensor_tensor",
                                              E_sb[:, par * HPG * QW:(par + 1) * HPG * QW],
                                              E_sb[:, par * HPG * QW:(par + 1) * HPG * QW],
                                              msk_sb[:, moff:moff + HPG * QW], ALU.mult,
                                              deps=[t_exp, t_mskl])
                                bE[par].wr(t_m)
                            if pend is not None:
                                emit_av(pend[0], pend[1], pend[2], False)
                            pend = (kk, par, bE[par].w)
                            git += 1
                        t_av = emit_av(pend[0], pend[1], pend[2], True)
                        t_evn = VE.emit("tensor_copy", num_sb[:], ps_nd[:],
                                        deps=[t_av] + bnsb.wdeps())
                        bnd.rd(t_evn)
                        bnsb.wr(t_evn)
                        VE.self_wait()
                        TE.self_wait()
                        t_den = TE.emit("matmul", ps_nd[:], sel_sb[:], num_sb[:],
                                        start=True, stop=True,
                                        deps=[t_evn] + bnd.wdeps())
                        bnsb.rd(t_den)
                        bnd.wr(t_den)
                        TE.self_wait()
                        t_rc = VE.emit("reciprocal", rec_sb[:], ps_nd[:],
                                       deps=[t_den] + brec.wdeps())
                        bnd.rd(t_rc)
                        brec.wr(t_rc)
                        VE.self_wait()
                        t_at = VE.emit("tensor_tensor", att_sb[:], num_sb[:],
                                       rec_sb[:], ALU.mult, deps=batt.wdeps())
                        bnsb.rd(t_at)
                        brec.rd(t_at)
                        batt.wr(t_at)
                        VE.self_wait()
                        t_ab = VE.emit("tensor_scalar", atb_sb[:], att_sb[:],
                                       vd_sb[:, 0:1], ve0_sb[:, 0:1], ALU.mult, ALU.add,
                                       deps=[t_vd] + batb.wdeps())
                        batt.rd(t_ab)
                        batb.wr(t_ab)
                        VE.self_wait()
                        t_stc = GP.emit("dma_start", out=attn_own_d[:, q0:q0 + QW],
                                        in_=atb_sb[:], deps=[t_ab])
                        batb.rd(t_stc)
                    GP.wait(t_stc)
                    nc.gpsimd.collective_compute("AllGather", ALU.bypass,
                                                 [list(range(NCORES))], [attn_own_d],
                                                 [attn_all]).then_inc(sem_ag2, 1)
                    t_ag2 = ("ag2", 1, sem_ag2)
                    if _DEBUG:
                        GP.emit("dma_start", out=d_attn, in_=attn_own_d, deps=[t_stc])
                barrier()
            barrier()

            if _PHASES <= 2:
                return nc
            with ExitStack() as c2:
                E2 = c2.enter_context
                atf = E2(nc.sbuf_tensor("atf", [128, S], BF16))
                atn = E2(nc.sbuf_tensor("atn", [128, 8 * TOK], BF16))
                wo_sb = E2(nc.sbuf_tensor("wo_sb", [128, 2 * 8 * 128], BF16))
                wgt_sb = E2(nc.sbuf_tensor("wgt_sb", [128, NKT * 128], BF16))
                gat_sb = E2(nc.sbuf_tensor("gat_sb", [128, TOK], BF16))
                tmp_sb = E2(nc.sbuf_tensor("tmp_sb", [128, TOK], F32))
                ps_wo = [E2(nc.psum_tensor(f"ps_wo{i}", [128, TOK], F32)) for i in range(2)]
                ps_gt = [E2(nc.psum_tensor(f"ps_gt{i}", [128, TOK], F32)) for i in range(2)]
                bwo = [Buf(), Buf()]
                bgt = [Buf(), Buf()]
                bwos = [Buf(), Buf()]
                bwgs, bgat, batf = Buf(), Buf(), Buf()
                batn = Buf()
                av = attn_all.rearrange("(g p) t -> g p t", p=128)
                for g in range(8):
                    t_af = SY.emit("dma_start", out=atf[:], in_=av[g],
                                   deps=[t_ag2] + batf.wdeps())
                    batf.wr(t_af)
                    dst = atn[:, TOK * g:TOK * (g + 1)]
                    t_a0 = VE.emit("tensor_scalar_mul", dst, atf[:, 0:TOK],
                                   ms_sb[:, 0:1], deps=[t_af, t_msel] + batn.wdeps())
                    for b in range(1, NCORES):
                        VE.self_wait()
                        t_a0 = VE.emit("scalar_tensor_tensor", dst,
                                       atf[:, TOK * b:TOK * (b + 1)], ms_sb[:, b:b + 1],
                                       dst, ALU.mult, ALU.add)
                    VE.self_wait()
                    batn.wr(t_a0)
                    batf.rd(t_a0)
                t_atn = VE.tok()
                for dm in range(NKT):
                    p = dm % 2
                    t_wol = GP.emit("dma_start",
                                    out=wo_sb[:, p * 1024:(p + 1) * 1024].rearrange(
                                        "p (g c) -> p g c", g=8),
                                    in_=wopad[dm].rearrange("(g p) c -> p g c", p=128),
                                    deps=bwos[p].wdeps())
                    bwos[p].wr(t_wol)
                    t_wgl = GP.emit("dma_start",
                                    out=wgt_sb.rearrange("p (k c) -> p k c", k=NKT),
                                    in_=wgate[dm].rearrange("(k p) c -> p k c", p=128),
                                    deps=bwgs.wdeps())
                    bwgs.wr(t_wgl)
                    for h in range(2):
                        for g in range(8):
                            t_womm = TE.emit(
                                "matmul", ps_wo[p][:, 512 * h:512 * (h + 1)],
                                wo_sb[:, p * 1024 + 128 * g:p * 1024 + 128 * (g + 1)],
                                atn[:, TOK * g + 512 * h:TOK * g + 512 * (h + 1)],
                                start=(g == 0), stop=(g == 7),
                                deps=[t_wol, t_atn] +
                                     (bwo[p].wdeps() if g == 0 and h == 0 else []))
                    bwo[p].wr(t_womm)
                    bwos[p].rd(t_womm)
                    for h in range(2):
                        for kt in range(NKT):
                            t_gtm = TE.emit(
                                "matmul", ps_gt[p][:, 512 * h:512 * (h + 1)],
                                wgt_sb[:, 128 * kt:128 * (kt + 1)],
                                ln1o[:, TOK * kt + 512 * h:TOK * kt + 512 * (h + 1)],
                                start=(kt == 0), stop=(kt == NKT - 1),
                                deps=[t_wgl] +
                                     (bgt[p].wdeps() if kt == 0 and h == 0 else []))
                    bgt[p].wr(t_gtm)
                    bwgs.rd(t_gtm)
                    t_sg = SC.emit("activation", gat_sb[:], ps_gt[p][:], AF.Sigmoid,
                                   deps=[t_gtm] + bgat.wdeps())
                    bgt[p].rd(t_sg)
                    bgat.wr(t_sg)
                    t_t = VE.emit("tensor_tensor", tmp_sb[:], ps_wo[p][:], gat_sb[:],
                                  ALU.mult, deps=[t_sg, t_womm])
                    bwo[p].rd(t_t)
                    bgat.rd(t_t)
                    VE.self_wait()
                    VE.emit("tensor_add", xT[:, TOK * dm:TOK * (dm + 1)],
                            xT[:, TOK * dm:TOK * (dm + 1)], tmp_sb[:])
                    VE.self_wait()
                if _DEBUG:
                    GP.emit("dma_start", out=d_xmid, in_=xT[:], deps=[VE.tok()])
                    GP.emit("dma_start", out=d_atn, in_=atn[:], deps=[t_atn])
            barrier()
        barrier()

        if _PHASES <= 3:
            return nc
        with ExitStack() as cC:
            EC = cC.enter_context
            idx_sb = EC(nc.sbuf_tensor("idx_sb", [128, NE * TCAP // 128],
                                       mybir.dt.int32))
            idx2_sb = EC(nc.sbuf_tensor("idx2_sb", [128, 2 * NE * TCAP // 128],
                                        mybir.dt.int32))
            cD = EC(ExitStack())
            ln2o = cD.enter_context(nc.sbuf_tensor("ln2o", [128, NKT * TOK], BF16))
            combT = cD.enter_context(nc.sbuf_tensor("combT", [NE, TOK], BF16))

            with ExitStack() as c2:
                E2 = c2.enter_context
                gw_sb = E2(nc.sbuf_tensor("gw_sb", [128, NKT * NE], F32))
                ps_sc = E2(nc.psum_tensor("ps_sc", [NE, TOK], F32))
                sc_sb = E2(nc.sbuf_tensor("sc_sb", [NE, TOK], F32))
                t_gwl = SY.emit("dma_start",
                                out=gw_sb.rearrange("p (k c) -> p k c", k=NKT),
                                in_=gate_w.rearrange("(k p) c -> p k c", p=128))
                t_scm = [None]

                def consume(kt, lbuf, t_n):
                    for h in range(2):
                        t_scm[0] = TE.emit("matmul", ps_sc[:, 512 * h:512 * (h + 1)],
                                           gw_sb[:, NE * kt:NE * (kt + 1)],
                                           lbuf[:, 512 * h:512 * (h + 1)],
                                           start=(kt == 0), stop=(kt == NKT - 1),
                                           deps=[t_n, t_gwl])
                    return t_scm[0]

                layernorm(ln2o, fp32_consumer=consume, name="l2")
                t_sig = SC.emit("activation", sc_sb[:], ps_sc[:], AF.Sigmoid,
                                deps=[t_scm[0]])
                sct = E2(nc.sbuf_tensor("sct", [128, NE], F32))
                work = E2(nc.sbuf_tensor("work", [128, NE], F32))
                tmq = E2(nc.sbuf_tensor("tmq", [128, NE], F32))
                msum = E2(nc.sbuf_tensor("msum", [128, 8], F32))
                comb = E2(nc.sbuf_tensor("comb", [128, NE], F32))
                ps_t1 = E2(nc.psum_tensor("ps_t1", [128, NE], F32))
                ps_t2 = E2(nc.psum_tensor("ps_t2", [NE, 128], F32))
                bt1, bt2 = Buf(), Buf()
                t_cstore = None
                for tt in range(8):
                    TE.self_wait()
                    t_tr = TE.emit("transpose", ps_t1[:],
                                   sc_sb[:, 128 * tt:128 * (tt + 1)],
                                   id_sb[0:NE, 0:NE], deps=[t_sig] + bt1.wdeps())
                    bt1.wr(t_tr)
                    t_sct = VE.emit("tensor_copy", sct[:], ps_t1[:], deps=[t_tr])
                    bt1.rd(t_sct)
                    VE.self_wait()
                    VE.emit("tensor_copy", work[:], sct[:])
                    VE.emit("memset", msum[:, 0:1], 0.0)
                    VE.self_wait()
                    for it in range(TOPK):
                        VE.emit("reduce_max", msum[:, 1:2], work[:], AX.X)
                        VE.self_wait()
                        VE.emit("tensor_add", msum[:, 0:1], msum[:, 0:1], msum[:, 1:2])
                        if it < TOPK - 1:
                            VE.emit("tensor_scalar", tmq[:], work[:], msum[:, 1:2],
                                    None, ALU.is_equal)
                            VE.self_wait()
                            VE.emit("scalar_tensor_tensor", work[:], tmq[:], -100.0,
                                    work[:], ALU.mult, ALU.add)
                            VE.self_wait()
                    VE.self_wait()
                    VE.emit("tensor_scalar", tmq[:], sct[:], msum[:, 1:2], None,
                            ALU.is_ge)
                    VE.emit("tensor_scalar_add", msum[:, 2:3], msum[:, 0:1], 1e-20)
                    VE.self_wait()
                    VE.emit("reciprocal", msum[:, 3:4], msum[:, 2:3])
                    VE.self_wait()
                    VE.emit("tensor_scalar", comb[:], sct[:], msum[:, 3:4], None,
                            ALU.mult, deps=([t_cstore] if tt > 0 else []))
                    VE.self_wait()
                    t_cb = VE.emit("tensor_tensor", comb[:], comb[:], tmq[:], ALU.mult)
                    VE.self_wait()
                    t_cstore = GP.emit("dma_start",
                                       out=comb_tok_d[128 * tt:128 * (tt + 1), :],
                                       in_=comb[:], deps=[t_cb])
                    bcombrow.rd(t_cstore) if False else None
                    TE.self_wait()
                    t_tr2 = TE.emit("transpose", ps_t2[:], comb[:], id_sb[:],
                                    deps=[t_cb] + bt2.wdeps())
                    bt2.wr(t_tr2)
                    t_cT = VE.emit("tensor_copy", combT[0:NE, 128 * tt:128 * (tt + 1)],
                                   ps_t2[:], deps=[t_tr2])
                    bt2.rd(t_cT)
                    VE.self_wait()
                if _DEBUG:
                    GP.emit("dma_start", out=d_ln2, in_=ln2o[:], deps=[VE.tok()])
                    GP.emit("dma_start", out=d_comb, in_=combT[:], deps=[VE.tok()])
                zero_sb = E2(nc.sbuf_tensor("zero_sb", [128, D], F32))
                t_zz0 = VE.emit("memset", zero_sb[:], 0.0)
                VE.self_wait()
                t_rz = None
                for i in range(4):
                    for tt in range(8):
                        t_rz = GP.emit("dma_start",
                                       out=racc[i][128 * tt:128 * (tt + 1), :],
                                       in_=zero_sb[:], deps=[t_zz0])
                ps_tb = E2(nc.psum_tensor("ps_tb", [128, 128], BF16))
                lntt = E2(nc.sbuf_tensor("lntt", [128, D], BF16))
                idb2 = E2(nc.sbuf_tensor("idb2", [128, 128], BF16))
                t_idb = VE.emit("tensor_copy", idb2[:], id_sb[:])
                VE.self_wait()
                btb, blntt = Buf(), Buf()
                for tt in range(8):
                    for kt in range(NKT):
                        TE.self_wait()
                        t_tr = TE.emit("transpose", ps_tb[:],
                                       ln2o[:, TOK * kt + 128 * tt:TOK * kt + 128 * (tt + 1)],
                                       idb2[:], deps=[t_idb] + btb.wdeps())
                        btb.wr(t_tr)
                        t_ev = VE.emit("tensor_copy", lntt[:, 128 * kt:128 * (kt + 1)],
                                       ps_tb[:], deps=[t_tr] + (blntt.wdeps() if kt == 0 else []))
                        btb.rd(t_ev)
                        VE.self_wait()
                    t_lnst = GP.emit("dma_start", out=ln2tok_d[128 * tt:128 * (tt + 1), :],
                                     in_=lntt[:], deps=[VE.tok()])
                    blntt.wr(t_lnst) if False else None
                    blntt.rd(t_lnst)
                t_ln2tok = t_lnst
                msk = E2(nc.sbuf_tensor("msk", [NE, TOK], F32))
                zsc = E2(nc.sbuf_tensor("zsc", [NE, TOK], F32))
                posx = E2(nc.sbuf_tensor("posx", [NE, TOK], F32))
                post_i = E2(nc.sbuf_tensor("post_i", [128, 8 * NE], mybir.dt.int32))
                tokid_sb = E2(nc.sbuf_tensor("tokid_sb", [128, 8], mybir.dt.int32))
                idxf = E2(nc.sbuf_tensor("idxf", [96, 128], mybir.dt.int32))
                t_tid = SY.emit("dma_start", out=tokid_sb[:], in_=tokid)
                eoff_sb = E2(nc.sbuf_tensor("eoff_sb", [NE, 1], F32))
                t_eo = SY.emit("dma_start", out=eoff_sb[:], in_=eoff)
                VE.emit("tensor_scalar", msk[:], combT[:], 0.0, None, ALU.is_gt,
                        deps=[t_cT])
                VE.emit("memset", zsc[:], 0.0)
                VE.self_wait()
                VE.emit("tensor_tensor_scan", posx[:], msk[:], zsc[:], 0.0,
                        ALU.add, ALU.add)
                VE.self_wait()
                VE.emit("tensor_scalar_add", posx[:], posx[:], -1.0)
                VE.self_wait()
                VE.emit("tensor_scalar", zsc[:], msk[:], -1.0, 1.0, ALU.mult, ALU.add)
                VE.self_wait()
                VE.emit("scalar_tensor_tensor", posx[:], zsc[:], 1.0e6,
                        posx[:], ALU.mult, ALU.add)
                VE.self_wait()
                t_posx = VE.emit("tensor_scalar_add", posx[:], posx[:],
                                 eoff_sb[:, 0:1], deps=[t_eo])
                VE.self_wait()
                t_if = VE.emit("memset", idxf[:], 100000)
                VE.self_wait()
                t_ifd = GP.emit("dma_start",
                                out=idx_d.rearrange("e (a p) -> (e a) p", p=128),
                                in_=idxf[:], deps=[t_if])
                t_scat = None
                for tt in range(8):
                    TE.self_wait()
                    t_trp = TE.emit("transpose", ps_t1[:],
                                    posx[:, 128 * tt:128 * (tt + 1)],
                                    id_sb[0:NE, 0:NE], deps=[t_posx] + bt1.wdeps())
                    bt1.wr(t_trp)
                    t_pi = VE.emit("tensor_copy", post_i[:, NE * tt:NE * (tt + 1)],
                                   ps_t1[:], deps=[t_trp])
                    bt1.rd(t_pi)
                    VE.self_wait()
                    for e in range(NE):
                        t_scat = GP.emit("indirect_dma_start",
                                         idx_d.rearrange("e (t o) -> (e t) o", o=1),
                                         IndirectOffsetOnAxis(
                                             post_i[:, NE * tt + e:NE * tt + e + 1], 0),
                                         tokid_sb[:, tt:tt + 1], None,
                                         bounds_check=r_btcap, oob_is_err=False,
                                         deps=[t_pi, t_tid, t_ifd])
                for e in range(NE):
                    for a in range(3):
                        t_idxload = GP.emit(
                            "dma_start", out=idx_sb[:, 3 * e + a:3 * e + a + 1],
                            in_=idx_d[e:e + 1, 128 * a:128 * (a + 1)].rearrange(
                                "e p -> p e"),
                            deps=[t_scat])
                if _DEBUG:
                    GP.emit("dma_start", out=d_idx, in_=idx_d, deps=[t_scat])
                for dmh in range(2):
                    VE.emit("tensor_scalar", idx2_sb[:, 96 * dmh:96 * (dmh + 1)],
                            idx_sb[:], 2, dmh, ALU.mult, ALU.add, deps=[t_idxload])
                VE.self_wait()
                t_idx2 = VE.tok()
            barrier()

            if _PHASES <= 4:
                return nc
            if _PHASES <= 5:
                return nc
            with ExitStack() as c2:
                E2 = c2.enter_context
                shact = E2(nc.sbuf_tensor("shact", [128, 12 * TOK], BF16))
                wsh_sb = E2(nc.sbuf_tensor("wsh_sb", [128, 4 * NKT * 128], BF16))
                wsd_sb = E2(nc.sbuf_tensor("wsd_sb", [128, 2 * 12 * 128], BF16))
                silu2 = E2(nc.sbuf_tensor("silu2", [128, TOK], BF16))
                ps_a = E2(nc.psum_tensor("ps_a2", [128, TOK], F32))
                ps_b = E2(nc.psum_tensor("ps_b2", [128, TOK], F32))
                ps_c = E2(nc.psum_tensor("ps_c2", [128, TOK], F32))
                ps_d = E2(nc.psum_tensor("ps_d2", [128, TOK], F32))
                bpa, bpb, bpc, bpd = Buf(), Buf(), Buf(), Buf()
                bwsh = [Buf(), Buf()]
                bwsd = [Buf(), Buf()]
                bsilu, bsh = Buf(), Buf()
                bxt = [Buf() for _ in range(NKT)]
                for m in range(12):
                    pbuf = m % 2
                    woff = pbuf * 2 * NKT * 128
                    t_lg = GP.emit("dma_start",
                                   out=wsh_sb[:, woff:woff + NKT * 128].rearrange(
                                       "p (k c) -> p k c", k=NKT),
                                   in_=sgm[m].rearrange("(k p) c -> p k c", p=128),
                                   deps=bwsh[pbuf].wdeps())
                    t_lu = GP.emit("dma_start",
                                   out=wsh_sb[:, woff + NKT * 128:woff + 2 * NKT * 128]
                                   .rearrange("p (k c) -> p k c", k=NKT),
                                   in_=sum_m[m].rearrange("(k p) c -> p k c", p=128))
                    bwsh[pbuf].wr(t_lu)
                    pg, bg = (ps_a, bpa) if m % 2 == 0 else (ps_b, bpb)
                    pu, bu = (ps_c, bpc) if m % 2 == 0 else (ps_d, bpd)
                    for h in range(2):
                        for kt in range(NKT):
                            t_g = TE.emit(
                                "matmul", pg[:, 512 * h:512 * (h + 1)],
                                wsh_sb[:, woff + 128 * kt:woff + 128 * (kt + 1)],
                                ln2o[:, TOK * kt + 512 * h:TOK * kt + 512 * (h + 1)],
                                start=(kt == 0), stop=(kt == NKT - 1),
                                deps=[t_lg] + (bg.wdeps() if kt == 0 and h == 0 else []))
                    bg.wr(t_g)
                    bwsh[pbuf].rd(t_g)
                    for h in range(2):
                        for kt in range(NKT):
                            t_u = TE.emit(
                                "matmul", pu[:, 512 * h:512 * (h + 1)],
                                wsh_sb[:, woff + NKT * 128 + 128 * kt:
                                       woff + NKT * 128 + 128 * (kt + 1)],
                                ln2o[:, TOK * kt + 512 * h:TOK * kt + 512 * (h + 1)],
                                start=(kt == 0), stop=(kt == NKT - 1),
                                deps=[t_lu] + (bu.wdeps() if kt == 0 and h == 0 else []))
                    bu.wr(t_u)
                    bwsh[pbuf].rd(t_u)
                    t_si = SC.emit("activation", silu2[:], pg[:], AF.Silu,
                                   deps=[t_g] + bsilu.wdeps())
                    bg.rd(t_si)
                    bsilu.wr(t_si)
                    a = shact[:, TOK * m:TOK * (m + 1)]
                    t_a1 = VE.emit("tensor_tensor", a, silu2[:], pu[:], ALU.mult,
                                   deps=[t_si, t_u])
                    bu.rd(t_a1)
                    bsilu.rd(t_a1)
                    bsh.wr(t_a1)
                    VE.self_wait()
                for dm in range(NKT):
                    pbuf = dm % 2
                    woff = pbuf * 12 * 128
                    t_lw = GP.emit("dma_start",
                                   out=wsd_sb[:, woff:woff + 12 * 128].rearrange(
                                       "p (f c) -> p f c", f=12),
                                   in_=sdm[dm].rearrange("(f p) c -> p f c", p=128),
                                   deps=bwsd[pbuf].wdeps())
                    bwsd[pbuf].wr(t_lw)
                    pr, br = (ps_a, bpa) if dm % 2 == 0 else (ps_b, bpb)
                    for h in range(2):
                        for f in range(12):
                            t_r = TE.emit(
                                "matmul", pr[:, 512 * h:512 * (h + 1)],
                                wsd_sb[:, woff + 128 * f:woff + 128 * (f + 1)],
                                shact[:, TOK * f + 512 * h:TOK * f + 512 * (h + 1)],
                                start=(f == 0), stop=(f == 11),
                                deps=[t_lw, bsh.w] +
                                     (br.wdeps() if f == 0 and h == 0 else []))
                        br.wr(t_r)
                    bwsd[pbuf].rd(t_r)
                    bsh.rd(t_r)
                    t_add = VE.emit("tensor_add", xT[:, TOK * dm:TOK * (dm + 1)],
                                    xT[:, TOK * dm:TOK * (dm + 1)], pr[:],
                                    deps=[t_r] + bxt[dm].wdeps())
                    br.rd(t_add)
                    bxt[dm].wr(t_add)
                    VE.self_wait()
            barrier()

            cD.close()
            barrier()
            if _PHASES <= 6:
                return nc
            with ExitStack() as c2:
                E2 = c2.enter_context
                xg = [E2(nc.sbuf_tensor(f"xg{i}", [128, NKT * TCAP], BF16))
                      for i in range(2)]
                actT = [E2(nc.sbuf_tensor(f"actT{i}", [128, 6 * TCAP], BF16))
                        for i in range(2)]
                gst = E2(nc.sbuf_tensor("gst", [128, D], BF16))
                cbg = [E2(nc.sbuf_tensor(f"cbg{i}", [128, 3 * NE], F32))
                       for i in range(2)]
                wgu_sb = E2(nc.sbuf_tensor("wgu_sb", [128, 4 * NKT * 128], BF16))
                wdd_sb = E2(nc.sbuf_tensor("wdd_sb", [128, 6 * D], BF16))
                silu_sb = E2(nc.sbuf_tensor("silu_sb", [128, TCAP], BF16))
                rout = [E2(nc.sbuf_tensor(f"rout{i}", [128, 1024], F32))
                        for i in range(2)]
                ps_g = [E2(nc.psum_tensor(f"ps_g{i}", [128, TCAP], F32)) for i in range(2)]
                ps_u = [E2(nc.psum_tensor(f"ps_u{i}", [128, TCAP], F32)) for i in range(2)]
                ps_r = [E2(nc.psum_tensor(f"ps_r{i}", [128, 1024], F32)) for i in range(2)]
                bxg = [Buf(), Buf()]
                bact = [Buf(), Buf()]
                bgst = Buf()
                bstg = [Buf(), Buf()]
                bcbg = [Buf(), Buf()]
                bwgu = [Buf(), Buf()]
                bwdd = [Buf(), Buf()]
                bpg = [Buf(), Buf()]
                bpu = [Buf(), Buf()]
                bpr = [Buf(), Buf()]
                brt = [Buf(), Buf()]
                bracc = [Buf(), Buf(), Buf(), Buf()]
                bsilu = Buf()
                for i in range(4):
                    bracc[i].wr(t_rz)
                def prefetch(e):
                    buf = e % 2
                    t_stg = None
                    for a in range(3):
                        t_ga = GP.emit("indirect_dma_start", gst[:], None, ln2tok_d,
                                       IndirectOffsetOnAxis(
                                           idx_sb[:, 3 * e + a:3 * e + a + 1], 0),
                                       bounds_check=r_btok, oob_is_err=False,
                                       deps=[t_idxload, t_ln2tok] + bgst.wdeps())
                        bgst.wr(t_ga)
                        t_stg = GP.emit("dma_start",
                                        out=stage_d[buf, 128 * a:128 * (a + 1), :],
                                        in_=gst[:], deps=[t_ga] +
                                        (bstg[buf].wdeps() if a == 0 else []))
                        bgst.rd(t_stg)
                    bstg[buf].wr(t_stg)
                    t_cg = None
                    for a in range(3):
                        t_cg = GP.emit("indirect_dma_start",
                                       cbg[buf][:, NE * a:NE * (a + 1)], None,
                                       comb_tok_d,
                                       IndirectOffsetOnAxis(
                                           idx_sb[:, 3 * e + a:3 * e + a + 1], 0),
                                       bounds_check=r_btok, oob_is_err=False,
                                       deps=[t_cstore] + (bcbg[buf].wdeps() if a == 0 else []))
                    bcbg[buf].wr(t_cg)
                    t_xg = None
                    for kt in range(NKT):
                        t_xg = SY.emit("dma_start",
                                       out=xg[buf][:, TCAP * kt:TCAP * (kt + 1)],
                                       in_=stage_d[buf, :, 128 * kt:128 * (kt + 1)],
                                       transpose=True, inc=16,
                                       deps=[t_stg] + (bxg[buf].wdeps() if kt == 0 else []))
                    t_xg_sy = SY.tok()
                    bxg[buf].wr(t_xg_sy)
                    bstg[buf].rd(t_xg_sy)
                    return t_xg_sy, t_cg

                pf = {0: prefetch(0)}
                for e in range(NE):
                    buf = e % 2
                    if e + 1 < NE:
                        pf[e + 1] = prefetch(e + 1)
                    t_xg_sy, t_cg = pf.pop(e)
                    if _DEBUG and e == 0:
                        GP.emit("dma_start", out=d_xg, in_=xg[0][:],
                                deps=[t_xg_sy])
                    for m in range(6):
                        pb = m % 2
                        woff = pb * 2 * NKT * 128
                        t_lg = GP.emit("dma_start",
                                       out=wgu_sb[:, woff:woff + NKT * 128].rearrange(
                                           "p (k c) -> p k c", k=NKT),
                                       in_=wgm[e, m].rearrange("(k p) c -> p k c", p=128),
                                       deps=bwgu[pb].wdeps())
                        t_lu = GP.emit("dma_start",
                                       out=wgu_sb[:, woff + NKT * 128:woff + 2 * NKT * 128]
                                       .rearrange("p (k c) -> p k c", k=NKT),
                                       in_=wum[e, m].rearrange("(k p) c -> p k c", p=128))
                        bwgu[pb].wr(t_lu)
                        for kt in range(NKT):
                            t_g = TE.emit(
                                "matmul", ps_g[pb][:],
                                wgu_sb[:, woff + 128 * kt:woff + 128 * (kt + 1)],
                                xg[buf][:, TCAP * kt:TCAP * (kt + 1)],
                                start=(kt == 0), stop=(kt == NKT - 1),
                                deps=[t_lg, t_xg_sy] +
                                     (bpg[pb].wdeps() if kt == 0 else []))
                        bpg[pb].wr(t_g)
                        bwgu[pb].rd(t_g)
                        bxg[buf].rd(t_g)
                        for kt in range(NKT):
                            t_u = TE.emit(
                                "matmul", ps_u[pb][:],
                                wgu_sb[:, woff + NKT * 128 + 128 * kt:
                                       woff + NKT * 128 + 128 * (kt + 1)],
                                xg[buf][:, TCAP * kt:TCAP * (kt + 1)],
                                start=(kt == 0), stop=(kt == NKT - 1),
                                deps=[t_lu] + (bpu[pb].wdeps() if kt == 0 else []))
                        bpu[pb].wr(t_u)
                        bwgu[pb].rd(t_u)
                        bxg[buf].rd(t_u)
                        t_si = SC.emit("activation", silu_sb[:], ps_g[pb][:], AF.Silu,
                                       deps=[t_g] + bsilu.wdeps())
                        bpg[pb].rd(t_si)
                        bsilu.wr(t_si)
                        a_ = actT[buf][:, TCAP * m:TCAP * (m + 1)]
                        t_a1 = VE.emit("tensor_tensor", a_, silu_sb[:], ps_u[pb][:],
                                       ALU.mult,
                                       deps=[t_si, t_u] + (bact[buf].wdeps() if m == 0 else []))
                        bpu[pb].rd(t_a1)
                        bsilu.rd(t_a1)
                        VE.self_wait()
                    bact[buf].wr(VE.tok())
                    t_lw = None
                    for m in range(6):
                        t_lw = GP.emit("dma_start",
                                       out=wdd_sb[:, D * m:D * (m + 1)],
                                       in_=wdp[e, 128 * m:128 * (m + 1), :],
                                       deps=(bwdd[0].wdeps() if m == 0 else []))
                    bwdd[0].wr(t_lw)
                    for a in range(3):
                        for dmh in range(2):
                            pb = (a * 2 + dmh) % 2
                            pr = ps_r[pb]
                            first = True
                            for m in range(6):
                                for ns in range(2):
                                    t_r = TE.emit(
                                        "matmul", pr[:, 512 * ns:512 * (ns + 1)],
                                        actT[buf][:, TCAP * m + 128 * a:
                                                  TCAP * m + 128 * (a + 1)],
                                        wdd_sb[:, D * m + 1024 * dmh + 512 * ns:
                                               D * m + 1024 * dmh + 512 * (ns + 1)],
                                        start=(m == 0), stop=(m == 5),
                                        skip_group_check=True,
                                        deps=[t_lw, bact[buf].w] +
                                             (bpr[pb].wdeps() if first else []))
                                    first = False
                            bpr[pb].wr(t_r)
                            bwdd[0].rd(t_r)
                            bact[buf].rd(t_r)
                            rb = (a * 2 + dmh) % 2
                            t_ev = VE.emit("tensor_scalar", rout[rb][:], pr[:],
                                           cbg[buf][:, NE * a + e:NE * a + e + 1],
                                           None, ALU.mult,
                                           deps=[t_r, t_cg] + brt[rb].wdeps())
                            bpr[pb].rd(t_ev)
                            bcbg[buf].rd(t_ev)
                            brt[rb].wr(t_ev)
                            VE.self_wait()
                            acc = e % 4
                            t_sc2 = GP.emit(
                                "indirect_dma_start",
                                racc[acc].rearrange("t (h c) -> (t h) c", h=2),
                                IndirectOffsetOnAxis(
                                    idx2_sb[:, 96 * dmh + 3 * e + a:
                                            96 * dmh + 3 * e + a + 1], 0),
                                rout[rb][:], None,
                                bounds_check=r_b2tok, oob_is_err=False,
                                compute_op=ALU.add,
                                deps=[t_ev, t_idx2] + bracc[acc].wdeps())
                            bracc[acc].wr(t_sc2)
                            brt[rb].rd(t_sc2)
            barrier()
        barrier()

        if _PHASES <= 6:
            return nc
        with ExitStack() as c2:
            E2 = c2.enter_context
            out_sb = E2(nc.sbuf_tensor("out_sb", [128, D], F32))
            rc_sb = [E2(nc.sbuf_tensor(f"rc_sb{i}", [128, D], F32)) for i in range(4)]
            ps_t = [E2(nc.psum_tensor(f"ps_to{i}", [128, 128], F32)) for i in range(2)]
            bt = [Buf(), Buf()]
            bout = Buf()
            brc = [Buf() for _ in range(4)]
            for tt in range(8):
                t_rl = []
                for i in range(4):
                    t_rl.append(SY.emit("dma_start", out=rc_sb[i][:],
                                        in_=racc[i][128 * tt:128 * (tt + 1), :],
                                        deps=brc[i].wdeps()))
                for kt in range(NKT):
                    p = kt % 2
                    TE.self_wait()
                    t_tr = TE.emit("transpose", ps_t[p][:],
                                   xT[:, TOK * kt + 128 * tt:TOK * kt + 128 * (tt + 1)],
                                   id_sb[:], deps=bt[p].wdeps())
                    bt[p].wr(t_tr)
                    t_cp = VE.emit("tensor_copy", out_sb[:, 128 * kt:128 * (kt + 1)],
                                   ps_t[p][:],
                                   deps=[t_tr] + (bout.wdeps() if kt == 0 else []))
                    bt[p].rd(t_cp)
                    VE.self_wait()
                for i in range(4):
                    VE.emit("tensor_add", out_sb[:], out_sb[:], rc_sb[i][:],
                            deps=[t_rl[i]])
                    VE.self_wait()
                    brc[i].rd(VE.tok())
                t_all = VE.tok()
                t_st = GP.emit("dma_start", out=y[128 * tt:128 * (tt + 1), :],
                               in_=out_sb[:], deps=[t_all])
                bout.wr(t_st)
            GP.self_wait()
        barrier()
    return nc


def _pad_heads(w, hg):
    out = np.zeros((w.shape[0], 128), np.float32)
    for j in range(HPG):
        out[:, 32 * j:32 * j + 8] = w[:, 8 * (HPG * hg + j):8 * (HPG * hg + j) + 8]
    return out


def kernel(L0_attn_hidden_states, L0_attn_Wq, L0_attn_Wk, L0_attn_Wv, L0_attn_Wo,
           L0_attn_Wg, L0_attn_v_emb0, L0_attn_v_emb1, L0_ffn_gate_w, L0_ffn_wg,
           L0_ffn_wu, L0_ffn_wd, L0_ffn_sg, L0_ffn_su, L0_ffn_sd):
    f32 = np.float32
    bf = ml_dtypes.bfloat16
    x = np.asarray(L0_attn_hidden_states, f32)[0]
    Wq = np.asarray(L0_attn_Wq, f32)
    Wk = np.asarray(L0_attn_Wk, f32)
    Wv = np.asarray(L0_attn_Wv, f32)
    Wo = np.asarray(L0_attn_Wo, f32)
    Wg = np.asarray(L0_attn_Wg, f32)
    ve0 = np.asarray(L0_attn_v_emb0, f32)
    ve1 = np.asarray(L0_attn_v_emb1, f32)
    gate_w = np.asarray(L0_ffn_gate_w, f32)
    wg = np.asarray(L0_ffn_wg, f32)
    wu = np.asarray(L0_ffn_wu, f32)
    wd = np.asarray(L0_ffn_wd, f32)
    sg = np.asarray(L0_ffn_sg, f32)
    su = np.asarray(L0_ffn_su, f32)
    sd = np.asarray(L0_ffn_sd, f32)

    xT = np.ascontiguousarray(x.T)
    wopad = np.zeros((16, 8 * 128, 128), f32)
    for g in range(8):
        for j in range(HPG):
            rows = slice(128 * g + 32 * j, 128 * g + 32 * j + 8)
            src = Wo[8 * (HPG * g + j):8 * (HPG * g + j) + 8]
            for dm in range(16):
                wopad[dm, rows, :] = src[:, 128 * dm:128 * (dm + 1)]
    wgate = np.ascontiguousarray(Wg.reshape(D, 16, 128).transpose(1, 0, 2))
    wgm = np.ascontiguousarray(wg.reshape(NE, D, 6, 128).transpose(0, 2, 1, 3))
    wum = np.ascontiguousarray(wu.reshape(NE, D, 6, 128).transpose(0, 2, 1, 3))
    tokid_h = (np.arange(128)[:, None] + 128 * np.arange(8)[None, :]).astype(np.int32)
    eoff_h = (TCAP * np.arange(NE)[:, None]).astype(np.float32)
    sgm = np.ascontiguousarray(sg.reshape(D, 12, 128).transpose(1, 0, 2))
    sum_m = np.ascontiguousarray(su.reshape(D, 12, 128).transpose(1, 0, 2))
    sdm = np.ascontiguousarray(sd.reshape(FSH, 16, 128).transpose(1, 0, 2))
    ident = np.eye(128, dtype=f32)
    ones = np.ones((128, 128), f32)
    sel = np.zeros((128, 128), f32)
    bdm = np.zeros((128, HPG), f32)
    padmask = np.zeros((128, 1), f32)
    for j in range(HPG):
        sel[32 * j + 8, 32 * j:32 * j + 32] = 1.0
        bdm[32 * j:32 * j + 8, j] = 1.0
        padmask[32 * j:32 * j + 8, 0] = 1.0
    cbsel = np.zeros((NE, NE * 128), f32)
    for e in range(NE):
        cbsel[e, 128 * e:128 * (e + 1)] = 1.0
    masks = np.zeros((128, 4 * HPG * QW), f32)
    qr = np.arange(QW)
    kr = np.arange(128)[:, None]
    for p in range(4):
        m = (128 * p + kr <= qr[None, :]).astype(f32)
        for h in range(HPG):
            masks[:, p * HPG * QW + QW * h:p * HPG * QW + QW * (h + 1)] = m

    shared = {
        "wopad": wopad, "wgate": wgate, "gate_w": gate_w, "wgm": wgm, "wum": wum,
        "wdp": wd, "sgm": sgm, "sum_m": sum_m, "sdm": sdm, "ident": ident,
        "tokid": tokid_h, "eoff": eoff_h,
        "ones": ones, "sel": sel, "bd": bdm.astype(bf), "padmask": padmask,
        "cbsel": cbsel.astype(bf), "masks": masks.astype(bf),
    }
    in_maps = []
    for c in range(NCORES):
        ve0p = np.zeros((128, 1), f32)
        ve1p = np.zeros((128, 1), f32)
        for j in range(HPG):
            ve0p[32 * j:32 * j + 8, 0] = ve0[8 * (HPG * c + j):8 * (HPG * c + j) + 8]
            ve1p[32 * j:32 * j + 8, 0] = ve1[8 * (HPG * c + j):8 * (HPG * c + j) + 8]
        mselc = np.zeros((128, NCORES), f32)
        mselc[:, c] = 1.0
        m = dict(shared)
        m.update({
            "xT_own": np.ascontiguousarray(xT[:, TOK * c:TOK * (c + 1)]),
            "wqp": _pad_heads(Wq, c), "wkp": _pad_heads(Wk, c),
            "wv": np.ascontiguousarray(Wv[:, 32 * c:32 * (c + 1)]),
            "vemb0p": ve0p, "vemb1p": ve1p, "msel": mselc,
        })
        in_maps.append(m)

    nc = build_program()
    res = run_bass_kernel_spmd(nc, in_maps, core_ids=list(range(NCORES)))
    out = np.concatenate([res.results[c]["y"] for c in range(NCORES)], axis=0)
    return out.reshape(1, S, D).astype(np.float32)


# version 08
# speedup vs baseline: 1.1889x; 1.1642x over previous
import numpy as np
import ml_dtypes
import concourse.bass as bass
import concourse.mybir as mybir
from concourse.bass_utils import run_bass_kernel_spmd
from concourse.bass import IndirectOffsetOnAxis
from contextlib import ExitStack
import os

_PHASES = int(os.environ.get("KPHASES", "99"))
_DEBUG = int(os.environ.get("KDEBUG", "0"))

F32 = mybir.dt.float32
BF16 = mybir.dt.bfloat16
AF = mybir.ActivationFunctionType
ALU = mybir.AluOpType
AX = mybir.AxisListType

NCORES = 8
S = 8192
D = 2048
NKT = D // 128
TOK = S // NCORES
HPG = 4
NE, TOPK, FE, FSH = 32, 6, 768, 1536
EPS = 1e-5
QW = 512
NQC = 16
SKT = S // 128
TCAP = 256
A = TCAP // 128
NA = NE * A


class Q:

    def __init__(self, eng, name, sem, inc):
        self.eng, self.name, self.sem, self.inc = eng, name, sem, inc
        self.n = 0
        self.waited = {}

    def wait(self, token):
        if token is None:
            return
        name, cnt, sem = token
        if self.waited.get(name, -1) >= cnt:
            return
        self.eng.wait_ge(sem, cnt)
        self.waited[name] = cnt

    def self_wait(self):
        self.wait((self.name, self.n, self.sem))

    def emit(self, method, *args, deps=(), inc=None, **kw):
        for d in deps:
            self.wait(d)
        instr = getattr(self.eng, method)(*args, **kw)
        step = self.inc if inc is None else inc
        instr.then_inc(self.sem, step)
        self.n += step
        return (self.name, self.n, self.sem)

    def tok(self):
        return (self.name, self.n, self.sem)


class Buf:

    def __init__(self):
        self.w = None
        self.r = []

    def wdeps(self):
        return ([self.w] if self.w else []) + self.r

    def rdeps(self):
        return [self.w] if self.w else []

    def wr(self, tok):
        self.w = tok
        self.r = []

    def rd(self, tok):
        self.r.append(tok)


def build_program():
    nc = bass.Bass("TRN2", target_bir_lowering=False)
    dt = nc.dram_tensor
    xT_own = dt("xT_own", [128, NKT * TOK], F32, kind="ExternalInput").ap()
    wqp = dt("wqp", [128, NKT * 128], BF16, kind="ExternalInput").ap()
    wkp = dt("wkp", [128, NKT * 128], BF16, kind="ExternalInput").ap()
    wv = dt("wv", [128, NKT * 32], BF16, kind="ExternalInput").ap()
    vemb0p = dt("vemb0p", [128, 1], F32, kind="ExternalInput").ap()
    vemb1p = dt("vemb1p", [128, 1], F32, kind="ExternalInput").ap()
    msel = dt("msel", [128, NCORES], F32, kind="ExternalInput").ap()
    wopad = dt("wopad", [16, 128, 8 * 128], BF16, kind="ExternalInput").ap()
    wgate = dt("wgate", [16, 128, NKT * 128], BF16, kind="ExternalInput").ap()
    gate_w = dt("gate_w", [128, NKT * NE], F32, kind="ExternalInput").ap()
    wgm = dt("wgm", [NE, 6, 128, NKT * 128], BF16, kind="ExternalInput").ap()
    wum = dt("wum", [NE, 6, 128, NKT * 128], BF16, kind="ExternalInput").ap()
    wdp = dt("wdp", [NE, FE, D], BF16, kind="ExternalInput").ap()
    sgm = dt("sgm", [12, 128, NKT * 128], BF16, kind="ExternalInput").ap()
    sum_m = dt("sum_m", [12, 128, NKT * 128], BF16, kind="ExternalInput").ap()
    sdm = dt("sdm", [16, 128, 12 * 128], BF16, kind="ExternalInput").ap()
    ident = dt("ident", [128, 128], F32, kind="ExternalInput").ap()
    ones = dt("ones", [128, 128], F32, kind="ExternalInput").ap()
    sel = dt("sel", [128, 128], F32, kind="ExternalInput").ap()
    bd = dt("bd", [128, HPG], BF16, kind="ExternalInput").ap()
    padmask = dt("padmask", [128, 1], F32, kind="ExternalInput").ap()
    cbsel = dt("cbsel", [NE, NE * 128], BF16, kind="ExternalInput").ap()
    masks = dt("masks", [128, 4 * HPG * QW], BF16, kind="ExternalInput").ap()
    tokid = dt("tokid", [128, 8], mybir.dt.int32, kind="ExternalInput").ap()
    eoff = dt("eoff", [NE, 1], F32, kind="ExternalInput").ap()
    ln1_own_d = dt("ln1_own_d", [D, TOK], BF16, kind="Internal").ap()
    ln1_all = dt("ln1_all", [NCORES * D, TOK], BF16, kind="Internal",
                 addr_space="Shared").ap()
    attn_own_d = dt("attn_own_d", [128, S], BF16, kind="Internal").ap()
    attn_all = dt("attn_all", [NCORES * 128, S], BF16, kind="Internal",
                  addr_space="Shared").ap()
    ln2tok_d = dt("ln2tok_d", [TOK, D], BF16, kind="Internal").ap()
    comb_tok_d = dt("comb_tok_d", [TOK, NE], F32, kind="Internal").ap()
    idx_d = dt("idx_d", [NE, TCAP], mybir.dt.int32, kind="Internal").ap()
    stage_d = dt("stage_d", [2, TCAP, D], BF16, kind="Internal").ap()
    racc = [dt(f"racc{i}", [TOK, D], F32, kind="Internal").ap() for i in range(4)]
    y = dt("y", [TOK, D], F32, kind="ExternalOutput").ap()
    if _DEBUG:
        d_ln1 = dt("d_ln1", [128, NKT * TOK], F32, kind="ExternalOutput").ap()
        d_attn = dt("d_attn", [128, S], F32, kind="ExternalOutput").ap()
        d_xmid = dt("d_xmid", [128, NKT * TOK], F32, kind="ExternalOutput").ap()
        d_ln2 = dt("d_ln2", [128, NKT * TOK], F32, kind="ExternalOutput").ap()
        d_comb = dt("d_comb", [NE, TOK], F32, kind="ExternalOutput").ap()
        d_atn = dt("d_atn", [128, 8 * TOK], F32, kind="ExternalOutput").ap()
        d_idx = dt("d_idx", [NE, TCAP], mybir.dt.int32, kind="ExternalOutput").ap()
        d_xg = dt("d_xg", [128, NKT * TCAP], F32, kind="ExternalOutput").ap()

    with ExitStack() as ctx:
        EE = ctx.enter_context
        TE = Q(nc.tensor, "te", EE(nc.semaphore("s_te")), 1)
        SC = Q(nc.scalar, "sc", EE(nc.semaphore("s_sc")), 1)
        VE = Q(nc.vector, "ve", EE(nc.semaphore("s_ve")), 1)
        SY = Q(nc.sync, "sy", EE(nc.semaphore("s_sy")), 16)
        GP = Q(nc.gpsimd, "gp", EE(nc.semaphore("s_gp")), 16)
        sem_ag1 = EE(nc.semaphore("s_ag1"))
        sem_ag2 = EE(nc.semaphore("s_ag2"))
        r_btcap = nc.gpsimd.to_reg(NE * TCAP)
        r_btok = nc.gpsimd.to_reg(TOK)
        r_b2tok = nc.gpsimd.to_reg(2 * TOK)
        ALL = (TE, SC, VE, SY, GP)

        def barrier():
            toks = [q.tok() for q in ALL]
            for q in ALL:
                for t in toks:
                    q.wait(t)

        xT = EE(nc.sbuf_tensor("xT", [128, NKT * TOK], F32))
        id_sb = EE(nc.sbuf_tensor("id_sb", [128, 128], F32))
        on_sb = EE(nc.sbuf_tensor("on_sb", [128, 128], F32))
        sel_sb = EE(nc.sbuf_tensor("sel_sb", [128, 128], F32))
        bd_sb = EE(nc.sbuf_tensor("bd_sb", [128, HPG], BF16))
        pm_sb = EE(nc.sbuf_tensor("pm_sb", [128, 1], F32))
        ve0_sb = EE(nc.sbuf_tensor("ve0_sb", [128, 1], F32))
        ve1_sb = EE(nc.sbuf_tensor("ve1_sb", [128, 1], F32))
        vd_sb = EE(nc.sbuf_tensor("vd_sb", [128, 1], F32))
        ms_sb = EE(nc.sbuf_tensor("ms_sb", [128, NCORES], F32))

        t_x = SY.emit("dma_start", out=xT[:], in_=xT_own)
        SY.emit("dma_start", out=id_sb[:], in_=ident)
        SY.emit("dma_start", out=on_sb[:], in_=ones)
        SY.emit("dma_start", out=sel_sb[:], in_=sel)
        SY.emit("dma_start", out=bd_sb[:], in_=bd)
        SY.emit("dma_start", out=pm_sb[:], in_=padmask)
        SY.emit("dma_start", out=ve0_sb[:], in_=vemb0p)
        t_ve = SY.emit("dma_start", out=ve1_sb[:], in_=vemb1p)
        t_msel = SY.emit("dma_start", out=ms_sb[:], in_=msel)
        t_vd = VE.emit("tensor_sub", vd_sb[:], ve1_sb[:], ve0_sb[:], deps=[t_ve])
        t_const = SY.tok()

        def layernorm(dst_bf16, fp32_consumer=None, name=""):
            with ExitStack() as c2:
                E2 = c2.enter_context
                sq = E2(nc.sbuf_tensor(f"sq{name}", [128, 2 * TOK], F32))
                stat = E2(nc.sbuf_tensor(f"stat{name}", [1, 4 * TOK], F32))
                lnf = E2(nc.sbuf_tensor(f"lnf{name}", [128, 2 * TOK], F32))
                mb_sb = E2(nc.sbuf_tensor(f"mb{name}", [128, TOK], F32))
                rb_sb = E2(nc.sbuf_tensor(f"rb{name}", [128, TOK], F32))
                with ExitStack() as c3:
                    E3 = c3.enter_context
                    ps_sum = E3(nc.psum_tensor(f"pssum{name}", [1, TOK], F32))
                    ps_ssq = E3(nc.psum_tensor(f"psssq{name}", [1, TOK], F32))
                    t_sq = [None, None]
                    t_ssqmm = None
                    for kt in range(NKT):
                        xkt = xT[:, TOK * kt:TOK * (kt + 1)]
                        sqb = sq[:, TOK * (kt % 2):TOK * (kt % 2 + 1)]
                        t_sq[kt % 2] = SC.emit("activation", sqb, xkt, AF.Square,
                                               deps=[t_x] + ([t_ssqmm] if kt >= 2 else []))
                        for h in range(2):
                            TE.emit("matmul", ps_sum[:, 512 * h:512 * (h + 1)],
                                    on_sb[:, 0:1], xkt[:, 512 * h:512 * (h + 1)],
                                    start=(kt == 0), stop=(kt == NKT - 1),
                                    deps=[t_x, t_const])
                        for h in range(2):
                            t_ssqmm = TE.emit("matmul", ps_ssq[:, 512 * h:512 * (h + 1)],
                                              on_sb[:, 0:1], sqb[:, 512 * h:512 * (h + 1)],
                                              start=(kt == 0), stop=(kt == NKT - 1),
                                              deps=[t_sq[kt % 2]])
                    t_summ = TE.tok()
                    VE.emit("tensor_scalar_mul", stat[:, 0:TOK], ps_sum[:], 1.0 / D,
                            deps=[t_summ])
                    VE.self_wait()
                    VE.emit("tensor_tensor", stat[:, TOK:2 * TOK], stat[:, 0:TOK],
                            stat[:, 0:TOK], ALU.mult)
                    VE.emit("tensor_scalar_mul", stat[:, 2 * TOK:3 * TOK], ps_ssq[:],
                            1.0 / D, deps=[t_summ])
                    VE.self_wait()
                    VE.emit("tensor_sub", stat[:, 2 * TOK:3 * TOK],
                            stat[:, 2 * TOK:3 * TOK], stat[:, TOK:2 * TOK])
                    VE.self_wait()
                    t_var = VE.emit("tensor_scalar_add", stat[:, 2 * TOK:3 * TOK],
                                    stat[:, 2 * TOK:3 * TOK], EPS)
                    VE.self_wait()
                    t_ln = SC.emit("activation", stat[:, 3 * TOK:4 * TOK],
                                   stat[:, 2 * TOK:3 * TOK], AF.Ln, deps=[t_var])
                    t_half = VE.emit("tensor_scalar_mul", stat[:, 3 * TOK:4 * TOK],
                                     stat[:, 3 * TOK:4 * TOK], -0.5, deps=[t_ln])
                    VE.self_wait()
                    t_rs = SC.emit("activation", stat[:, 3 * TOK:4 * TOK],
                                   stat[:, 3 * TOK:4 * TOK], AF.Exp, deps=[t_half])
                with ExitStack() as c3:
                    E3 = c3.enter_context
                    ps_mb = E3(nc.psum_tensor(f"psmb{name}", [128, TOK], F32))
                    ps_rb = E3(nc.psum_tensor(f"psrb{name}", [128, TOK], F32))
                    TE.self_wait()
                    for h in range(2):
                        TE.emit("matmul", ps_mb[:, 512 * h:512 * (h + 1)], on_sb[0:1, :],
                                stat[0:1, 512 * h:512 * (h + 1)], start=True, stop=True,
                                deps=[t_rs])
                    for h in range(2):
                        t_bc = TE.emit("matmul", ps_rb[:, 512 * h:512 * (h + 1)],
                                       on_sb[0:1, :],
                                       stat[0:1, 3 * TOK + 512 * h:3 * TOK + 512 * (h + 1)],
                                       start=True, stop=True)
                    VE.emit("tensor_copy", mb_sb[:], ps_mb[:], deps=[t_bc])
                    t_rbc = VE.emit("tensor_copy", rb_sb[:], ps_rb[:])
                    VE.self_wait()
                prev = [None, None]
                for kt in range(NKT):
                    xkt = xT[:, TOK * kt:TOK * (kt + 1)]
                    lbuf = lnf[:, TOK * (kt % 2):TOK * (kt % 2 + 1)]
                    dep_reuse = list(prev[kt % 2] or ())
                    VE.emit("tensor_sub", lbuf, xkt, mb_sb[:],
                            deps=[t_rbc] + [d for d in dep_reuse if d])
                    VE.self_wait()
                    t_n = VE.emit("tensor_tensor", lbuf, lbuf, rb_sb[:], ALU.mult)
                    VE.self_wait()
                    t_cast = SC.emit("activation", dst_bf16[:, TOK * kt:TOK * (kt + 1)],
                                     lbuf, AF.Copy, deps=[t_n])
                    t_cons = fp32_consumer(kt, lbuf, t_n) if fp32_consumer else None
                    prev[kt % 2] = (t_cast, t_cons)
                barrier()

        with ExitStack() as cA:
            EA = cA.enter_context
            ln1o = EA(nc.sbuf_tensor("ln1o", [128, NKT * TOK], BF16))

            layernorm(ln1o, name="l1")
            t_st1 = GP.emit("dma_start",
                            out=ln1_own_d.rearrange("(k p) t -> p k t", p=128),
                            in_=ln1o.rearrange("p (k t) -> p k t", k=NKT))
            GP.wait(t_st1)
            nc.gpsimd.collective_compute("AllGather", ALU.bypass,
                                         [list(range(NCORES))], [ln1_own_d],
                                         [ln1_all]).then_inc(sem_ag1, 1)
            t_ag1 = ("ag1", 1, sem_ag1)
            if _DEBUG:
                GP.emit("dma_start", out=d_ln1, in_=ln1o[:], deps=[t_st1])

            with ExitStack() as cB:
                EB = cB.enter_context
                msk_sb = EB(nc.sbuf_tensor("msk_sb", [128, 4 * HPG * QW], BF16))
                pq_pad = EB(nc.sbuf_tensor("pq_pad", [128, S], BF16))
                pk_pad = EB(nc.sbuf_tensor("pk_pad", [128, S], BF16))
                pv_pad = EB(nc.sbuf_tensor("pv_pad", [128, SKT * HPG * 9], BF16))
                t_mskl = SY.emit("dma_start", out=msk_sb[:], in_=masks)

                if _PHASES <= 0:
                    return nc
                with ExitStack() as c2:
                    E2 = c2.enter_context
                    lns = E2(nc.sbuf_tensor("lns", [128, NKT * 512], BF16))
                    wq_sb = E2(nc.sbuf_tensor("wq_sb", [128, NKT * 128], BF16))
                    wk_sb = E2(nc.sbuf_tensor("wk_sb", [128, NKT * 128], BF16))
                    wv_sb = E2(nc.sbuf_tensor("wv_sb", [128, NKT * 32], BF16))
                    prq = E2(nc.sbuf_tensor("prq", [128, 512], BF16))
                    esk = E2(nc.sbuf_tensor("esk", [128, HPG], F32))
                    ps_q = [E2(nc.psum_tensor(f"ps_q{i}", [128, 512], F32)) for i in range(2)]
                    ps_k = [E2(nc.psum_tensor(f"ps_k{i}", [128, 512], F32)) for i in range(2)]
                    ps_v = [E2(nc.psum_tensor(f"ps_v{i}", [128, 32], F32)) for i in range(2)]
                    ps_sk = E2(nc.psum_tensor("ps_sk", [128, HPG], F32))
                    bq = [Buf(), Buf()]
                    bk = [Buf(), Buf()]
                    bv = [Buf(), Buf()]
                    bsk, blns, bprq, besk = Buf(), Buf(), Buf(), Buf()

                    t_wq = GP.emit("dma_start", out=wq_sb[:], in_=wqp)
                    t_wk = GP.emit("dma_start", out=wk_sb[:], in_=wkp)
                    t_wv = GP.emit("dma_start", out=wv_sb[:], in_=wv)
                    t_pv1 = VE.emit(
                        "memset",
                        pv_pad.rearrange("p (k h n) -> p k h n", k=SKT, h=HPG)[:, :, :, 8:9],
                        1.0)
                    ln1v = ln1_all.rearrange("(r k p) t -> r k p t", r=NCORES, k=NKT)
                    for ch in range(16):
                        r, s0 = ch // 2, 512 * (ch % 2)
                        t_ld = SY.emit("dma_start",
                                       out=lns.rearrange("p (k t) -> p k t", k=NKT),
                                       in_=ln1v[r, :, :, s0:s0 + 512].rearrange(
                                           "k p t -> p k t"),
                                       deps=[t_ag1] + blns.wdeps())
                        blns.wr(t_ld)
                        pq_, bq_ = ps_q[ch % 2], bq[ch % 2]
                        for kt in range(NKT):
                            t_mm = TE.emit("matmul", pq_[:],
                                           wq_sb[:, 128 * kt:128 * (kt + 1)],
                                           lns[:, 512 * kt:512 * (kt + 1)],
                                           start=(kt == 0), stop=(kt == NKT - 1),
                                           deps=[t_ld, t_wq] + (bq_.wdeps() if kt == 0 else []))
                        bq_.wr(t_mm)
                        blns.rd(t_mm)
                        t_sg = SC.emit("activation", prq[:], pq_[:], AF.Sigmoid,
                                       deps=[t_mm] + bprq.wdeps())
                        bq_.rd(t_sg)
                        bprq.wr(t_sg)
                        t_pq = VE.emit("tensor_scalar_mul",
                                       pq_pad[:, 512 * ch:512 * (ch + 1)], prq[:],
                                       pm_sb[:, 0:1], deps=[t_sg, t_const])
                        bprq.rd(t_pq)
                        pk_, bk_ = ps_k[ch % 2], bk[ch % 2]
                        for kt in range(NKT):
                            t_mm = TE.emit("matmul", pk_[:],
                                           wk_sb[:, 128 * kt:128 * (kt + 1)],
                                           lns[:, 512 * kt:512 * (kt + 1)],
                                           start=(kt == 0), stop=(kt == NKT - 1),
                                           deps=[t_wk] + (bk_.wdeps() if kt == 0 else []))
                        bk_.wr(t_mm)
                        blns.rd(t_mm)
                        t_pk = SC.emit("activation", pk_pad[:, 512 * ch:512 * (ch + 1)],
                                       pk_[:], AF.Sigmoid, deps=[t_mm])
                        bk_.rd(t_pk)
                        for tt in range(4):
                            kk = 4 * ch + tt
                            pv_, bv_ = ps_v[tt % 2], bv[tt % 2]
                            for kt in range(NKT):
                                t_mm = TE.emit(
                                    "matmul", pv_[:],
                                    lns[:, 512 * kt + 128 * tt:512 * kt + 128 * (tt + 1)],
                                    wv_sb[:, 32 * kt:32 * (kt + 1)],
                                    start=(kt == 0), stop=(kt == NKT - 1),
                                    deps=[t_wv] + (bv_.wdeps() if kt == 0 else []))
                            bv_.wr(t_mm)
                            blns.rd(t_mm)
                            pvv = pv_pad.rearrange("p (k h n) -> p k h n", k=SKT, h=HPG)
                            t_ev = SC.emit("activation", pvv[:, kk, :, 0:8],
                                           pv_.ap().rearrange("p (h n) -> p h n", h=HPG),
                                           AF.Sigmoid, deps=[t_mm, t_pv1])
                            bv_.rd(t_ev)
                            TE.self_wait()
                            t_skm = TE.emit("matmul", ps_sk[:],
                                            pk_pad[:, 128 * kk:128 * (kk + 1)], bd_sb[:],
                                            start=True, stop=True,
                                            deps=[t_pk] + bsk.wdeps())
                            bsk.wr(t_skm)
                            t_esk = SC.emit("activation", esk[:], ps_sk[:], AF.Exp,
                                            scale=-1.0, deps=[t_skm] + besk.wdeps())
                            bsk.rd(t_esk)
                            besk.wr(t_esk)
                            for h in range(HPG):
                                pvh = pvv[:, kk, h, 0:9]
                                t_sc9 = VE.emit("tensor_scalar_mul", pvh, pvh,
                                                esk[:, h:h + 1], deps=[t_esk, t_ev])
                                besk.rd(t_sc9)
                            VE.self_wait()
                barrier()

                if _PHASES <= 1:
                    return nc
                with ExitStack() as c2:
                    E2 = c2.enter_context
                    ps_dot = E2(nc.psum_tensor("ps_dot", [128, HPG * QW], F32))
                    ps_nd = E2(nc.psum_tensor("ps_nd", [128, QW], F32))
                    E_sb = E2(nc.sbuf_tensor("E_sb", [128, 2 * HPG * QW], BF16))
                    num_sb = E2(nc.sbuf_tensor("num_sb", [128, QW], F32))
                    rec_sb = E2(nc.sbuf_tensor("rec_sb", [128, QW], F32))
                    att_sb = E2(nc.sbuf_tensor("att_sb", [128, QW], F32))
                    atb_sb = E2(nc.sbuf_tensor("atb_sb", [128, QW], BF16))
                    bdh = [Buf(), Buf()]
                    bE = [Buf(), Buf()]
                    bnd, bnsb, brec, batt, batb = (Buf() for _ in range(5))

                    t_z = VE.emit("memset", ps_nd[:], 0.0)
                    bnd.wr(t_z)
                    t_stc = None
                    git = 0
                    for j in range(NQC):
                        q0 = QW * j
                        Kj = 4 * (j + 1)
                        pend = None

                        def emit_av(pkk, pp, pw, last):
                            TE.self_wait()
                            for h in range(HPG):
                                t_av = TE.emit(
                                    "matmul", ps_nd[32 * h:32 * h + 9, :],
                                    pv_pad[:, (HPG * pkk + h) * 9:(HPG * pkk + h) * 9 + 9],
                                    E_sb[:, pp * HPG * QW + QW * h:pp * HPG * QW + QW * h + QW],
                                    start=(pkk == 0), stop=last,
                                    tile_position=(0, 32 * h), skip_group_check=True,
                                    deps=([pw] + (bnd.wdeps() if pkk == 0 else []))
                                    if h == 0 else [])
                            bnd.wr(t_av)
                            bE[pp].rd(t_av)
                            TE.self_wait()
                            return t_av

                        for kk in range(Kj):
                            par = git % 2
                            for half in range(2):
                                for h in (2 * half, 2 * half + 1):
                                    t_dot = TE.emit(
                                        "matmul", ps_dot[:, QW * h:QW * (h + 1)],
                                        pk_pad[32 * h:32 * h + 32, 128 * kk:128 * (kk + 1)],
                                        pq_pad[32 * h:32 * h + 32, q0:q0 + QW],
                                        start=True, stop=True, tile_position=(32 * h, 0),
                                        deps=(bdh[half].wdeps() if h == 2 * half else []))
                                bdh[half].wr(t_dot)
                            t_exp = None
                            for half in range(2):
                                Ebh = E_sb[:, par * HPG * QW + 2 * QW * half:
                                           par * HPG * QW + 2 * QW * (half + 1)]
                                t_exp = SC.emit(
                                    "activation", Ebh,
                                    ps_dot[:, 2 * QW * half:2 * QW * (half + 1)],
                                    AF.Exp, scale=2.0,
                                    deps=[bdh[half].w] + (bE[par].wdeps() if half == 0 else []))
                                bdh[half].rd(t_exp)
                                if half == 0:
                                    bE[par].wr(t_exp)
                            bE[par].wr(t_exp)
                            dpat = kk - 4 * j
                            if 0 <= dpat <= 3:
                                moff = dpat * HPG * QW
                                t_m = VE.emit("tensor_tensor",
                                              E_sb[:, par * HPG * QW:(par + 1) * HPG * QW],
                                              E_sb[:, par * HPG * QW:(par + 1) * HPG * QW],
                                              msk_sb[:, moff:moff + HPG * QW], ALU.mult,
                                              deps=[t_exp, t_mskl])
                                bE[par].wr(t_m)
                            if pend is not None:
                                emit_av(pend[0], pend[1], pend[2], False)
                            pend = (kk, par, bE[par].w)
                            git += 1
                        t_av = emit_av(pend[0], pend[1], pend[2], True)
                        t_evn = VE.emit("tensor_copy", num_sb[:], ps_nd[:],
                                        deps=[t_av] + bnsb.wdeps())
                        bnd.rd(t_evn)
                        bnsb.wr(t_evn)
                        VE.self_wait()
                        TE.self_wait()
                        t_den = TE.emit("matmul", ps_nd[:], sel_sb[:], num_sb[:],
                                        start=True, stop=True,
                                        deps=[t_evn] + bnd.wdeps())
                        bnsb.rd(t_den)
                        bnd.wr(t_den)
                        TE.self_wait()
                        t_rc = VE.emit("reciprocal", rec_sb[:], ps_nd[:],
                                       deps=[t_den] + brec.wdeps())
                        bnd.rd(t_rc)
                        brec.wr(t_rc)
                        VE.self_wait()
                        t_at = VE.emit("tensor_tensor", att_sb[:], num_sb[:],
                                       rec_sb[:], ALU.mult, deps=batt.wdeps())
                        bnsb.rd(t_at)
                        brec.rd(t_at)
                        batt.wr(t_at)
                        VE.self_wait()
                        t_ab = VE.emit("tensor_scalar", atb_sb[:], att_sb[:],
                                       vd_sb[:, 0:1], ve0_sb[:, 0:1], ALU.mult, ALU.add,
                                       deps=[t_vd] + batb.wdeps())
                        batt.rd(t_ab)
                        batb.wr(t_ab)
                        VE.self_wait()
                        t_stc = GP.emit("dma_start", out=attn_own_d[:, q0:q0 + QW],
                                        in_=atb_sb[:], deps=[t_ab])
                        batb.rd(t_stc)
                    GP.wait(t_stc)
                    nc.gpsimd.collective_compute("AllGather", ALU.bypass,
                                                 [list(range(NCORES))], [attn_own_d],
                                                 [attn_all]).then_inc(sem_ag2, 1)
                    t_ag2 = ("ag2", 1, sem_ag2)
                    if _DEBUG:
                        GP.emit("dma_start", out=d_attn, in_=attn_own_d, deps=[t_stc])
                barrier()
            barrier()

            if _PHASES <= 2:
                return nc
            with ExitStack() as c2:
                E2 = c2.enter_context
                atf = E2(nc.sbuf_tensor("atf", [128, S], BF16))
                atn = E2(nc.sbuf_tensor("atn", [128, 8 * TOK], BF16))
                wo_sb = E2(nc.sbuf_tensor("wo_sb", [128, 2 * 8 * 128], BF16))
                wgt_sb = E2(nc.sbuf_tensor("wgt_sb", [128, NKT * 128], BF16))
                gat_sb = E2(nc.sbuf_tensor("gat_sb", [128, TOK], BF16))
                tmp_sb = E2(nc.sbuf_tensor("tmp_sb", [128, TOK], F32))
                ps_wo = [E2(nc.psum_tensor(f"ps_wo{i}", [128, TOK], F32)) for i in range(2)]
                ps_gt = [E2(nc.psum_tensor(f"ps_gt{i}", [128, TOK], F32)) for i in range(2)]
                bwo = [Buf(), Buf()]
                bgt = [Buf(), Buf()]
                bwos = [Buf(), Buf()]
                bwgs, bgat, batf = Buf(), Buf(), Buf()
                batn = Buf()
                av = attn_all.rearrange("(g p) t -> g p t", p=128)
                for g in range(8):
                    t_af = SY.emit("dma_start", out=atf[:], in_=av[g],
                                   deps=[t_ag2] + batf.wdeps())
                    batf.wr(t_af)
                    dst = atn[:, TOK * g:TOK * (g + 1)]
                    t_a0 = VE.emit("tensor_scalar_mul", dst, atf[:, 0:TOK],
                                   ms_sb[:, 0:1], deps=[t_af, t_msel] + batn.wdeps())
                    for b in range(1, NCORES):
                        VE.self_wait()
                        t_a0 = VE.emit("scalar_tensor_tensor", dst,
                                       atf[:, TOK * b:TOK * (b + 1)], ms_sb[:, b:b + 1],
                                       dst, ALU.mult, ALU.add)
                    VE.self_wait()
                    batn.wr(t_a0)
                    batf.rd(t_a0)
                t_atn = VE.tok()
                for dm in range(NKT):
                    p = dm % 2
                    t_wol = GP.emit("dma_start",
                                    out=wo_sb[:, p * 1024:(p + 1) * 1024],
                                    in_=wopad[dm],
                                    deps=bwos[p].wdeps())
                    bwos[p].wr(t_wol)
                    t_wgl = GP.emit("dma_start",
                                    out=wgt_sb[:],
                                    in_=wgate[dm],
                                    deps=bwgs.wdeps())
                    bwgs.wr(t_wgl)
                    for h in range(2):
                        for g in range(8):
                            t_womm = TE.emit(
                                "matmul", ps_wo[p][:, 512 * h:512 * (h + 1)],
                                wo_sb[:, p * 1024 + 128 * g:p * 1024 + 128 * (g + 1)],
                                atn[:, TOK * g + 512 * h:TOK * g + 512 * (h + 1)],
                                start=(g == 0), stop=(g == 7),
                                deps=[t_wol, t_atn] +
                                     (bwo[p].wdeps() if g == 0 and h == 0 else []))
                    bwo[p].wr(t_womm)
                    bwos[p].rd(t_womm)
                    for h in range(2):
                        for kt in range(NKT):
                            t_gtm = TE.emit(
                                "matmul", ps_gt[p][:, 512 * h:512 * (h + 1)],
                                wgt_sb[:, 128 * kt:128 * (kt + 1)],
                                ln1o[:, TOK * kt + 512 * h:TOK * kt + 512 * (h + 1)],
                                start=(kt == 0), stop=(kt == NKT - 1),
                                deps=[t_wgl] +
                                     (bgt[p].wdeps() if kt == 0 and h == 0 else []))
                    bgt[p].wr(t_gtm)
                    bwgs.rd(t_gtm)
                    t_sg = SC.emit("activation", gat_sb[:], ps_gt[p][:], AF.Sigmoid,
                                   deps=[t_gtm] + bgat.wdeps())
                    bgt[p].rd(t_sg)
                    bgat.wr(t_sg)
                    t_t = VE.emit("tensor_tensor", tmp_sb[:], ps_wo[p][:], gat_sb[:],
                                  ALU.mult, deps=[t_sg, t_womm])
                    bwo[p].rd(t_t)
                    bgat.rd(t_t)
                    VE.self_wait()
                    VE.emit("tensor_add", xT[:, TOK * dm:TOK * (dm + 1)],
                            xT[:, TOK * dm:TOK * (dm + 1)], tmp_sb[:])
                    VE.self_wait()
                if _DEBUG:
                    GP.emit("dma_start", out=d_xmid, in_=xT[:], deps=[VE.tok()])
                    GP.emit("dma_start", out=d_atn, in_=atn[:], deps=[t_atn])
            barrier()
        barrier()

        if _PHASES <= 3:
            return nc
        with ExitStack() as cC:
            EC = cC.enter_context
            idx_sb = EC(nc.sbuf_tensor("idx_sb", [128, NE * TCAP // 128],
                                       mybir.dt.int32))
            idx2_sb = EC(nc.sbuf_tensor("idx2_sb", [128, 2 * NE * TCAP // 128],
                                        mybir.dt.int32))
            cD = EC(ExitStack())
            ln2o = cD.enter_context(nc.sbuf_tensor("ln2o", [128, NKT * TOK], BF16))
            combT = cD.enter_context(nc.sbuf_tensor("combT", [NE, TOK], BF16))

            with ExitStack() as c2:
                E2 = c2.enter_context
                gw_sb = E2(nc.sbuf_tensor("gw_sb", [128, NKT * NE], F32))
                ps_sc = E2(nc.psum_tensor("ps_sc", [NE, TOK], F32))
                sc_sb = E2(nc.sbuf_tensor("sc_sb", [NE, TOK], F32))
                t_gwl = SY.emit("dma_start", out=gw_sb[:], in_=gate_w)
                t_scm = [None]

                def consume(kt, lbuf, t_n):
                    for h in range(2):
                        t_scm[0] = TE.emit("matmul", ps_sc[:, 512 * h:512 * (h + 1)],
                                           gw_sb[:, NE * kt:NE * (kt + 1)],
                                           lbuf[:, 512 * h:512 * (h + 1)],
                                           start=(kt == 0), stop=(kt == NKT - 1),
                                           deps=[t_n, t_gwl])
                    return t_scm[0]

                layernorm(ln2o, fp32_consumer=consume, name="l2")
                t_sig = SC.emit("activation", sc_sb[:], ps_sc[:], AF.Sigmoid,
                                deps=[t_scm[0]])
                sct = E2(nc.sbuf_tensor("sct", [128, NE], F32))
                work = E2(nc.sbuf_tensor("work", [128, NE], F32))
                tmq = E2(nc.sbuf_tensor("tmq", [128, NE], F32))
                msum = E2(nc.sbuf_tensor("msum", [128, 8], F32))
                comb = E2(nc.sbuf_tensor("comb", [128, NE], F32))
                ps_t1 = E2(nc.psum_tensor("ps_t1", [128, NE], F32))
                ps_t2 = E2(nc.psum_tensor("ps_t2", [NE, 128], F32))
                bt1, bt2 = Buf(), Buf()
                t_cstore = None
                for tt in range(8):
                    TE.self_wait()
                    t_tr = TE.emit("transpose", ps_t1[:],
                                   sc_sb[:, 128 * tt:128 * (tt + 1)],
                                   id_sb[0:NE, 0:NE], deps=[t_sig] + bt1.wdeps())
                    bt1.wr(t_tr)
                    t_sct = VE.emit("tensor_copy", sct[:], ps_t1[:], deps=[t_tr])
                    bt1.rd(t_sct)
                    VE.self_wait()
                    VE.emit("tensor_copy", work[:], sct[:])
                    VE.emit("memset", msum[:, 0:1], 0.0)
                    VE.self_wait()
                    for it in range(TOPK):
                        VE.emit("reduce_max", msum[:, 1:2], work[:], AX.X)
                        VE.self_wait()
                        VE.emit("tensor_add", msum[:, 0:1], msum[:, 0:1], msum[:, 1:2])
                        if it < TOPK - 1:
                            VE.emit("tensor_scalar", tmq[:], work[:], msum[:, 1:2],
                                    None, ALU.is_equal)
                            VE.self_wait()
                            VE.emit("scalar_tensor_tensor", work[:], tmq[:], -100.0,
                                    work[:], ALU.mult, ALU.add)
                            VE.self_wait()
                    VE.self_wait()
                    VE.emit("tensor_scalar", tmq[:], sct[:], msum[:, 1:2], None,
                            ALU.is_ge)
                    VE.emit("tensor_scalar_add", msum[:, 2:3], msum[:, 0:1], 1e-20)
                    VE.self_wait()
                    VE.emit("reciprocal", msum[:, 3:4], msum[:, 2:3])
                    VE.self_wait()
                    VE.emit("tensor_scalar", comb[:], sct[:], msum[:, 3:4], None,
                            ALU.mult, deps=([t_cstore] if tt > 0 else []))
                    VE.self_wait()
                    t_cb = VE.emit("tensor_tensor", comb[:], comb[:], tmq[:], ALU.mult)
                    VE.self_wait()
                    t_cstore = GP.emit("dma_start",
                                       out=comb_tok_d[128 * tt:128 * (tt + 1), :],
                                       in_=comb[:], deps=[t_cb])
                    bcombrow.rd(t_cstore) if False else None
                    TE.self_wait()
                    t_tr2 = TE.emit("transpose", ps_t2[:], comb[:], id_sb[:],
                                    deps=[t_cb] + bt2.wdeps())
                    bt2.wr(t_tr2)
                    t_cT = VE.emit("tensor_copy", combT[0:NE, 128 * tt:128 * (tt + 1)],
                                   ps_t2[:], deps=[t_tr2])
                    bt2.rd(t_cT)
                    VE.self_wait()
                if _DEBUG:
                    GP.emit("dma_start", out=d_ln2, in_=ln2o[:], deps=[VE.tok()])
                    GP.emit("dma_start", out=d_comb, in_=combT[:], deps=[VE.tok()])
                zero_sb = E2(nc.sbuf_tensor("zero_sb", [128, D], F32))
                t_zz0 = VE.emit("memset", zero_sb[:], 0.0)
                VE.self_wait()
                t_rz = None
                for i in range(4):
                    for tt in range(8):
                        t_rz = GP.emit("dma_start",
                                       out=racc[i][128 * tt:128 * (tt + 1), :],
                                       in_=zero_sb[:], deps=[t_zz0])
                ps_tb = E2(nc.psum_tensor("ps_tb", [128, 128], BF16))
                lntt = E2(nc.sbuf_tensor("lntt", [128, D], BF16))
                idb2 = E2(nc.sbuf_tensor("idb2", [128, 128], BF16))
                t_idb = VE.emit("tensor_copy", idb2[:], id_sb[:])
                VE.self_wait()
                btb, blntt = Buf(), Buf()
                for tt in range(8):
                    for kt in range(NKT):
                        TE.self_wait()
                        t_tr = TE.emit("transpose", ps_tb[:],
                                       ln2o[:, TOK * kt + 128 * tt:TOK * kt + 128 * (tt + 1)],
                                       idb2[:], deps=[t_idb] + btb.wdeps())
                        btb.wr(t_tr)
                        t_ev = VE.emit("tensor_copy", lntt[:, 128 * kt:128 * (kt + 1)],
                                       ps_tb[:], deps=[t_tr] + (blntt.wdeps() if kt == 0 else []))
                        btb.rd(t_ev)
                        VE.self_wait()
                    t_lnst = GP.emit("dma_start", out=ln2tok_d[128 * tt:128 * (tt + 1), :],
                                     in_=lntt[:], deps=[VE.tok()])
                    blntt.wr(t_lnst) if False else None
                    blntt.rd(t_lnst)
                t_ln2tok = t_lnst
                msk = E2(nc.sbuf_tensor("msk", [NE, TOK], F32))
                zsc = E2(nc.sbuf_tensor("zsc", [NE, TOK], F32))
                posx = E2(nc.sbuf_tensor("posx", [NE, TOK], F32))
                post_i = E2(nc.sbuf_tensor("post_i", [128, 8 * NE], mybir.dt.int32))
                tokid_sb = E2(nc.sbuf_tensor("tokid_sb", [128, 8], mybir.dt.int32))
                idxf = E2(nc.sbuf_tensor("idxf", [NA, 128], mybir.dt.int32))
                t_tid = SY.emit("dma_start", out=tokid_sb[:], in_=tokid)
                eoff_sb = E2(nc.sbuf_tensor("eoff_sb", [NE, 1], F32))
                t_eo = SY.emit("dma_start", out=eoff_sb[:], in_=eoff)
                VE.emit("tensor_scalar", msk[:], combT[:], 0.0, None, ALU.is_gt,
                        deps=[t_cT])
                VE.emit("memset", zsc[:], 0.0)
                VE.self_wait()
                VE.emit("tensor_tensor_scan", posx[:], msk[:], zsc[:], 0.0,
                        ALU.add, ALU.add)
                VE.self_wait()
                VE.emit("tensor_scalar_add", posx[:], posx[:], -1.0)
                VE.self_wait()
                VE.emit("tensor_scalar", zsc[:], msk[:], -1.0, 1.0, ALU.mult, ALU.add)
                VE.self_wait()
                VE.emit("scalar_tensor_tensor", posx[:], zsc[:], 1.0e6,
                        posx[:], ALU.mult, ALU.add)
                VE.self_wait()
                VE.emit("tensor_scalar", zsc[:], posx[:], float(TCAP) - 0.5, None,
                        ALU.is_ge)
                VE.self_wait()
                VE.emit("scalar_tensor_tensor", posx[:], zsc[:], 1.0e6,
                        posx[:], ALU.mult, ALU.add)
                VE.self_wait()
                t_posx = VE.emit("tensor_scalar_add", posx[:], posx[:],
                                 eoff_sb[:, 0:1], deps=[t_eo])
                VE.self_wait()
                t_if = VE.emit("memset", idxf[:], 100000)
                VE.self_wait()
                t_ifd = GP.emit("dma_start",
                                out=idx_d.rearrange("e (a p) -> (e a) p", p=128),
                                in_=idxf[:], deps=[t_if])
                t_scat = None
                for tt in range(8):
                    TE.self_wait()
                    t_trp = TE.emit("transpose", ps_t1[:],
                                    posx[:, 128 * tt:128 * (tt + 1)],
                                    id_sb[0:NE, 0:NE], deps=[t_posx] + bt1.wdeps())
                    bt1.wr(t_trp)
                    t_pi = VE.emit("tensor_copy", post_i[:, NE * tt:NE * (tt + 1)],
                                   ps_t1[:], deps=[t_trp])
                    bt1.rd(t_pi)
                    VE.self_wait()
                    for e in range(NE):
                        t_scat = GP.emit("indirect_dma_start",
                                         idx_d.rearrange("e (t o) -> (e t) o", o=1),
                                         IndirectOffsetOnAxis(
                                             post_i[:, NE * tt + e:NE * tt + e + 1], 0),
                                         tokid_sb[:, tt:tt + 1], None,
                                         bounds_check=r_btcap, oob_is_err=False,
                                         deps=[t_pi, t_tid, t_ifd])
                for e in range(NE):
                    for a in range(A):
                        t_idxload = GP.emit(
                            "dma_start", out=idx_sb[:, A * e + a:A * e + a + 1],
                            in_=idx_d[e:e + 1, 128 * a:128 * (a + 1)].rearrange(
                                "e p -> p e"),
                            deps=[t_scat])
                if _DEBUG:
                    GP.emit("dma_start", out=d_idx, in_=idx_d, deps=[t_scat])
                for dmh in range(2):
                    VE.emit("tensor_scalar", idx2_sb[:, NA * dmh:NA * (dmh + 1)],
                            idx_sb[:], 2, dmh, ALU.mult, ALU.add, deps=[t_idxload])
                VE.self_wait()
                t_idx2 = VE.tok()
            barrier()

            if _PHASES <= 4:
                return nc
            if _PHASES <= 5:
                return nc
            with ExitStack() as c2:
                E2 = c2.enter_context
                shact = E2(nc.sbuf_tensor("shact", [128, 12 * TOK], BF16))
                wsh_sb = E2(nc.sbuf_tensor("wsh_sb", [128, 4 * NKT * 128], BF16))
                wsd_sb = E2(nc.sbuf_tensor("wsd_sb", [128, 2 * 12 * 128], BF16))
                silu2 = E2(nc.sbuf_tensor("silu2", [128, TOK], BF16))
                ps_a = E2(nc.psum_tensor("ps_a2", [128, TOK], F32))
                ps_b = E2(nc.psum_tensor("ps_b2", [128, TOK], F32))
                ps_c = E2(nc.psum_tensor("ps_c2", [128, TOK], F32))
                ps_d = E2(nc.psum_tensor("ps_d2", [128, TOK], F32))
                bpa, bpb, bpc, bpd = Buf(), Buf(), Buf(), Buf()
                bwsh = [Buf(), Buf()]
                bwsd = [Buf(), Buf()]
                bsilu, bsh = Buf(), Buf()
                bxt = [Buf() for _ in range(NKT)]
                for m in range(12):
                    pbuf = m % 2
                    woff = pbuf * 2 * NKT * 128
                    t_lg = GP.emit("dma_start",
                                   out=wsh_sb[:, woff:woff + NKT * 128],
                                   in_=sgm[m],
                                   deps=bwsh[pbuf].wdeps())
                    t_lu = GP.emit("dma_start",
                                   out=wsh_sb[:, woff + NKT * 128:woff + 2 * NKT * 128],
                                   in_=sum_m[m])
                    bwsh[pbuf].wr(t_lu)
                    pg, bg = (ps_a, bpa) if m % 2 == 0 else (ps_b, bpb)
                    pu, bu = (ps_c, bpc) if m % 2 == 0 else (ps_d, bpd)
                    for h in range(2):
                        for kt in range(NKT):
                            t_g = TE.emit(
                                "matmul", pg[:, 512 * h:512 * (h + 1)],
                                wsh_sb[:, woff + 128 * kt:woff + 128 * (kt + 1)],
                                ln2o[:, TOK * kt + 512 * h:TOK * kt + 512 * (h + 1)],
                                start=(kt == 0), stop=(kt == NKT - 1),
                                deps=[t_lg] + (bg.wdeps() if kt == 0 and h == 0 else []))
                    bg.wr(t_g)
                    bwsh[pbuf].rd(t_g)
                    for h in range(2):
                        for kt in range(NKT):
                            t_u = TE.emit(
                                "matmul", pu[:, 512 * h:512 * (h + 1)],
                                wsh_sb[:, woff + NKT * 128 + 128 * kt:
                                       woff + NKT * 128 + 128 * (kt + 1)],
                                ln2o[:, TOK * kt + 512 * h:TOK * kt + 512 * (h + 1)],
                                start=(kt == 0), stop=(kt == NKT - 1),
                                deps=[t_lu] + (bu.wdeps() if kt == 0 and h == 0 else []))
                    bu.wr(t_u)
                    bwsh[pbuf].rd(t_u)
                    t_si = SC.emit("activation", silu2[:], pg[:], AF.Silu,
                                   deps=[t_g] + bsilu.wdeps())
                    bg.rd(t_si)
                    bsilu.wr(t_si)
                    a = shact[:, TOK * m:TOK * (m + 1)]
                    t_a1 = VE.emit("tensor_tensor", a, silu2[:], pu[:], ALU.mult,
                                   deps=[t_si, t_u])
                    bu.rd(t_a1)
                    bsilu.rd(t_a1)
                    bsh.wr(t_a1)
                    VE.self_wait()
                for dm in range(NKT):
                    pbuf = dm % 2
                    woff = pbuf * 12 * 128
                    t_lw = GP.emit("dma_start",
                                   out=wsd_sb[:, woff:woff + 12 * 128],
                                   in_=sdm[dm],
                                   deps=bwsd[pbuf].wdeps())
                    bwsd[pbuf].wr(t_lw)
                    pr, br = (ps_a, bpa) if dm % 2 == 0 else (ps_b, bpb)
                    for h in range(2):
                        for f in range(12):
                            t_r = TE.emit(
                                "matmul", pr[:, 512 * h:512 * (h + 1)],
                                wsd_sb[:, woff + 128 * f:woff + 128 * (f + 1)],
                                shact[:, TOK * f + 512 * h:TOK * f + 512 * (h + 1)],
                                start=(f == 0), stop=(f == 11),
                                deps=[t_lw, bsh.w] +
                                     (br.wdeps() if f == 0 and h == 0 else []))
                        br.wr(t_r)
                    bwsd[pbuf].rd(t_r)
                    bsh.rd(t_r)
                    t_add = VE.emit("tensor_add", xT[:, TOK * dm:TOK * (dm + 1)],
                                    xT[:, TOK * dm:TOK * (dm + 1)], pr[:],
                                    deps=[t_r] + bxt[dm].wdeps())
                    br.rd(t_add)
                    bxt[dm].wr(t_add)
                    VE.self_wait()
            barrier()

            cD.close()
            barrier()
            if _PHASES <= 6:
                return nc
            with ExitStack() as c2:
                E2 = c2.enter_context
                xg = [E2(nc.sbuf_tensor(f"xg{i}", [128, NKT * TCAP], BF16))
                      for i in range(2)]
                actT = [E2(nc.sbuf_tensor(f"actT{i}", [128, 6 * TCAP], BF16))
                        for i in range(2)]
                gst = E2(nc.sbuf_tensor("gst", [128, D], BF16))
                cbg = [E2(nc.sbuf_tensor(f"cbg{i}", [128, A * NE], F32))
                       for i in range(2)]
                wgu_sb = E2(nc.sbuf_tensor("wgu_sb", [128, 4 * NKT * 128], BF16))
                wdd_sb = E2(nc.sbuf_tensor("wdd_sb", [128, 6 * D], BF16))
                silu_sb = E2(nc.sbuf_tensor("silu_sb", [128, TCAP], BF16))
                rout = [E2(nc.sbuf_tensor(f"rout{i}", [128, 1024], F32))
                        for i in range(2)]
                ps_g = [E2(nc.psum_tensor(f"ps_g{i}", [128, TCAP], F32)) for i in range(2)]
                ps_u = [E2(nc.psum_tensor(f"ps_u{i}", [128, TCAP], F32)) for i in range(2)]
                ps_r = [E2(nc.psum_tensor(f"ps_r{i}", [128, 1024], F32)) for i in range(2)]
                bxg = [Buf(), Buf()]
                bact = [Buf(), Buf()]
                bgst = Buf()
                bstg = [Buf(), Buf()]
                bcbg = [Buf(), Buf()]
                bwgu = [Buf(), Buf()]
                bwdd = [Buf(), Buf()]
                bpg = [Buf(), Buf()]
                bpu = [Buf(), Buf()]
                bpr = [Buf(), Buf()]
                brt = [Buf(), Buf()]
                bracc = [Buf(), Buf(), Buf(), Buf()]
                bsilu = Buf()
                for i in range(4):
                    bracc[i].wr(t_rz)
                def prefetch(e):
                    buf = e % 2
                    t_stg = None
                    for a in range(A):
                        t_ga = GP.emit("indirect_dma_start", gst[:], None, ln2tok_d,
                                       IndirectOffsetOnAxis(
                                           idx_sb[:, A * e + a:A * e + a + 1], 0),
                                       bounds_check=r_btok, oob_is_err=False,
                                       deps=[t_idxload, t_ln2tok] + bgst.wdeps())
                        bgst.wr(t_ga)
                        t_stg = GP.emit("dma_start",
                                        out=stage_d[buf, 128 * a:128 * (a + 1), :],
                                        in_=gst[:], deps=[t_ga] +
                                        (bstg[buf].wdeps() if a == 0 else []))
                        bgst.rd(t_stg)
                    bstg[buf].wr(t_stg)
                    t_cg = None
                    for a in range(A):
                        t_cg = GP.emit("indirect_dma_start",
                                       cbg[buf][:, NE * a:NE * (a + 1)], None,
                                       comb_tok_d,
                                       IndirectOffsetOnAxis(
                                           idx_sb[:, A * e + a:A * e + a + 1], 0),
                                       bounds_check=r_btok, oob_is_err=False,
                                       deps=[t_cstore] + (bcbg[buf].wdeps() if a == 0 else []))
                    bcbg[buf].wr(t_cg)
                    t_xg = None
                    for kt in range(NKT):
                        t_xg = SY.emit("dma_start",
                                       out=xg[buf][:, TCAP * kt:TCAP * (kt + 1)],
                                       in_=stage_d[buf, :, 128 * kt:128 * (kt + 1)],
                                       transpose=True, inc=16,
                                       deps=[t_stg] + (bxg[buf].wdeps() if kt == 0 else []))
                    t_xg_sy = SY.tok()
                    bxg[buf].wr(t_xg_sy)
                    bstg[buf].rd(t_xg_sy)
                    return t_xg_sy, t_cg

                pf = {0: prefetch(0)}
                for e in range(NE):
                    buf = e % 2
                    if e + 1 < NE:
                        pf[e + 1] = prefetch(e + 1)
                    t_xg_sy, t_cg = pf.pop(e)
                    if _DEBUG and e == 0:
                        GP.emit("dma_start", out=d_xg, in_=xg[0][:],
                                deps=[t_xg_sy])
                    for m in range(6):
                        pb = m % 2
                        woff = pb * 2 * NKT * 128
                        t_lg = GP.emit("dma_start",
                                       out=wgu_sb[:, woff:woff + NKT * 128],
                                       in_=wgm[e, m],
                                       deps=bwgu[pb].wdeps())
                        t_lu = GP.emit("dma_start",
                                       out=wgu_sb[:, woff + NKT * 128:woff + 2 * NKT * 128],
                                       in_=wum[e, m])
                        bwgu[pb].wr(t_lu)
                        for kt in range(NKT):
                            t_g = TE.emit(
                                "matmul", ps_g[pb][:],
                                wgu_sb[:, woff + 128 * kt:woff + 128 * (kt + 1)],
                                xg[buf][:, TCAP * kt:TCAP * (kt + 1)],
                                start=(kt == 0), stop=(kt == NKT - 1),
                                deps=[t_lg, t_xg_sy] +
                                     (bpg[pb].wdeps() if kt == 0 else []))
                        bpg[pb].wr(t_g)
                        bwgu[pb].rd(t_g)
                        bxg[buf].rd(t_g)
                        for kt in range(NKT):
                            t_u = TE.emit(
                                "matmul", ps_u[pb][:],
                                wgu_sb[:, woff + NKT * 128 + 128 * kt:
                                       woff + NKT * 128 + 128 * (kt + 1)],
                                xg[buf][:, TCAP * kt:TCAP * (kt + 1)],
                                start=(kt == 0), stop=(kt == NKT - 1),
                                deps=[t_lu] + (bpu[pb].wdeps() if kt == 0 else []))
                        bpu[pb].wr(t_u)
                        bwgu[pb].rd(t_u)
                        bxg[buf].rd(t_u)
                        t_si = SC.emit("activation", silu_sb[:], ps_g[pb][:], AF.Silu,
                                       deps=[t_g] + bsilu.wdeps())
                        bpg[pb].rd(t_si)
                        bsilu.wr(t_si)
                        a_ = actT[buf][:, TCAP * m:TCAP * (m + 1)]
                        t_a1 = VE.emit("tensor_tensor", a_, silu_sb[:], ps_u[pb][:],
                                       ALU.mult,
                                       deps=[t_si, t_u] + (bact[buf].wdeps() if m == 0 else []))
                        bpu[pb].rd(t_a1)
                        bsilu.rd(t_a1)
                        VE.self_wait()
                    bact[buf].wr(VE.tok())
                    t_lw = None
                    for m in range(6):
                        t_lw = GP.emit("dma_start",
                                       out=wdd_sb[:, D * m:D * (m + 1)],
                                       in_=wdp[e, 128 * m:128 * (m + 1), :],
                                       deps=(bwdd[0].wdeps() if m == 0 else []))
                    bwdd[0].wr(t_lw)
                    for a in range(A):
                        for dmh in range(2):
                            pb = (a * 2 + dmh) % 2
                            pr = ps_r[pb]
                            first = True
                            for m in range(6):
                                for ns in range(2):
                                    t_r = TE.emit(
                                        "matmul", pr[:, 512 * ns:512 * (ns + 1)],
                                        actT[buf][:, TCAP * m + 128 * a:
                                                  TCAP * m + 128 * (a + 1)],
                                        wdd_sb[:, D * m + 1024 * dmh + 512 * ns:
                                               D * m + 1024 * dmh + 512 * (ns + 1)],
                                        start=(m == 0), stop=(m == 5),
                                        skip_group_check=True,
                                        deps=[t_lw, bact[buf].w] +
                                             (bpr[pb].wdeps() if first else []))
                                    first = False
                            bpr[pb].wr(t_r)
                            bwdd[0].rd(t_r)
                            bact[buf].rd(t_r)
                            rb = (a * 2 + dmh) % 2
                            t_ev = VE.emit("tensor_scalar", rout[rb][:], pr[:],
                                           cbg[buf][:, NE * a + e:NE * a + e + 1],
                                           None, ALU.mult,
                                           deps=[t_r, t_cg] + brt[rb].wdeps())
                            bpr[pb].rd(t_ev)
                            bcbg[buf].rd(t_ev)
                            brt[rb].wr(t_ev)
                            VE.self_wait()
                            acc = e % 4
                            t_sc2 = GP.emit(
                                "indirect_dma_start",
                                racc[acc].rearrange("t (h c) -> (t h) c", h=2),
                                IndirectOffsetOnAxis(
                                    idx2_sb[:, NA * dmh + A * e + a:
                                            NA * dmh + A * e + a + 1], 0),
                                rout[rb][:], None,
                                bounds_check=r_b2tok, oob_is_err=False,
                                compute_op=ALU.add,
                                deps=[t_ev, t_idx2] + bracc[acc].wdeps())
                            bracc[acc].wr(t_sc2)
                            brt[rb].rd(t_sc2)
            barrier()
        barrier()

        if _PHASES <= 6:
            return nc
        with ExitStack() as c2:
            E2 = c2.enter_context
            out_sb = E2(nc.sbuf_tensor("out_sb", [128, D], F32))
            rc_sb = [E2(nc.sbuf_tensor(f"rc_sb{i}", [128, D], F32)) for i in range(4)]
            ps_t = [E2(nc.psum_tensor(f"ps_to{i}", [128, 128], F32)) for i in range(2)]
            bt = [Buf(), Buf()]
            bout = Buf()
            brc = [Buf() for _ in range(4)]
            for tt in range(8):
                t_rl = []
                for i in range(4):
                    t_rl.append(SY.emit("dma_start", out=rc_sb[i][:],
                                        in_=racc[i][128 * tt:128 * (tt + 1), :],
                                        deps=brc[i].wdeps()))
                for kt in range(NKT):
                    p = kt % 2
                    TE.self_wait()
                    t_tr = TE.emit("transpose", ps_t[p][:],
                                   xT[:, TOK * kt + 128 * tt:TOK * kt + 128 * (tt + 1)],
                                   id_sb[:], deps=bt[p].wdeps())
                    bt[p].wr(t_tr)
                    t_cp = VE.emit("tensor_copy", out_sb[:, 128 * kt:128 * (kt + 1)],
                                   ps_t[p][:],
                                   deps=[t_tr] + (bout.wdeps() if kt == 0 else []))
                    bt[p].rd(t_cp)
                    VE.self_wait()
                for i in range(4):
                    VE.emit("tensor_add", out_sb[:], out_sb[:], rc_sb[i][:],
                            deps=[t_rl[i]])
                    VE.self_wait()
                    brc[i].rd(VE.tok())
                t_all = VE.tok()
                t_st = GP.emit("dma_start", out=y[128 * tt:128 * (tt + 1), :],
                               in_=out_sb[:], deps=[t_all])
                bout.wr(t_st)
            GP.self_wait()
        barrier()
    return nc


def _pad_heads(w, hg):
    out = np.zeros((w.shape[0], 128), np.float32)
    for j in range(HPG):
        out[:, 32 * j:32 * j + 8] = w[:, 8 * (HPG * hg + j):8 * (HPG * hg + j) + 8]
    return out


def kernel(L0_attn_hidden_states, L0_attn_Wq, L0_attn_Wk, L0_attn_Wv, L0_attn_Wo,
           L0_attn_Wg, L0_attn_v_emb0, L0_attn_v_emb1, L0_ffn_gate_w, L0_ffn_wg,
           L0_ffn_wu, L0_ffn_wd, L0_ffn_sg, L0_ffn_su, L0_ffn_sd):
    f32 = np.float32
    bf = ml_dtypes.bfloat16
    x = np.asarray(L0_attn_hidden_states, f32)[0]
    Wq = np.asarray(L0_attn_Wq, f32)
    Wk = np.asarray(L0_attn_Wk, f32)
    Wv = np.asarray(L0_attn_Wv, f32)
    Wo = np.asarray(L0_attn_Wo, f32)
    Wg = np.asarray(L0_attn_Wg, f32)
    ve0 = np.asarray(L0_attn_v_emb0, f32)
    ve1 = np.asarray(L0_attn_v_emb1, f32)
    gate_w = np.asarray(L0_ffn_gate_w, f32)
    wg = np.asarray(L0_ffn_wg, f32)
    wu = np.asarray(L0_ffn_wu, f32)
    wd = np.asarray(L0_ffn_wd, f32)
    sg = np.asarray(L0_ffn_sg, f32)
    su = np.asarray(L0_ffn_su, f32)
    sd = np.asarray(L0_ffn_sd, f32)

    def _pmaj(w, nco):
        return np.ascontiguousarray(
            w.reshape(NKT, 128, nco, w.shape[1] // nco).transpose(2, 1, 0, 3)
            .reshape(nco, 128, NKT * (w.shape[1] // nco)))

    xT = np.ascontiguousarray(x.T)
    wopad = np.zeros((16, 128, 8 * 128), f32)
    for g in range(8):
        for j in range(HPG):
            src = Wo[8 * (HPG * g + j):8 * (HPG * g + j) + 8]
            for dm in range(16):
                wopad[dm, 32 * j:32 * j + 8, 128 * g:128 * (g + 1)] = \
                    src[:, 128 * dm:128 * (dm + 1)]
    wopad = wopad.astype(bf)
    wgate = _pmaj(Wg, 16).astype(bf)
    wgm = np.ascontiguousarray(
        wg.reshape(NE, NKT, 128, 6, 128).transpose(0, 3, 2, 1, 4)
        .reshape(NE, 6, 128, NKT * 128)).astype(bf)
    wum = np.ascontiguousarray(
        wu.reshape(NE, NKT, 128, 6, 128).transpose(0, 3, 2, 1, 4)
        .reshape(NE, 6, 128, NKT * 128)).astype(bf)
    wd = wd.astype(bf)
    tokid_h = (np.arange(128)[:, None] + 128 * np.arange(8)[None, :]).astype(np.int32)
    eoff_h = (TCAP * np.arange(NE)[:, None]).astype(np.float32)
    sgm = _pmaj(sg, 12).astype(bf)
    sum_m = _pmaj(su, 12).astype(bf)
    sdm = np.ascontiguousarray(
        sd.reshape(12, 128, 16, 128).transpose(2, 1, 0, 3)
        .reshape(16, 128, 12 * 128)).astype(bf)
    gate_w = np.ascontiguousarray(
        gate_w.reshape(NKT, 128, NE).transpose(1, 0, 2).reshape(128, NKT * NE))
    ident = np.eye(128, dtype=f32)
    ones = np.ones((128, 128), f32)
    sel = np.zeros((128, 128), f32)
    bdm = np.zeros((128, HPG), f32)
    padmask = np.zeros((128, 1), f32)
    for j in range(HPG):
        sel[32 * j + 8, 32 * j:32 * j + 32] = 1.0
        bdm[32 * j:32 * j + 8, j] = 1.0
        padmask[32 * j:32 * j + 8, 0] = 1.0
    cbsel = np.zeros((NE, NE * 128), f32)
    for e in range(NE):
        cbsel[e, 128 * e:128 * (e + 1)] = 1.0
    masks = np.zeros((128, 4 * HPG * QW), f32)
    qr = np.arange(QW)
    kr = np.arange(128)[:, None]
    for p in range(4):
        m = (128 * p + kr <= qr[None, :]).astype(f32)
        for h in range(HPG):
            masks[:, p * HPG * QW + QW * h:p * HPG * QW + QW * (h + 1)] = m

    shared = {
        "wopad": wopad, "wgate": wgate, "gate_w": gate_w, "wgm": wgm, "wum": wum,
        "wdp": wd, "sgm": sgm, "sum_m": sum_m, "sdm": sdm, "ident": ident,
        "tokid": tokid_h, "eoff": eoff_h,
        "ones": ones, "sel": sel, "bd": bdm.astype(bf), "padmask": padmask,
        "cbsel": cbsel.astype(bf), "masks": masks.astype(bf),
    }
    in_maps = []
    for c in range(NCORES):
        ve0p = np.zeros((128, 1), f32)
        ve1p = np.zeros((128, 1), f32)
        for j in range(HPG):
            ve0p[32 * j:32 * j + 8, 0] = ve0[8 * (HPG * c + j):8 * (HPG * c + j) + 8]
            ve1p[32 * j:32 * j + 8, 0] = ve1[8 * (HPG * c + j):8 * (HPG * c + j) + 8]
        mselc = np.zeros((128, NCORES), f32)
        mselc[:, c] = 1.0
        m = dict(shared)
        xc = xT[:, TOK * c:TOK * (c + 1)]
        m.update({
            "xT_own": np.ascontiguousarray(
                xc.reshape(NKT, 128, TOK).transpose(1, 0, 2).reshape(128, NKT * TOK)),
            "wqp": np.ascontiguousarray(
                _pad_heads(Wq, c).reshape(NKT, 128, 128).transpose(1, 0, 2)
                .reshape(128, NKT * 128)).astype(bf),
            "wkp": np.ascontiguousarray(
                _pad_heads(Wk, c).reshape(NKT, 128, 128).transpose(1, 0, 2)
                .reshape(128, NKT * 128)).astype(bf),
            "wv": np.ascontiguousarray(
                Wv[:, 32 * c:32 * (c + 1)].reshape(NKT, 128, 32).transpose(1, 0, 2)
                .reshape(128, NKT * 32)).astype(bf),
            "vemb0p": ve0p, "vemb1p": ve1p, "msel": mselc,
        })
        in_maps.append(m)

    nc = build_program()
    res = run_bass_kernel_spmd(nc, in_maps, core_ids=list(range(NCORES)))
    out = np.concatenate([res.results[c]["y"] for c in range(NCORES)], axis=0)
    return out.reshape(1, S, D).astype(np.float32)


# version 09
# speedup vs baseline: 1.3330x; 1.1212x over previous
import numpy as np
import ml_dtypes
import concourse.bass as bass
import concourse.mybir as mybir
from concourse.bass_utils import run_bass_kernel_spmd
from concourse.bass import IndirectOffsetOnAxis
from contextlib import ExitStack
import os

_PHASES = int(os.environ.get("KPHASES", "99"))
_DEBUG = int(os.environ.get("KDEBUG", "0"))

F32 = mybir.dt.float32
BF16 = mybir.dt.bfloat16
AF = mybir.ActivationFunctionType
ALU = mybir.AluOpType
AX = mybir.AxisListType

NCORES = 8
S = 8192
D = 2048
NKT = D // 128
TOK = S // NCORES
HPG = 4
NE, TOPK, FE, FSH = 32, 6, 768, 1536
EPS = 1e-5
QW = 512
NQC = 16
SKT = S // 128
TCAP = 256
A = TCAP // 128
NA = NE * A


class Q:

    def __init__(self, eng, name, sem, inc):
        self.eng, self.name, self.sem, self.inc = eng, name, sem, inc
        self.n = 0
        self.waited = {}

    def wait(self, token):
        if token is None:
            return
        name, cnt, sem = token
        if self.waited.get(name, -1) >= cnt:
            return
        self.eng.wait_ge(sem, cnt)
        self.waited[name] = cnt

    def self_wait(self):
        self.wait((self.name, self.n, self.sem))

    def emit(self, method, *args, deps=(), inc=None, **kw):
        for d in deps:
            self.wait(d)
        instr = getattr(self.eng, method)(*args, **kw)
        step = self.inc if inc is None else inc
        instr.then_inc(self.sem, step)
        self.n += step
        return (self.name, self.n, self.sem)

    def tok(self):
        return (self.name, self.n, self.sem)


class Buf:

    def __init__(self):
        self.w = None
        self.r = []

    def wdeps(self):
        return ([self.w] if self.w else []) + self.r

    def rdeps(self):
        return [self.w] if self.w else []

    def wr(self, tok):
        self.w = tok
        self.r = []

    def rd(self, tok):
        self.r.append(tok)


def build_program():
    nc = bass.Bass("TRN2", target_bir_lowering=False)
    dt = nc.dram_tensor
    xT_own = dt("xT_own", [128, NKT * TOK], F32, kind="ExternalInput").ap()
    wqp = dt("wqp", [128, NKT * 128], BF16, kind="ExternalInput").ap()
    wkp = dt("wkp", [128, NKT * 128], BF16, kind="ExternalInput").ap()
    wv = dt("wv", [128, NKT * 32], BF16, kind="ExternalInput").ap()
    vemb0p = dt("vemb0p", [128, 1], F32, kind="ExternalInput").ap()
    vemb1p = dt("vemb1p", [128, 1], F32, kind="ExternalInput").ap()
    msel = dt("msel", [128, NCORES], F32, kind="ExternalInput").ap()
    wopad = dt("wopad", [16, 128, 8 * 128], BF16, kind="ExternalInput").ap()
    wgate = dt("wgate", [16, 128, NKT * 128], BF16, kind="ExternalInput").ap()
    gate_w = dt("gate_w", [128, NKT * NE], F32, kind="ExternalInput").ap()
    wgm = dt("wgm", [NE, 6, 128, NKT * 128], BF16, kind="ExternalInput").ap()
    wum = dt("wum", [NE, 6, 128, NKT * 128], BF16, kind="ExternalInput").ap()
    wdp = dt("wdp", [NE, FE, D], BF16, kind="ExternalInput").ap()
    sgm = dt("sgm", [12, 128, NKT * 128], BF16, kind="ExternalInput").ap()
    sum_m = dt("sum_m", [12, 128, NKT * 128], BF16, kind="ExternalInput").ap()
    sdm = dt("sdm", [16, 128, 12 * 128], BF16, kind="ExternalInput").ap()
    ident = dt("ident", [128, 128], F32, kind="ExternalInput").ap()
    ones = dt("ones", [128, 128], F32, kind="ExternalInput").ap()
    sel = dt("sel", [128, 128], F32, kind="ExternalInput").ap()
    bd = dt("bd", [128, HPG], BF16, kind="ExternalInput").ap()
    padmask = dt("padmask", [128, 1], F32, kind="ExternalInput").ap()
    cbsel = dt("cbsel", [NE, NE * 128], BF16, kind="ExternalInput").ap()
    masks = dt("masks", [128, 4 * HPG * QW], BF16, kind="ExternalInput").ap()
    tokid = dt("tokid", [128, 8], mybir.dt.int32, kind="ExternalInput").ap()
    eoff = dt("eoff", [NE, 1], F32, kind="ExternalInput").ap()
    ln1_own_d = dt("ln1_own_d", [D, TOK], BF16, kind="Internal").ap()
    ln1_all = dt("ln1_all", [NCORES * D, TOK], BF16, kind="Internal",
                 addr_space="Shared").ap()
    attn_own_d = dt("attn_own_d", [128, S], BF16, kind="Internal").ap()
    attn_all = dt("attn_all", [NCORES * 128, S], BF16, kind="Internal",
                  addr_space="Shared").ap()
    ln2tok_d = dt("ln2tok_d", [TOK, D], BF16, kind="Internal").ap()
    comb_tok_d = dt("comb_tok_d", [TOK, NE], F32, kind="Internal").ap()
    idx_d = dt("idx_d", [NE, TCAP], mybir.dt.int32, kind="Internal").ap()
    stage_d = dt("stage_d", [3, TCAP, D], BF16, kind="Internal").ap()
    racc = [dt(f"racc{i}", [TOK, D], F32, kind="Internal").ap() for i in range(4)]
    y = dt("y", [TOK, D], F32, kind="ExternalOutput").ap()
    if _DEBUG:
        d_ln1 = dt("d_ln1", [128, NKT * TOK], F32, kind="ExternalOutput").ap()
        d_attn = dt("d_attn", [128, S], F32, kind="ExternalOutput").ap()
        d_xmid = dt("d_xmid", [128, NKT * TOK], F32, kind="ExternalOutput").ap()
        d_ln2 = dt("d_ln2", [128, NKT * TOK], F32, kind="ExternalOutput").ap()
        d_comb = dt("d_comb", [NE, TOK], F32, kind="ExternalOutput").ap()
        d_atn = dt("d_atn", [128, 8 * TOK], F32, kind="ExternalOutput").ap()
        d_idx = dt("d_idx", [NE, TCAP], mybir.dt.int32, kind="ExternalOutput").ap()
        d_xg = dt("d_xg", [128, NKT * TCAP], F32, kind="ExternalOutput").ap()

    with ExitStack() as ctx:
        EE = ctx.enter_context
        TE = Q(nc.tensor, "te", EE(nc.semaphore("s_te")), 1)
        SC = Q(nc.scalar, "sc", EE(nc.semaphore("s_sc")), 1)
        VE = Q(nc.vector, "ve", EE(nc.semaphore("s_ve")), 1)
        SY = Q(nc.sync, "sy", EE(nc.semaphore("s_sy")), 16)
        GP = Q(nc.gpsimd, "gp", EE(nc.semaphore("s_gp")), 16)
        sem_ag1 = EE(nc.semaphore("s_ag1"))
        sem_ag2 = EE(nc.semaphore("s_ag2"))
        r_btcap = nc.gpsimd.to_reg(NE * TCAP)
        r_btok = nc.gpsimd.to_reg(TOK)
        r_b2tok = nc.gpsimd.to_reg(2 * TOK)
        ALL = (TE, SC, VE, SY, GP)

        def barrier():
            toks = [q.tok() for q in ALL]
            for q in ALL:
                for t in toks:
                    q.wait(t)

        xT = EE(nc.sbuf_tensor("xT", [128, NKT * TOK], F32))
        id_sb = EE(nc.sbuf_tensor("id_sb", [128, 128], F32))
        on_sb = EE(nc.sbuf_tensor("on_sb", [128, 128], F32))
        sel_sb = EE(nc.sbuf_tensor("sel_sb", [128, 128], F32))
        bd_sb = EE(nc.sbuf_tensor("bd_sb", [128, HPG], BF16))
        pm_sb = EE(nc.sbuf_tensor("pm_sb", [128, 1], F32))
        ve0_sb = EE(nc.sbuf_tensor("ve0_sb", [128, 1], F32))
        ve1_sb = EE(nc.sbuf_tensor("ve1_sb", [128, 1], F32))
        vd_sb = EE(nc.sbuf_tensor("vd_sb", [128, 1], F32))
        ms_sb = EE(nc.sbuf_tensor("ms_sb", [128, NCORES], F32))

        t_x = SY.emit("dma_start", out=xT[:], in_=xT_own)
        SY.emit("dma_start", out=id_sb[:], in_=ident)
        SY.emit("dma_start", out=on_sb[:], in_=ones)
        SY.emit("dma_start", out=sel_sb[:], in_=sel)
        SY.emit("dma_start", out=bd_sb[:], in_=bd)
        SY.emit("dma_start", out=pm_sb[:], in_=padmask)
        SY.emit("dma_start", out=ve0_sb[:], in_=vemb0p)
        t_ve = SY.emit("dma_start", out=ve1_sb[:], in_=vemb1p)
        t_msel = SY.emit("dma_start", out=ms_sb[:], in_=msel)
        t_vd = VE.emit("tensor_sub", vd_sb[:], ve1_sb[:], ve0_sb[:], deps=[t_ve])
        t_const = SY.tok()

        def layernorm(dst_bf16, fp32_consumer=None, name=""):
            with ExitStack() as c2:
                E2 = c2.enter_context
                sq = E2(nc.sbuf_tensor(f"sq{name}", [128, 2 * TOK], F32))
                stat = E2(nc.sbuf_tensor(f"stat{name}", [1, 4 * TOK], F32))
                lnf = E2(nc.sbuf_tensor(f"lnf{name}", [128, 2 * TOK], F32))
                mb_sb = E2(nc.sbuf_tensor(f"mb{name}", [128, TOK], F32))
                rb_sb = E2(nc.sbuf_tensor(f"rb{name}", [128, TOK], F32))
                with ExitStack() as c3:
                    E3 = c3.enter_context
                    ps_sum = E3(nc.psum_tensor(f"pssum{name}", [1, TOK], F32))
                    ps_ssq = E3(nc.psum_tensor(f"psssq{name}", [1, TOK], F32))
                    t_sq = [None, None]
                    t_ssqmm = None
                    for kt in range(NKT):
                        xkt = xT[:, TOK * kt:TOK * (kt + 1)]
                        sqb = sq[:, TOK * (kt % 2):TOK * (kt % 2 + 1)]
                        t_sq[kt % 2] = SC.emit("activation", sqb, xkt, AF.Square,
                                               deps=[t_x] + ([t_ssqmm] if kt >= 2 else []))
                        for h in range(2):
                            TE.emit("matmul", ps_sum[:, 512 * h:512 * (h + 1)],
                                    on_sb[:, 0:1], xkt[:, 512 * h:512 * (h + 1)],
                                    start=(kt == 0), stop=(kt == NKT - 1),
                                    deps=[t_x, t_const])
                        for h in range(2):
                            t_ssqmm = TE.emit("matmul", ps_ssq[:, 512 * h:512 * (h + 1)],
                                              on_sb[:, 0:1], sqb[:, 512 * h:512 * (h + 1)],
                                              start=(kt == 0), stop=(kt == NKT - 1),
                                              deps=[t_sq[kt % 2]])
                    t_summ = TE.tok()
                    VE.emit("tensor_scalar_mul", stat[:, 0:TOK], ps_sum[:], 1.0 / D,
                            deps=[t_summ])
                    VE.self_wait()
                    VE.emit("tensor_tensor", stat[:, TOK:2 * TOK], stat[:, 0:TOK],
                            stat[:, 0:TOK], ALU.mult)
                    VE.emit("tensor_scalar_mul", stat[:, 2 * TOK:3 * TOK], ps_ssq[:],
                            1.0 / D, deps=[t_summ])
                    VE.self_wait()
                    VE.emit("tensor_sub", stat[:, 2 * TOK:3 * TOK],
                            stat[:, 2 * TOK:3 * TOK], stat[:, TOK:2 * TOK])
                    VE.self_wait()
                    t_var = VE.emit("tensor_scalar_add", stat[:, 2 * TOK:3 * TOK],
                                    stat[:, 2 * TOK:3 * TOK], EPS)
                    VE.self_wait()
                    t_ln = SC.emit("activation", stat[:, 3 * TOK:4 * TOK],
                                   stat[:, 2 * TOK:3 * TOK], AF.Ln, deps=[t_var])
                    t_half = VE.emit("tensor_scalar_mul", stat[:, 3 * TOK:4 * TOK],
                                     stat[:, 3 * TOK:4 * TOK], -0.5, deps=[t_ln])
                    VE.self_wait()
                    t_rs = SC.emit("activation", stat[:, 3 * TOK:4 * TOK],
                                   stat[:, 3 * TOK:4 * TOK], AF.Exp, deps=[t_half])
                with ExitStack() as c3:
                    E3 = c3.enter_context
                    ps_mb = E3(nc.psum_tensor(f"psmb{name}", [128, TOK], F32))
                    ps_rb = E3(nc.psum_tensor(f"psrb{name}", [128, TOK], F32))
                    TE.self_wait()
                    for h in range(2):
                        TE.emit("matmul", ps_mb[:, 512 * h:512 * (h + 1)], on_sb[0:1, :],
                                stat[0:1, 512 * h:512 * (h + 1)], start=True, stop=True,
                                deps=[t_rs])
                    for h in range(2):
                        t_bc = TE.emit("matmul", ps_rb[:, 512 * h:512 * (h + 1)],
                                       on_sb[0:1, :],
                                       stat[0:1, 3 * TOK + 512 * h:3 * TOK + 512 * (h + 1)],
                                       start=True, stop=True)
                    VE.emit("tensor_copy", mb_sb[:], ps_mb[:], deps=[t_bc])
                    t_rbc = VE.emit("tensor_copy", rb_sb[:], ps_rb[:])
                    VE.self_wait()
                prev = [None, None]
                for kt in range(NKT):
                    xkt = xT[:, TOK * kt:TOK * (kt + 1)]
                    lbuf = lnf[:, TOK * (kt % 2):TOK * (kt % 2 + 1)]
                    dep_reuse = list(prev[kt % 2] or ())
                    VE.emit("tensor_sub", lbuf, xkt, mb_sb[:],
                            deps=[t_rbc] + [d for d in dep_reuse if d])
                    VE.self_wait()
                    t_n = VE.emit("tensor_tensor", lbuf, lbuf, rb_sb[:], ALU.mult)
                    VE.self_wait()
                    t_cast = SC.emit("activation", dst_bf16[:, TOK * kt:TOK * (kt + 1)],
                                     lbuf, AF.Copy, deps=[t_n])
                    t_cons = fp32_consumer(kt, lbuf, t_n) if fp32_consumer else None
                    prev[kt % 2] = (t_cast, t_cons)
                barrier()

        with ExitStack() as cA:
            EA = cA.enter_context
            ln1o = EA(nc.sbuf_tensor("ln1o", [128, NKT * TOK], BF16))

            layernorm(ln1o, name="l1")
            t_st1 = GP.emit("dma_start",
                            out=ln1_own_d.rearrange("(k p) t -> p k t", p=128),
                            in_=ln1o.rearrange("p (k t) -> p k t", k=NKT))
            GP.wait(t_st1)
            nc.gpsimd.collective_compute("AllGather", ALU.bypass,
                                         [list(range(NCORES))], [ln1_own_d],
                                         [ln1_all]).then_inc(sem_ag1, 1)
            t_ag1 = ("ag1", 1, sem_ag1)
            if _DEBUG:
                GP.emit("dma_start", out=d_ln1, in_=ln1o[:], deps=[t_st1])

            with ExitStack() as cB:
                EB = cB.enter_context
                msk_sb = EB(nc.sbuf_tensor("msk_sb", [128, 4 * HPG * QW], BF16))
                pq_pad = EB(nc.sbuf_tensor("pq_pad", [128, S], BF16))
                pk_pad = EB(nc.sbuf_tensor("pk_pad", [128, S], BF16))
                pv_pad = EB(nc.sbuf_tensor("pv_pad", [128, SKT * HPG * 9], BF16))
                t_mskl = SY.emit("dma_start", out=msk_sb[:], in_=masks)

                if _PHASES <= 0:
                    return nc
                with ExitStack() as c2:
                    E2 = c2.enter_context
                    lns = E2(nc.sbuf_tensor("lns", [128, NKT * 512], BF16))
                    wq_sb = E2(nc.sbuf_tensor("wq_sb", [128, NKT * 128], BF16))
                    wk_sb = E2(nc.sbuf_tensor("wk_sb", [128, NKT * 128], BF16))
                    wv_sb = E2(nc.sbuf_tensor("wv_sb", [128, NKT * 32], BF16))
                    prq = E2(nc.sbuf_tensor("prq", [128, 512], BF16))
                    esk = E2(nc.sbuf_tensor("esk", [128, HPG], F32))
                    ps_q = [E2(nc.psum_tensor(f"ps_q{i}", [128, 512], F32)) for i in range(2)]
                    ps_k = [E2(nc.psum_tensor(f"ps_k{i}", [128, 512], F32)) for i in range(2)]
                    ps_v = [E2(nc.psum_tensor(f"ps_v{i}", [128, 32], F32)) for i in range(2)]
                    ps_sk = E2(nc.psum_tensor("ps_sk", [128, HPG], F32))
                    bq = [Buf(), Buf()]
                    bk = [Buf(), Buf()]
                    bv = [Buf(), Buf()]
                    bsk, blns, bprq, besk = Buf(), Buf(), Buf(), Buf()

                    t_wq = GP.emit("dma_start", out=wq_sb[:], in_=wqp)
                    t_wk = GP.emit("dma_start", out=wk_sb[:], in_=wkp)
                    t_wv = GP.emit("dma_start", out=wv_sb[:], in_=wv)
                    t_pv1 = VE.emit(
                        "memset",
                        pv_pad.rearrange("p (k h n) -> p k h n", k=SKT, h=HPG)[:, :, :, 8:9],
                        1.0)
                    ln1v = ln1_all.rearrange("(r k p) t -> r k p t", r=NCORES, k=NKT)
                    for ch in range(16):
                        r, s0 = ch // 2, 512 * (ch % 2)
                        t_ld = SY.emit("dma_start",
                                       out=lns.rearrange("p (k t) -> p k t", k=NKT),
                                       in_=ln1v[r, :, :, s0:s0 + 512].rearrange(
                                           "k p t -> p k t"),
                                       deps=[t_ag1] + blns.wdeps())
                        blns.wr(t_ld)
                        pq_, bq_ = ps_q[ch % 2], bq[ch % 2]
                        for kt in range(NKT):
                            t_mm = TE.emit("matmul", pq_[:],
                                           wq_sb[:, 128 * kt:128 * (kt + 1)],
                                           lns[:, 512 * kt:512 * (kt + 1)],
                                           start=(kt == 0), stop=(kt == NKT - 1),
                                           deps=[t_ld, t_wq] + (bq_.wdeps() if kt == 0 else []))
                        bq_.wr(t_mm)
                        blns.rd(t_mm)
                        t_sg = SC.emit("activation", prq[:], pq_[:], AF.Sigmoid,
                                       deps=[t_mm] + bprq.wdeps())
                        bq_.rd(t_sg)
                        bprq.wr(t_sg)
                        t_pq = VE.emit("tensor_scalar_mul",
                                       pq_pad[:, 512 * ch:512 * (ch + 1)], prq[:],
                                       pm_sb[:, 0:1], deps=[t_sg, t_const])
                        bprq.rd(t_pq)
                        pk_, bk_ = ps_k[ch % 2], bk[ch % 2]
                        for kt in range(NKT):
                            t_mm = TE.emit("matmul", pk_[:],
                                           wk_sb[:, 128 * kt:128 * (kt + 1)],
                                           lns[:, 512 * kt:512 * (kt + 1)],
                                           start=(kt == 0), stop=(kt == NKT - 1),
                                           deps=[t_wk] + (bk_.wdeps() if kt == 0 else []))
                        bk_.wr(t_mm)
                        blns.rd(t_mm)
                        t_pk = SC.emit("activation", pk_pad[:, 512 * ch:512 * (ch + 1)],
                                       pk_[:], AF.Sigmoid, deps=[t_mm])
                        bk_.rd(t_pk)
                        for tt in range(4):
                            kk = 4 * ch + tt
                            pv_, bv_ = ps_v[tt % 2], bv[tt % 2]
                            for kt in range(NKT):
                                t_mm = TE.emit(
                                    "matmul", pv_[:],
                                    lns[:, 512 * kt + 128 * tt:512 * kt + 128 * (tt + 1)],
                                    wv_sb[:, 32 * kt:32 * (kt + 1)],
                                    start=(kt == 0), stop=(kt == NKT - 1),
                                    deps=[t_wv] + (bv_.wdeps() if kt == 0 else []))
                            bv_.wr(t_mm)
                            blns.rd(t_mm)
                            pvv = pv_pad.rearrange("p (k h n) -> p k h n", k=SKT, h=HPG)
                            t_ev = SC.emit("activation", pvv[:, kk, :, 0:8],
                                           pv_.ap().rearrange("p (h n) -> p h n", h=HPG),
                                           AF.Sigmoid, deps=[t_mm, t_pv1])
                            bv_.rd(t_ev)
                            TE.self_wait()
                            t_skm = TE.emit("matmul", ps_sk[:],
                                            pk_pad[:, 128 * kk:128 * (kk + 1)], bd_sb[:],
                                            start=True, stop=True,
                                            deps=[t_pk] + bsk.wdeps())
                            bsk.wr(t_skm)
                            t_esk = SC.emit("activation", esk[:], ps_sk[:], AF.Exp,
                                            scale=-1.0, deps=[t_skm] + besk.wdeps())
                            bsk.rd(t_esk)
                            besk.wr(t_esk)
                            for h in range(HPG):
                                pvh = pvv[:, kk, h, 0:9]
                                t_sc9 = VE.emit("tensor_scalar_mul", pvh, pvh,
                                                esk[:, h:h + 1], deps=[t_esk, t_ev])
                                besk.rd(t_sc9)
                            VE.self_wait()
                barrier()

                if _PHASES <= 1:
                    return nc
                with ExitStack() as c2:
                    E2 = c2.enter_context
                    ps_dot = E2(nc.psum_tensor("ps_dot", [128, HPG * QW], F32))
                    ps_nd = E2(nc.psum_tensor("ps_nd", [128, QW], F32))
                    E_sb = E2(nc.sbuf_tensor("E_sb", [128, 2 * HPG * QW], BF16))
                    num_sb = E2(nc.sbuf_tensor("num_sb", [128, QW], F32))
                    rec_sb = E2(nc.sbuf_tensor("rec_sb", [128, QW], F32))
                    att_sb = E2(nc.sbuf_tensor("att_sb", [128, QW], F32))
                    atb_sb = E2(nc.sbuf_tensor("atb_sb", [128, QW], BF16))
                    bdh = [Buf(), Buf()]
                    bE = [Buf(), Buf()]
                    bnd, bnsb, brec, batt, batb = (Buf() for _ in range(5))

                    t_z = VE.emit("memset", ps_nd[:], 0.0)
                    bnd.wr(t_z)
                    t_stc = None
                    git = 0
                    for j in range(NQC):
                        q0 = QW * j
                        Kj = 4 * (j + 1)
                        pend = None

                        def emit_av(pkk, pp, pw, last):
                            TE.self_wait()
                            for h in range(HPG):
                                t_av = TE.emit(
                                    "matmul", ps_nd[32 * h:32 * h + 9, :],
                                    pv_pad[:, (HPG * pkk + h) * 9:(HPG * pkk + h) * 9 + 9],
                                    E_sb[:, pp * HPG * QW + QW * h:pp * HPG * QW + QW * h + QW],
                                    start=(pkk == 0), stop=last,
                                    tile_position=(0, 32 * h), skip_group_check=True,
                                    deps=([pw] + (bnd.wdeps() if pkk == 0 else []))
                                    if h == 0 else [])
                            bnd.wr(t_av)
                            bE[pp].rd(t_av)
                            TE.self_wait()
                            return t_av

                        for kk in range(Kj):
                            par = git % 2
                            for half in range(2):
                                for h in (2 * half, 2 * half + 1):
                                    t_dot = TE.emit(
                                        "matmul", ps_dot[:, QW * h:QW * (h + 1)],
                                        pk_pad[32 * h:32 * h + 32, 128 * kk:128 * (kk + 1)],
                                        pq_pad[32 * h:32 * h + 32, q0:q0 + QW],
                                        start=True, stop=True, tile_position=(32 * h, 0),
                                        deps=(bdh[half].wdeps() if h == 2 * half else []))
                                bdh[half].wr(t_dot)
                            t_exp = None
                            for half in range(2):
                                Ebh = E_sb[:, par * HPG * QW + 2 * QW * half:
                                           par * HPG * QW + 2 * QW * (half + 1)]
                                t_exp = SC.emit(
                                    "activation", Ebh,
                                    ps_dot[:, 2 * QW * half:2 * QW * (half + 1)],
                                    AF.Exp, scale=2.0,
                                    deps=[bdh[half].w] + (bE[par].wdeps() if half == 0 else []))
                                bdh[half].rd(t_exp)
                                if half == 0:
                                    bE[par].wr(t_exp)
                            bE[par].wr(t_exp)
                            dpat = kk - 4 * j
                            if 0 <= dpat <= 3:
                                moff = dpat * HPG * QW
                                t_m = VE.emit("tensor_tensor",
                                              E_sb[:, par * HPG * QW:(par + 1) * HPG * QW],
                                              E_sb[:, par * HPG * QW:(par + 1) * HPG * QW],
                                              msk_sb[:, moff:moff + HPG * QW], ALU.mult,
                                              deps=[t_exp, t_mskl])
                                bE[par].wr(t_m)
                            if pend is not None:
                                emit_av(pend[0], pend[1], pend[2], False)
                            pend = (kk, par, bE[par].w)
                            git += 1
                        t_av = emit_av(pend[0], pend[1], pend[2], True)
                        t_evn = VE.emit("tensor_copy", num_sb[:], ps_nd[:],
                                        deps=[t_av] + bnsb.wdeps())
                        bnd.rd(t_evn)
                        bnsb.wr(t_evn)
                        VE.self_wait()
                        TE.self_wait()
                        t_den = TE.emit("matmul", ps_nd[:], sel_sb[:], num_sb[:],
                                        start=True, stop=True,
                                        deps=[t_evn] + bnd.wdeps())
                        bnsb.rd(t_den)
                        bnd.wr(t_den)
                        TE.self_wait()
                        t_rc = VE.emit("reciprocal", rec_sb[:], ps_nd[:],
                                       deps=[t_den] + brec.wdeps())
                        bnd.rd(t_rc)
                        brec.wr(t_rc)
                        VE.self_wait()
                        t_at = VE.emit("tensor_tensor", att_sb[:], num_sb[:],
                                       rec_sb[:], ALU.mult, deps=batt.wdeps())
                        bnsb.rd(t_at)
                        brec.rd(t_at)
                        batt.wr(t_at)
                        VE.self_wait()
                        t_ab = VE.emit("tensor_scalar", atb_sb[:], att_sb[:],
                                       vd_sb[:, 0:1], ve0_sb[:, 0:1], ALU.mult, ALU.add,
                                       deps=[t_vd] + batb.wdeps())
                        batt.rd(t_ab)
                        batb.wr(t_ab)
                        VE.self_wait()
                        t_stc = GP.emit("dma_start", out=attn_own_d[:, q0:q0 + QW],
                                        in_=atb_sb[:], deps=[t_ab])
                        batb.rd(t_stc)
                    GP.wait(t_stc)
                    nc.gpsimd.collective_compute("AllGather", ALU.bypass,
                                                 [list(range(NCORES))], [attn_own_d],
                                                 [attn_all]).then_inc(sem_ag2, 1)
                    t_ag2 = ("ag2", 1, sem_ag2)
                    if _DEBUG:
                        GP.emit("dma_start", out=d_attn, in_=attn_own_d, deps=[t_stc])
                barrier()
            barrier()

            if _PHASES <= 2:
                return nc
            with ExitStack() as c2:
                E2 = c2.enter_context
                atf = E2(nc.sbuf_tensor("atf", [128, S], BF16))
                atn = E2(nc.sbuf_tensor("atn", [128, 8 * TOK], BF16))
                wo_sb = E2(nc.sbuf_tensor("wo_sb", [128, 2 * 8 * 128], BF16))
                wgt_sb = E2(nc.sbuf_tensor("wgt_sb", [128, NKT * 128], BF16))
                gat_sb = E2(nc.sbuf_tensor("gat_sb", [128, TOK], BF16))
                tmp_sb = E2(nc.sbuf_tensor("tmp_sb", [128, TOK], F32))
                ps_wo = [E2(nc.psum_tensor(f"ps_wo{i}", [128, TOK], F32)) for i in range(2)]
                ps_gt = [E2(nc.psum_tensor(f"ps_gt{i}", [128, TOK], F32)) for i in range(2)]
                bwo = [Buf(), Buf()]
                bgt = [Buf(), Buf()]
                bwos = [Buf(), Buf()]
                bwgs, bgat, batf = Buf(), Buf(), Buf()
                batn = Buf()
                av = attn_all.rearrange("(g p) t -> g p t", p=128)
                for g in range(8):
                    t_af = SY.emit("dma_start", out=atf[:], in_=av[g],
                                   deps=[t_ag2] + batf.wdeps())
                    batf.wr(t_af)
                    dst = atn[:, TOK * g:TOK * (g + 1)]
                    t_a0 = VE.emit("tensor_scalar_mul", dst, atf[:, 0:TOK],
                                   ms_sb[:, 0:1], deps=[t_af, t_msel] + batn.wdeps())
                    for b in range(1, NCORES):
                        VE.self_wait()
                        t_a0 = VE.emit("scalar_tensor_tensor", dst,
                                       atf[:, TOK * b:TOK * (b + 1)], ms_sb[:, b:b + 1],
                                       dst, ALU.mult, ALU.add)
                    VE.self_wait()
                    batn.wr(t_a0)
                    batf.rd(t_a0)
                t_atn = VE.tok()
                for dm in range(NKT):
                    p = dm % 2
                    t_wol = GP.emit("dma_start",
                                    out=wo_sb[:, p * 1024:(p + 1) * 1024],
                                    in_=wopad[dm],
                                    deps=bwos[p].wdeps())
                    bwos[p].wr(t_wol)
                    t_wgl = GP.emit("dma_start",
                                    out=wgt_sb[:],
                                    in_=wgate[dm],
                                    deps=bwgs.wdeps())
                    bwgs.wr(t_wgl)
                    for h in range(2):
                        for g in range(8):
                            t_womm = TE.emit(
                                "matmul", ps_wo[p][:, 512 * h:512 * (h + 1)],
                                wo_sb[:, p * 1024 + 128 * g:p * 1024 + 128 * (g + 1)],
                                atn[:, TOK * g + 512 * h:TOK * g + 512 * (h + 1)],
                                start=(g == 0), stop=(g == 7),
                                deps=[t_wol, t_atn] +
                                     (bwo[p].wdeps() if g == 0 and h == 0 else []))
                    bwo[p].wr(t_womm)
                    bwos[p].rd(t_womm)
                    for h in range(2):
                        for kt in range(NKT):
                            t_gtm = TE.emit(
                                "matmul", ps_gt[p][:, 512 * h:512 * (h + 1)],
                                wgt_sb[:, 128 * kt:128 * (kt + 1)],
                                ln1o[:, TOK * kt + 512 * h:TOK * kt + 512 * (h + 1)],
                                start=(kt == 0), stop=(kt == NKT - 1),
                                deps=[t_wgl] +
                                     (bgt[p].wdeps() if kt == 0 and h == 0 else []))
                    bgt[p].wr(t_gtm)
                    bwgs.rd(t_gtm)
                    t_sg = SC.emit("activation", gat_sb[:], ps_gt[p][:], AF.Sigmoid,
                                   deps=[t_gtm] + bgat.wdeps())
                    bgt[p].rd(t_sg)
                    bgat.wr(t_sg)
                    t_t = VE.emit("tensor_tensor", tmp_sb[:], ps_wo[p][:], gat_sb[:],
                                  ALU.mult, deps=[t_sg, t_womm])
                    bwo[p].rd(t_t)
                    bgat.rd(t_t)
                    VE.self_wait()
                    VE.emit("tensor_add", xT[:, TOK * dm:TOK * (dm + 1)],
                            xT[:, TOK * dm:TOK * (dm + 1)], tmp_sb[:])
                    VE.self_wait()
                if _DEBUG:
                    GP.emit("dma_start", out=d_xmid, in_=xT[:], deps=[VE.tok()])
                    GP.emit("dma_start", out=d_atn, in_=atn[:], deps=[t_atn])
            barrier()
        barrier()

        if _PHASES <= 3:
            return nc
        with ExitStack() as cC:
            EC = cC.enter_context
            idx_sb = EC(nc.sbuf_tensor("idx_sb", [128, NE * TCAP // 128],
                                       mybir.dt.int32))
            idx2_sb = EC(nc.sbuf_tensor("idx2_sb", [128, 2 * NE * TCAP // 128],
                                        mybir.dt.int32))
            cD = EC(ExitStack())
            ln2o = cD.enter_context(nc.sbuf_tensor("ln2o", [128, NKT * TOK], BF16))
            combT = cD.enter_context(nc.sbuf_tensor("combT", [NE, TOK], BF16))

            with ExitStack() as c2:
                E2 = c2.enter_context
                gw_sb = E2(nc.sbuf_tensor("gw_sb", [128, NKT * NE], F32))
                ps_sc = E2(nc.psum_tensor("ps_sc", [NE, TOK], F32))
                sc_sb = E2(nc.sbuf_tensor("sc_sb", [NE, TOK], F32))
                t_gwl = SY.emit("dma_start", out=gw_sb[:], in_=gate_w)
                t_scm = [None]

                def consume(kt, lbuf, t_n):
                    for h in range(2):
                        t_scm[0] = TE.emit("matmul", ps_sc[:, 512 * h:512 * (h + 1)],
                                           gw_sb[:, NE * kt:NE * (kt + 1)],
                                           lbuf[:, 512 * h:512 * (h + 1)],
                                           start=(kt == 0), stop=(kt == NKT - 1),
                                           deps=[t_n, t_gwl])
                    return t_scm[0]

                layernorm(ln2o, fp32_consumer=consume, name="l2")
                t_sig = SC.emit("activation", sc_sb[:], ps_sc[:], AF.Sigmoid,
                                deps=[t_scm[0]])
                sct = E2(nc.sbuf_tensor("sct", [128, NE], F32))
                work = E2(nc.sbuf_tensor("work", [128, NE], F32))
                tmq = E2(nc.sbuf_tensor("tmq", [128, NE], F32))
                msum = E2(nc.sbuf_tensor("msum", [128, 8], F32))
                comb = E2(nc.sbuf_tensor("comb", [128, NE], F32))
                ps_t1 = E2(nc.psum_tensor("ps_t1", [128, NE], F32))
                ps_t2 = E2(nc.psum_tensor("ps_t2", [NE, 128], F32))
                bt1, bt2 = Buf(), Buf()
                t_cstore = None
                for tt in range(8):
                    TE.self_wait()
                    t_tr = TE.emit("transpose", ps_t1[:],
                                   sc_sb[:, 128 * tt:128 * (tt + 1)],
                                   id_sb[0:NE, 0:NE], deps=[t_sig] + bt1.wdeps())
                    bt1.wr(t_tr)
                    t_sct = VE.emit("tensor_copy", sct[:], ps_t1[:], deps=[t_tr])
                    bt1.rd(t_sct)
                    VE.self_wait()
                    VE.emit("tensor_copy", work[:], sct[:])
                    VE.emit("memset", msum[:, 0:1], 0.0)
                    VE.self_wait()
                    for it in range(TOPK):
                        VE.emit("reduce_max", msum[:, 1:2], work[:], AX.X)
                        VE.self_wait()
                        VE.emit("tensor_add", msum[:, 0:1], msum[:, 0:1], msum[:, 1:2])
                        if it < TOPK - 1:
                            VE.emit("tensor_scalar", tmq[:], work[:], msum[:, 1:2],
                                    None, ALU.is_equal)
                            VE.self_wait()
                            VE.emit("scalar_tensor_tensor", work[:], tmq[:], -100.0,
                                    work[:], ALU.mult, ALU.add)
                            VE.self_wait()
                    VE.self_wait()
                    VE.emit("tensor_scalar", tmq[:], sct[:], msum[:, 1:2], None,
                            ALU.is_ge)
                    VE.emit("tensor_scalar_add", msum[:, 2:3], msum[:, 0:1], 1e-20)
                    VE.self_wait()
                    VE.emit("reciprocal", msum[:, 3:4], msum[:, 2:3])
                    VE.self_wait()
                    VE.emit("tensor_scalar", comb[:], sct[:], msum[:, 3:4], None,
                            ALU.mult, deps=([t_cstore] if tt > 0 else []))
                    VE.self_wait()
                    t_cb = VE.emit("tensor_tensor", comb[:], comb[:], tmq[:], ALU.mult)
                    VE.self_wait()
                    t_cstore = GP.emit("dma_start",
                                       out=comb_tok_d[128 * tt:128 * (tt + 1), :],
                                       in_=comb[:], deps=[t_cb])
                    bcombrow.rd(t_cstore) if False else None
                    TE.self_wait()
                    t_tr2 = TE.emit("transpose", ps_t2[:], comb[:], id_sb[:],
                                    deps=[t_cb] + bt2.wdeps())
                    bt2.wr(t_tr2)
                    t_cT = VE.emit("tensor_copy", combT[0:NE, 128 * tt:128 * (tt + 1)],
                                   ps_t2[:], deps=[t_tr2])
                    bt2.rd(t_cT)
                    VE.self_wait()
                if _DEBUG:
                    GP.emit("dma_start", out=d_ln2, in_=ln2o[:], deps=[VE.tok()])
                    GP.emit("dma_start", out=d_comb, in_=combT[:], deps=[VE.tok()])
                zero_sb = E2(nc.sbuf_tensor("zero_sb", [128, D], F32))
                t_zz0 = VE.emit("memset", zero_sb[:], 0.0)
                VE.self_wait()
                t_rz = None
                for i in range(4):
                    for tt in range(8):
                        t_rz = GP.emit("dma_start",
                                       out=racc[i][128 * tt:128 * (tt + 1), :],
                                       in_=zero_sb[:], deps=[t_zz0])
                ps_tb = E2(nc.psum_tensor("ps_tb", [128, 128], BF16))
                lntt = E2(nc.sbuf_tensor("lntt", [128, D], BF16))
                idb2 = E2(nc.sbuf_tensor("idb2", [128, 128], BF16))
                t_idb = VE.emit("tensor_copy", idb2[:], id_sb[:])
                VE.self_wait()
                btb, blntt = Buf(), Buf()
                for tt in range(8):
                    for kt in range(NKT):
                        TE.self_wait()
                        t_tr = TE.emit("transpose", ps_tb[:],
                                       ln2o[:, TOK * kt + 128 * tt:TOK * kt + 128 * (tt + 1)],
                                       idb2[:], deps=[t_idb] + btb.wdeps())
                        btb.wr(t_tr)
                        t_ev = VE.emit("tensor_copy", lntt[:, 128 * kt:128 * (kt + 1)],
                                       ps_tb[:], deps=[t_tr] + (blntt.wdeps() if kt == 0 else []))
                        btb.rd(t_ev)
                        VE.self_wait()
                    t_lnst = GP.emit("dma_start", out=ln2tok_d[128 * tt:128 * (tt + 1), :],
                                     in_=lntt[:], deps=[VE.tok()])
                    blntt.wr(t_lnst) if False else None
                    blntt.rd(t_lnst)
                t_ln2tok = t_lnst
                msk = E2(nc.sbuf_tensor("msk", [NE, TOK], F32))
                zsc = E2(nc.sbuf_tensor("zsc", [NE, TOK], F32))
                posx = E2(nc.sbuf_tensor("posx", [NE, TOK], F32))
                post_i = E2(nc.sbuf_tensor("post_i", [128, 8 * NE], mybir.dt.int32))
                tokid_sb = E2(nc.sbuf_tensor("tokid_sb", [128, 8], mybir.dt.int32))
                idxf = E2(nc.sbuf_tensor("idxf", [NA, 128], mybir.dt.int32))
                t_tid = SY.emit("dma_start", out=tokid_sb[:], in_=tokid)
                eoff_sb = E2(nc.sbuf_tensor("eoff_sb", [NE, 1], F32))
                t_eo = SY.emit("dma_start", out=eoff_sb[:], in_=eoff)
                VE.emit("tensor_scalar", msk[:], combT[:], 0.0, None, ALU.is_gt,
                        deps=[t_cT])
                VE.emit("memset", zsc[:], 0.0)
                VE.self_wait()
                VE.emit("tensor_tensor_scan", posx[:], msk[:], zsc[:], 0.0,
                        ALU.add, ALU.add)
                VE.self_wait()
                VE.emit("tensor_scalar_add", posx[:], posx[:], -1.0)
                VE.self_wait()
                VE.emit("tensor_scalar", zsc[:], msk[:], -1.0, 1.0, ALU.mult, ALU.add)
                VE.self_wait()
                VE.emit("scalar_tensor_tensor", posx[:], zsc[:], 1.0e6,
                        posx[:], ALU.mult, ALU.add)
                VE.self_wait()
                VE.emit("tensor_scalar", zsc[:], posx[:], float(TCAP) - 0.5, None,
                        ALU.is_ge)
                VE.self_wait()
                VE.emit("scalar_tensor_tensor", posx[:], zsc[:], 1.0e6,
                        posx[:], ALU.mult, ALU.add)
                VE.self_wait()
                t_posx = VE.emit("tensor_scalar_add", posx[:], posx[:],
                                 eoff_sb[:, 0:1], deps=[t_eo])
                VE.self_wait()
                t_if = VE.emit("memset", idxf[:], 100000)
                VE.self_wait()
                t_ifd = GP.emit("dma_start",
                                out=idx_d.rearrange("e (a p) -> (e a) p", p=128),
                                in_=idxf[:], deps=[t_if])
                t_scat = None
                for tt in range(8):
                    TE.self_wait()
                    t_trp = TE.emit("transpose", ps_t1[:],
                                    posx[:, 128 * tt:128 * (tt + 1)],
                                    id_sb[0:NE, 0:NE], deps=[t_posx] + bt1.wdeps())
                    bt1.wr(t_trp)
                    t_pi = VE.emit("tensor_copy", post_i[:, NE * tt:NE * (tt + 1)],
                                   ps_t1[:], deps=[t_trp])
                    bt1.rd(t_pi)
                    VE.self_wait()
                    for e in range(NE):
                        t_scat = GP.emit("indirect_dma_start",
                                         idx_d.rearrange("e (t o) -> (e t) o", o=1),
                                         IndirectOffsetOnAxis(
                                             post_i[:, NE * tt + e:NE * tt + e + 1], 0),
                                         tokid_sb[:, tt:tt + 1], None,
                                         bounds_check=r_btcap, oob_is_err=False,
                                         deps=[t_pi, t_tid, t_ifd])
                for e in range(NE):
                    for a in range(A):
                        t_idxload = GP.emit(
                            "dma_start", out=idx_sb[:, A * e + a:A * e + a + 1],
                            in_=idx_d[e:e + 1, 128 * a:128 * (a + 1)].rearrange(
                                "e p -> p e"),
                            deps=[t_scat])
                if _DEBUG:
                    GP.emit("dma_start", out=d_idx, in_=idx_d, deps=[t_scat])
                for dmh in range(2):
                    VE.emit("tensor_scalar", idx2_sb[:, NA * dmh:NA * (dmh + 1)],
                            idx_sb[:], 2, dmh, ALU.mult, ALU.add, deps=[t_idxload])
                VE.self_wait()
                t_idx2 = VE.tok()
            barrier()

            if _PHASES <= 4:
                return nc
            if _PHASES <= 5:
                return nc
            with ExitStack() as c2:
                E2 = c2.enter_context
                shact = E2(nc.sbuf_tensor("shact", [128, 12 * TOK], BF16))
                wsh_sb = E2(nc.sbuf_tensor("wsh_sb", [128, 4 * NKT * 128], BF16))
                wsd_sb = E2(nc.sbuf_tensor("wsd_sb", [128, 2 * 12 * 128], BF16))
                silu2 = E2(nc.sbuf_tensor("silu2", [128, TOK], BF16))
                ps_a = E2(nc.psum_tensor("ps_a2", [128, TOK], F32))
                ps_b = E2(nc.psum_tensor("ps_b2", [128, TOK], F32))
                ps_c = E2(nc.psum_tensor("ps_c2", [128, TOK], F32))
                ps_d = E2(nc.psum_tensor("ps_d2", [128, TOK], F32))
                bpa, bpb, bpc, bpd = Buf(), Buf(), Buf(), Buf()
                bwsh = [Buf(), Buf()]
                bwsd = [Buf(), Buf()]
                bsilu, bsh = Buf(), Buf()
                bxt = [Buf() for _ in range(NKT)]
                for m in range(12):
                    pbuf = m % 2
                    woff = pbuf * 2 * NKT * 128
                    t_lg = GP.emit("dma_start",
                                   out=wsh_sb[:, woff:woff + NKT * 128],
                                   in_=sgm[m],
                                   deps=bwsh[pbuf].wdeps())
                    t_lu = GP.emit("dma_start",
                                   out=wsh_sb[:, woff + NKT * 128:woff + 2 * NKT * 128],
                                   in_=sum_m[m])
                    bwsh[pbuf].wr(t_lu)
                    pg, bg = (ps_a, bpa) if m % 2 == 0 else (ps_b, bpb)
                    pu, bu = (ps_c, bpc) if m % 2 == 0 else (ps_d, bpd)
                    for h in range(2):
                        for kt in range(NKT):
                            t_g = TE.emit(
                                "matmul", pg[:, 512 * h:512 * (h + 1)],
                                wsh_sb[:, woff + 128 * kt:woff + 128 * (kt + 1)],
                                ln2o[:, TOK * kt + 512 * h:TOK * kt + 512 * (h + 1)],
                                start=(kt == 0), stop=(kt == NKT - 1),
                                deps=[t_lg] + (bg.wdeps() if kt == 0 and h == 0 else []))
                    bg.wr(t_g)
                    bwsh[pbuf].rd(t_g)
                    for h in range(2):
                        for kt in range(NKT):
                            t_u = TE.emit(
                                "matmul", pu[:, 512 * h:512 * (h + 1)],
                                wsh_sb[:, woff + NKT * 128 + 128 * kt:
                                       woff + NKT * 128 + 128 * (kt + 1)],
                                ln2o[:, TOK * kt + 512 * h:TOK * kt + 512 * (h + 1)],
                                start=(kt == 0), stop=(kt == NKT - 1),
                                deps=[t_lu] + (bu.wdeps() if kt == 0 and h == 0 else []))
                    bu.wr(t_u)
                    bwsh[pbuf].rd(t_u)
                    t_si = SC.emit("activation", silu2[:], pg[:], AF.Silu,
                                   deps=[t_g] + bsilu.wdeps())
                    bg.rd(t_si)
                    bsilu.wr(t_si)
                    a = shact[:, TOK * m:TOK * (m + 1)]
                    t_a1 = VE.emit("tensor_tensor", a, silu2[:], pu[:], ALU.mult,
                                   deps=[t_si, t_u])
                    bu.rd(t_a1)
                    bsilu.rd(t_a1)
                    bsh.wr(t_a1)
                    VE.self_wait()
                for dm in range(NKT):
                    pbuf = dm % 2
                    woff = pbuf * 12 * 128
                    t_lw = GP.emit("dma_start",
                                   out=wsd_sb[:, woff:woff + 12 * 128],
                                   in_=sdm[dm],
                                   deps=bwsd[pbuf].wdeps())
                    bwsd[pbuf].wr(t_lw)
                    pr, br = (ps_a, bpa) if dm % 2 == 0 else (ps_b, bpb)
                    for h in range(2):
                        for f in range(12):
                            t_r = TE.emit(
                                "matmul", pr[:, 512 * h:512 * (h + 1)],
                                wsd_sb[:, woff + 128 * f:woff + 128 * (f + 1)],
                                shact[:, TOK * f + 512 * h:TOK * f + 512 * (h + 1)],
                                start=(f == 0), stop=(f == 11),
                                deps=[t_lw, bsh.w] +
                                     (br.wdeps() if f == 0 and h == 0 else []))
                        br.wr(t_r)
                    bwsd[pbuf].rd(t_r)
                    bsh.rd(t_r)
                    t_add = VE.emit("tensor_add", xT[:, TOK * dm:TOK * (dm + 1)],
                                    xT[:, TOK * dm:TOK * (dm + 1)], pr[:],
                                    deps=[t_r] + bxt[dm].wdeps())
                    br.rd(t_add)
                    bxt[dm].wr(t_add)
                    VE.self_wait()
            barrier()

            cD.close()
            barrier()
            if _PHASES <= 6:
                return nc
            with ExitStack() as c2:
                E2 = c2.enter_context
                xg = [E2(nc.sbuf_tensor(f"xg{i}", [128, NKT * TCAP], BF16))
                      for i in range(3)]
                actT = [E2(nc.sbuf_tensor(f"actT{i}", [128, 6 * TCAP], BF16))
                        for i in range(2)]
                gst = [E2(nc.sbuf_tensor(f"gst{i}", [128, A * D], BF16))
                       for i in range(3)]
                cbg = [E2(nc.sbuf_tensor(f"cbg{i}", [128, A * NE], F32))
                       for i in range(3)]
                wgu_sb = E2(nc.sbuf_tensor("wgu_sb", [128, 4 * NKT * 128], BF16))
                wdd_sb = E2(nc.sbuf_tensor("wdd_sb", [128, 6 * D], BF16))
                silu_sb = E2(nc.sbuf_tensor("silu_sb", [128, TCAP], BF16))
                rout = [E2(nc.sbuf_tensor(f"rout{i}", [128, 1024], F32))
                        for i in range(2)]
                ps_g = [E2(nc.psum_tensor(f"ps_g{i}", [128, TCAP], F32)) for i in range(2)]
                ps_u = [E2(nc.psum_tensor(f"ps_u{i}", [128, TCAP], F32)) for i in range(2)]
                ps_r = [E2(nc.psum_tensor(f"ps_r{i}", [128, 1024], F32)) for i in range(2)]
                bxg = [Buf(), Buf(), Buf()]
                bact = [Buf(), Buf()]
                bgst = [Buf(), Buf(), Buf()]
                bstg = [Buf(), Buf(), Buf()]
                bcbg = [Buf(), Buf(), Buf()]
                bwgu = [Buf(), Buf()]
                bwdd = [Buf(), Buf()]
                bpg = [Buf(), Buf()]
                bpu = [Buf(), Buf()]
                bpr = [Buf(), Buf()]
                brt = [Buf(), Buf()]
                bracc = [Buf(), Buf(), Buf(), Buf()]
                bsilu = Buf()
                for i in range(4):
                    bracc[i].wr(t_rz)
                def prefetch(e):
                    buf = e % 3
                    t_stg = None
                    for a in range(A):
                        ga = gst[buf][:, D * a:D * (a + 1)]
                        t_ga = GP.emit("indirect_dma_start", ga, None, ln2tok_d,
                                       IndirectOffsetOnAxis(
                                           idx_sb[:, A * e + a:A * e + a + 1], 0),
                                       bounds_check=r_btok, oob_is_err=False,
                                       deps=[t_idxload, t_ln2tok] +
                                       (bgst[buf].wdeps() if a == 0 else []))
                        bgst[buf].wr(t_ga)
                        t_stg = GP.emit("dma_start",
                                        out=stage_d[buf, 128 * a:128 * (a + 1), :],
                                        in_=ga, deps=[t_ga] +
                                        (bstg[buf].wdeps() if a == 0 else []))
                        bgst[buf].rd(t_stg)
                    bstg[buf].wr(t_stg)
                    t_cg = None
                    for a in range(A):
                        t_cg = GP.emit("indirect_dma_start",
                                       cbg[buf][:, NE * a:NE * (a + 1)], None,
                                       comb_tok_d,
                                       IndirectOffsetOnAxis(
                                           idx_sb[:, A * e + a:A * e + a + 1], 0),
                                       bounds_check=r_btok, oob_is_err=False,
                                       deps=[t_cstore] + (bcbg[buf].wdeps() if a == 0 else []))
                    bcbg[buf].wr(t_cg)
                    t_xg = None
                    for kt in range(NKT):
                        t_xg = SY.emit("dma_start",
                                       out=xg[buf][:, TCAP * kt:TCAP * (kt + 1)],
                                       in_=stage_d[buf, :, 128 * kt:128 * (kt + 1)],
                                       transpose=True, inc=16,
                                       deps=[t_stg] + (bxg[buf].wdeps() if kt == 0 else []))
                    t_xg_sy = SY.tok()
                    bxg[buf].wr(t_xg_sy)
                    bstg[buf].rd(t_xg_sy)
                    return t_xg_sy, t_cg

                pf = {0: prefetch(0), 1: prefetch(1)}
                for e in range(NE):
                    buf = e % 3
                    abuf = e % 2
                    if e + 2 < NE:
                        pf[e + 2] = prefetch(e + 2)
                    t_xg_sy, t_cg = pf.pop(e)
                    if _DEBUG and e == 0:
                        GP.emit("dma_start", out=d_xg, in_=xg[0][:],
                                deps=[t_xg_sy])
                    for m in range(6):
                        pb = m % 2
                        woff = pb * 2 * NKT * 128
                        t_lg = GP.emit("dma_start",
                                       out=wgu_sb[:, woff:woff + NKT * 128],
                                       in_=wgm[e, m],
                                       deps=bwgu[pb].wdeps())
                        t_lu = GP.emit("dma_start",
                                       out=wgu_sb[:, woff + NKT * 128:woff + 2 * NKT * 128],
                                       in_=wum[e, m])
                        bwgu[pb].wr(t_lu)
                        for kt in range(NKT):
                            t_g = TE.emit(
                                "matmul", ps_g[pb][:],
                                wgu_sb[:, woff + 128 * kt:woff + 128 * (kt + 1)],
                                xg[buf][:, TCAP * kt:TCAP * (kt + 1)],
                                start=(kt == 0), stop=(kt == NKT - 1),
                                deps=[t_lg, t_xg_sy] +
                                     (bpg[pb].wdeps() if kt == 0 else []))
                        bpg[pb].wr(t_g)
                        bwgu[pb].rd(t_g)
                        bxg[buf].rd(t_g)
                        for kt in range(NKT):
                            t_u = TE.emit(
                                "matmul", ps_u[pb][:],
                                wgu_sb[:, woff + NKT * 128 + 128 * kt:
                                       woff + NKT * 128 + 128 * (kt + 1)],
                                xg[buf][:, TCAP * kt:TCAP * (kt + 1)],
                                start=(kt == 0), stop=(kt == NKT - 1),
                                deps=[t_lu] + (bpu[pb].wdeps() if kt == 0 else []))
                        bpu[pb].wr(t_u)
                        bwgu[pb].rd(t_u)
                        bxg[buf].rd(t_u)
                        t_si = SC.emit("activation", silu_sb[:], ps_g[pb][:], AF.Silu,
                                       deps=[t_g] + bsilu.wdeps())
                        bpg[pb].rd(t_si)
                        bsilu.wr(t_si)
                        a_ = actT[abuf][:, TCAP * m:TCAP * (m + 1)]
                        t_a1 = VE.emit("tensor_tensor", a_, silu_sb[:], ps_u[pb][:],
                                       ALU.mult,
                                       deps=[t_si, t_u] + (bact[abuf].wdeps() if m == 0 else []))
                        bpu[pb].rd(t_a1)
                        bsilu.rd(t_a1)
                        VE.self_wait()
                    bact[abuf].wr(VE.tok())
                    t_lw = None
                    for m in range(6):
                        t_lw = GP.emit("dma_start",
                                       out=wdd_sb[:, D * m:D * (m + 1)],
                                       in_=wdp[e, 128 * m:128 * (m + 1), :],
                                       deps=(bwdd[0].wdeps() if m == 0 else []))
                    bwdd[0].wr(t_lw)
                    for a in range(A):
                        for dmh in range(2):
                            pb = (a * 2 + dmh) % 2
                            pr = ps_r[pb]
                            first = True
                            for m in range(6):
                                for ns in range(2):
                                    t_r = TE.emit(
                                        "matmul", pr[:, 512 * ns:512 * (ns + 1)],
                                        actT[abuf][:, TCAP * m + 128 * a:
                                                   TCAP * m + 128 * (a + 1)],
                                        wdd_sb[:, D * m + 1024 * dmh + 512 * ns:
                                               D * m + 1024 * dmh + 512 * (ns + 1)],
                                        start=(m == 0), stop=(m == 5),
                                        skip_group_check=True,
                                        deps=[t_lw, bact[abuf].w] +
                                             (bpr[pb].wdeps() if first else []))
                                    first = False
                            bpr[pb].wr(t_r)
                            bwdd[0].rd(t_r)
                            bact[abuf].rd(t_r)
                            rb = (a * 2 + dmh) % 2
                            t_ev = VE.emit("tensor_scalar", rout[rb][:], pr[:],
                                           cbg[buf][:, NE * a + e:NE * a + e + 1],
                                           None, ALU.mult,
                                           deps=[t_r, t_cg] + brt[rb].wdeps())
                            bpr[pb].rd(t_ev)
                            bcbg[buf].rd(t_ev)
                            brt[rb].wr(t_ev)
                            VE.self_wait()
                            acc = e % 4
                            t_sc2 = GP.emit(
                                "indirect_dma_start",
                                racc[acc].rearrange("t (h c) -> (t h) c", h=2),
                                IndirectOffsetOnAxis(
                                    idx2_sb[:, NA * dmh + A * e + a:
                                            NA * dmh + A * e + a + 1], 0),
                                rout[rb][:], None,
                                bounds_check=r_b2tok, oob_is_err=False,
                                compute_op=ALU.add,
                                deps=[t_ev, t_idx2] + bracc[acc].wdeps())
                            bracc[acc].wr(t_sc2)
                            brt[rb].rd(t_sc2)
            barrier()
        barrier()

        if _PHASES <= 6:
            return nc
        with ExitStack() as c2:
            E2 = c2.enter_context
            out_sb = E2(nc.sbuf_tensor("out_sb", [128, D], F32))
            rc_sb = [E2(nc.sbuf_tensor(f"rc_sb{i}", [128, D], F32)) for i in range(4)]
            ps_t = [E2(nc.psum_tensor(f"ps_to{i}", [128, 128], F32)) for i in range(2)]
            bt = [Buf(), Buf()]
            bout = Buf()
            brc = [Buf() for _ in range(4)]
            for tt in range(8):
                t_rl = []
                for i in range(4):
                    t_rl.append(SY.emit("dma_start", out=rc_sb[i][:],
                                        in_=racc[i][128 * tt:128 * (tt + 1), :],
                                        deps=brc[i].wdeps()))
                for kt in range(NKT):
                    p = kt % 2
                    TE.self_wait()
                    t_tr = TE.emit("transpose", ps_t[p][:],
                                   xT[:, TOK * kt + 128 * tt:TOK * kt + 128 * (tt + 1)],
                                   id_sb[:], deps=bt[p].wdeps())
                    bt[p].wr(t_tr)
                    t_cp = VE.emit("tensor_copy", out_sb[:, 128 * kt:128 * (kt + 1)],
                                   ps_t[p][:],
                                   deps=[t_tr] + (bout.wdeps() if kt == 0 else []))
                    bt[p].rd(t_cp)
                    VE.self_wait()
                for i in range(4):
                    VE.emit("tensor_add", out_sb[:], out_sb[:], rc_sb[i][:],
                            deps=[t_rl[i]])
                    VE.self_wait()
                    brc[i].rd(VE.tok())
                t_all = VE.tok()
                t_st = GP.emit("dma_start", out=y[128 * tt:128 * (tt + 1), :],
                               in_=out_sb[:], deps=[t_all])
                bout.wr(t_st)
            GP.self_wait()
        barrier()
    return nc


def _pad_heads(w, hg):
    out = np.zeros((w.shape[0], 128), np.float32)
    for j in range(HPG):
        out[:, 32 * j:32 * j + 8] = w[:, 8 * (HPG * hg + j):8 * (HPG * hg + j) + 8]
    return out


def kernel(L0_attn_hidden_states, L0_attn_Wq, L0_attn_Wk, L0_attn_Wv, L0_attn_Wo,
           L0_attn_Wg, L0_attn_v_emb0, L0_attn_v_emb1, L0_ffn_gate_w, L0_ffn_wg,
           L0_ffn_wu, L0_ffn_wd, L0_ffn_sg, L0_ffn_su, L0_ffn_sd):
    f32 = np.float32
    bf = ml_dtypes.bfloat16
    x = np.asarray(L0_attn_hidden_states, f32)[0]
    Wq = np.asarray(L0_attn_Wq, f32)
    Wk = np.asarray(L0_attn_Wk, f32)
    Wv = np.asarray(L0_attn_Wv, f32)
    Wo = np.asarray(L0_attn_Wo, f32)
    Wg = np.asarray(L0_attn_Wg, f32)
    ve0 = np.asarray(L0_attn_v_emb0, f32)
    ve1 = np.asarray(L0_attn_v_emb1, f32)
    gate_w = np.asarray(L0_ffn_gate_w, f32)
    wg = np.asarray(L0_ffn_wg, f32)
    wu = np.asarray(L0_ffn_wu, f32)
    wd = np.asarray(L0_ffn_wd, f32)
    sg = np.asarray(L0_ffn_sg, f32)
    su = np.asarray(L0_ffn_su, f32)
    sd = np.asarray(L0_ffn_sd, f32)

    def _pmaj(w, nco):
        return np.ascontiguousarray(
            w.reshape(NKT, 128, nco, w.shape[1] // nco).transpose(2, 1, 0, 3)
            .reshape(nco, 128, NKT * (w.shape[1] // nco)))

    xT = np.ascontiguousarray(x.T)
    wopad = np.zeros((16, 128, 8 * 128), f32)
    for g in range(8):
        for j in range(HPG):
            src = Wo[8 * (HPG * g + j):8 * (HPG * g + j) + 8]
            for dm in range(16):
                wopad[dm, 32 * j:32 * j + 8, 128 * g:128 * (g + 1)] = \
                    src[:, 128 * dm:128 * (dm + 1)]
    wopad = wopad.astype(bf)
    wgate = _pmaj(Wg, 16).astype(bf)
    wgm = np.ascontiguousarray(
        wg.reshape(NE, NKT, 128, 6, 128).transpose(0, 3, 2, 1, 4)
        .reshape(NE, 6, 128, NKT * 128)).astype(bf)
    wum = np.ascontiguousarray(
        wu.reshape(NE, NKT, 128, 6, 128).transpose(0, 3, 2, 1, 4)
        .reshape(NE, 6, 128, NKT * 128)).astype(bf)
    wd = wd.astype(bf)
    tokid_h = (np.arange(128)[:, None] + 128 * np.arange(8)[None, :]).astype(np.int32)
    eoff_h = (TCAP * np.arange(NE)[:, None]).astype(np.float32)
    sgm = _pmaj(sg, 12).astype(bf)
    sum_m = _pmaj(su, 12).astype(bf)
    sdm = np.ascontiguousarray(
        sd.reshape(12, 128, 16, 128).transpose(2, 1, 0, 3)
        .reshape(16, 128, 12 * 128)).astype(bf)
    gate_w = np.ascontiguousarray(
        gate_w.reshape(NKT, 128, NE).transpose(1, 0, 2).reshape(128, NKT * NE))
    ident = np.eye(128, dtype=f32)
    ones = np.ones((128, 128), f32)
    sel = np.zeros((128, 128), f32)
    bdm = np.zeros((128, HPG), f32)
    padmask = np.zeros((128, 1), f32)
    for j in range(HPG):
        sel[32 * j + 8, 32 * j:32 * j + 32] = 1.0
        bdm[32 * j:32 * j + 8, j] = 1.0
        padmask[32 * j:32 * j + 8, 0] = 1.0
    cbsel = np.zeros((NE, NE * 128), f32)
    for e in range(NE):
        cbsel[e, 128 * e:128 * (e + 1)] = 1.0
    masks = np.zeros((128, 4 * HPG * QW), f32)
    qr = np.arange(QW)
    kr = np.arange(128)[:, None]
    for p in range(4):
        m = (128 * p + kr <= qr[None, :]).astype(f32)
        for h in range(HPG):
            masks[:, p * HPG * QW + QW * h:p * HPG * QW + QW * (h + 1)] = m

    shared = {
        "wopad": wopad, "wgate": wgate, "gate_w": gate_w, "wgm": wgm, "wum": wum,
        "wdp": wd, "sgm": sgm, "sum_m": sum_m, "sdm": sdm, "ident": ident,
        "tokid": tokid_h, "eoff": eoff_h,
        "ones": ones, "sel": sel, "bd": bdm.astype(bf), "padmask": padmask,
        "cbsel": cbsel.astype(bf), "masks": masks.astype(bf),
    }
    in_maps = []
    for c in range(NCORES):
        ve0p = np.zeros((128, 1), f32)
        ve1p = np.zeros((128, 1), f32)
        for j in range(HPG):
            ve0p[32 * j:32 * j + 8, 0] = ve0[8 * (HPG * c + j):8 * (HPG * c + j) + 8]
            ve1p[32 * j:32 * j + 8, 0] = ve1[8 * (HPG * c + j):8 * (HPG * c + j) + 8]
        mselc = np.zeros((128, NCORES), f32)
        mselc[:, c] = 1.0
        m = dict(shared)
        xc = xT[:, TOK * c:TOK * (c + 1)]
        m.update({
            "xT_own": np.ascontiguousarray(
                xc.reshape(NKT, 128, TOK).transpose(1, 0, 2).reshape(128, NKT * TOK)),
            "wqp": np.ascontiguousarray(
                _pad_heads(Wq, c).reshape(NKT, 128, 128).transpose(1, 0, 2)
                .reshape(128, NKT * 128)).astype(bf),
            "wkp": np.ascontiguousarray(
                _pad_heads(Wk, c).reshape(NKT, 128, 128).transpose(1, 0, 2)
                .reshape(128, NKT * 128)).astype(bf),
            "wv": np.ascontiguousarray(
                Wv[:, 32 * c:32 * (c + 1)].reshape(NKT, 128, 32).transpose(1, 0, 2)
                .reshape(128, NKT * 32)).astype(bf),
            "vemb0p": ve0p, "vemb1p": ve1p, "msel": mselc,
        })
        in_maps.append(m)

    nc = build_program()
    res = run_bass_kernel_spmd(nc, in_maps, core_ids=list(range(NCORES)))
    out = np.concatenate([res.results[c]["y"] for c in range(NCORES)], axis=0)
    return out.reshape(1, S, D).astype(np.float32)


# version 11 (final)
# speedup vs baseline: 1.3427x; 1.0073x over previous
import numpy as np
import ml_dtypes
import concourse.bass as bass
import concourse.mybir as mybir
from concourse.bass_utils import run_bass_kernel_spmd
from concourse.bass import IndirectOffsetOnAxis
from contextlib import ExitStack
import os

_PHASES = int(os.environ.get("KPHASES", "99"))
_DEBUG = int(os.environ.get("KDEBUG", "0"))

F32 = mybir.dt.float32
BF16 = mybir.dt.bfloat16
AF = mybir.ActivationFunctionType
ALU = mybir.AluOpType
AX = mybir.AxisListType

NCORES = 8
S = 8192
D = 2048
NKT = D // 128
TOK = S // NCORES
HPG = 4
NE, TOPK, FE, FSH = 32, 6, 768, 1536
EPS = 1e-5
QW = 512
NQC = 16
SKT = S // 128
TCAP = 256
A = TCAP // 128
NA = NE * A


class Q:

    def __init__(self, eng, name, sem, inc):
        self.eng, self.name, self.sem, self.inc = eng, name, sem, inc
        self.n = 0
        self.waited = {}

    def wait(self, token):
        if token is None:
            return
        name, cnt, sem = token
        if self.waited.get(name, -1) >= cnt:
            return
        self.eng.wait_ge(sem, cnt)
        self.waited[name] = cnt

    def self_wait(self):
        self.wait((self.name, self.n, self.sem))

    def emit(self, method, *args, deps=(), inc=None, **kw):
        for d in deps:
            self.wait(d)
        instr = getattr(self.eng, method)(*args, **kw)
        step = self.inc if inc is None else inc
        instr.then_inc(self.sem, step)
        self.n += step
        return (self.name, self.n, self.sem)

    def tok(self):
        return (self.name, self.n, self.sem)


class Buf:

    def __init__(self):
        self.w = None
        self.r = []

    def wdeps(self):
        return ([self.w] if self.w else []) + self.r

    def rdeps(self):
        return [self.w] if self.w else []

    def wr(self, tok):
        self.w = tok
        self.r = []

    def rd(self, tok):
        self.r.append(tok)


def build_program():
    nc = bass.Bass("TRN2", target_bir_lowering=False)
    dt = nc.dram_tensor
    xT_own = dt("xT_own", [128, NKT * TOK], F32, kind="ExternalInput").ap()
    wqp = dt("wqp", [128, NKT * 128], BF16, kind="ExternalInput").ap()
    wkp = dt("wkp", [128, NKT * 128], BF16, kind="ExternalInput").ap()
    wv = dt("wv", [128, NKT * 32], BF16, kind="ExternalInput").ap()
    vemb0p = dt("vemb0p", [128, 1], F32, kind="ExternalInput").ap()
    vemb1p = dt("vemb1p", [128, 1], F32, kind="ExternalInput").ap()
    msel = dt("msel", [128, NCORES], F32, kind="ExternalInput").ap()
    wopad = dt("wopad", [16, 128, 8 * 128], BF16, kind="ExternalInput").ap()
    wgate = dt("wgate", [16, 128, NKT * 128], BF16, kind="ExternalInput").ap()
    gate_w = dt("gate_w", [128, NKT * NE], F32, kind="ExternalInput").ap()
    wgm = dt("wgm", [NE, 6, 128, NKT * 128], BF16, kind="ExternalInput").ap()
    wum = dt("wum", [NE, 6, 128, NKT * 128], BF16, kind="ExternalInput").ap()
    wdp = dt("wdp", [NE, FE, D], BF16, kind="ExternalInput").ap()
    sgm = dt("sgm", [12, 128, NKT * 128], BF16, kind="ExternalInput").ap()
    sum_m = dt("sum_m", [12, 128, NKT * 128], BF16, kind="ExternalInput").ap()
    sdm = dt("sdm", [16, 128, 12 * 128], BF16, kind="ExternalInput").ap()
    ident = dt("ident", [128, 128], F32, kind="ExternalInput").ap()
    ones = dt("ones", [128, 128], F32, kind="ExternalInput").ap()
    sel = dt("sel", [128, 128], F32, kind="ExternalInput").ap()
    bd = dt("bd", [128, HPG], BF16, kind="ExternalInput").ap()
    padmask = dt("padmask", [128, 1], F32, kind="ExternalInput").ap()
    cbsel = dt("cbsel", [NE, NE * 128], BF16, kind="ExternalInput").ap()
    masks = dt("masks", [128, 4 * HPG * QW], BF16, kind="ExternalInput").ap()
    tokid = dt("tokid", [128, 8], mybir.dt.int32, kind="ExternalInput").ap()
    eoff = dt("eoff", [NE, 1], F32, kind="ExternalInput").ap()
    ln1_own_d = dt("ln1_own_d", [D, TOK], BF16, kind="Internal").ap()
    ln1_all = dt("ln1_all", [NCORES * D, TOK], BF16, kind="Internal",
                 addr_space="Shared").ap()
    attn_own_d = dt("attn_own_d", [128, S], BF16, kind="Internal").ap()
    attn_all = dt("attn_all", [NCORES * 128, S], BF16, kind="Internal",
                  addr_space="Shared").ap()
    ln2tok_d = dt("ln2tok_d", [TOK, D], BF16, kind="Internal").ap()
    comb_tok_d = dt("comb_tok_d", [TOK, NE], F32, kind="Internal").ap()
    idx_d = dt("idx_d", [NE, TCAP], mybir.dt.int32, kind="Internal").ap()
    stage_d = dt("stage_d", [3, TCAP, D], BF16, kind="Internal").ap()
    racc = [dt(f"racc{i}", [TOK, D], F32, kind="Internal").ap() for i in range(4)]
    y = dt("y", [TOK, D], F32, kind="ExternalOutput").ap()
    if _DEBUG:
        d_ln1 = dt("d_ln1", [128, NKT * TOK], F32, kind="ExternalOutput").ap()
        d_attn = dt("d_attn", [128, S], F32, kind="ExternalOutput").ap()
        d_xmid = dt("d_xmid", [128, NKT * TOK], F32, kind="ExternalOutput").ap()
        d_ln2 = dt("d_ln2", [128, NKT * TOK], F32, kind="ExternalOutput").ap()
        d_comb = dt("d_comb", [NE, TOK], F32, kind="ExternalOutput").ap()
        d_atn = dt("d_atn", [128, 8 * TOK], F32, kind="ExternalOutput").ap()
        d_idx = dt("d_idx", [NE, TCAP], mybir.dt.int32, kind="ExternalOutput").ap()
        d_xg = dt("d_xg", [128, NKT * TCAP], F32, kind="ExternalOutput").ap()

    with ExitStack() as ctx:
        EE = ctx.enter_context
        TE = Q(nc.tensor, "te", EE(nc.semaphore("s_te")), 1)
        SC = Q(nc.scalar, "sc", EE(nc.semaphore("s_sc")), 1)
        VE = Q(nc.vector, "ve", EE(nc.semaphore("s_ve")), 1)
        SY = Q(nc.sync, "sy", EE(nc.semaphore("s_sy")), 16)
        GP = Q(nc.gpsimd, "gp", EE(nc.semaphore("s_gp")), 16)
        sem_ag1 = EE(nc.semaphore("s_ag1"))
        sem_ag2 = EE(nc.semaphore("s_ag2"))
        r_btcap = nc.gpsimd.to_reg(NE * TCAP)
        r_btok = nc.gpsimd.to_reg(TOK)
        r_b2tok = nc.gpsimd.to_reg(2 * TOK)
        ALL = (TE, SC, VE, SY, GP)

        def barrier():
            toks = [q.tok() for q in ALL]
            for q in ALL:
                for t in toks:
                    q.wait(t)

        xT = EE(nc.sbuf_tensor("xT", [128, NKT * TOK], F32))
        id_sb = EE(nc.sbuf_tensor("id_sb", [128, 128], F32))
        on_sb = EE(nc.sbuf_tensor("on_sb", [128, 128], F32))
        sel_sb = EE(nc.sbuf_tensor("sel_sb", [128, 128], F32))
        bd_sb = EE(nc.sbuf_tensor("bd_sb", [128, HPG], BF16))
        pm_sb = EE(nc.sbuf_tensor("pm_sb", [128, 1], F32))
        ve0_sb = EE(nc.sbuf_tensor("ve0_sb", [128, 1], F32))
        ve1_sb = EE(nc.sbuf_tensor("ve1_sb", [128, 1], F32))
        vd_sb = EE(nc.sbuf_tensor("vd_sb", [128, 1], F32))
        ms_sb = EE(nc.sbuf_tensor("ms_sb", [128, NCORES], F32))

        t_x = SY.emit("dma_start", out=xT[:], in_=xT_own)
        SY.emit("dma_start", out=id_sb[:], in_=ident)
        SY.emit("dma_start", out=on_sb[:], in_=ones)
        SY.emit("dma_start", out=sel_sb[:], in_=sel)
        SY.emit("dma_start", out=bd_sb[:], in_=bd)
        SY.emit("dma_start", out=pm_sb[:], in_=padmask)
        SY.emit("dma_start", out=ve0_sb[:], in_=vemb0p)
        t_ve = SY.emit("dma_start", out=ve1_sb[:], in_=vemb1p)
        t_msel = SY.emit("dma_start", out=ms_sb[:], in_=msel)
        t_vd = VE.emit("tensor_sub", vd_sb[:], ve1_sb[:], ve0_sb[:], deps=[t_ve])
        t_const = SY.tok()

        def layernorm(dst_bf16, fp32_consumer=None, name=""):
            with ExitStack() as c2:
                E2 = c2.enter_context
                sq = E2(nc.sbuf_tensor(f"sq{name}", [128, 2 * TOK], F32))
                stat = E2(nc.sbuf_tensor(f"stat{name}", [1, 4 * TOK], F32))
                lnf = E2(nc.sbuf_tensor(f"lnf{name}", [128, 2 * TOK], F32))
                mb_sb = E2(nc.sbuf_tensor(f"mb{name}", [128, TOK], F32))
                rb_sb = E2(nc.sbuf_tensor(f"rb{name}", [128, TOK], F32))
                with ExitStack() as c3:
                    E3 = c3.enter_context
                    ps_sum = E3(nc.psum_tensor(f"pssum{name}", [1, TOK], F32))
                    ps_ssq = E3(nc.psum_tensor(f"psssq{name}", [1, TOK], F32))
                    t_sq = [None, None]
                    t_ssqmm = None
                    for kt in range(NKT):
                        xkt = xT[:, TOK * kt:TOK * (kt + 1)]
                        sqb = sq[:, TOK * (kt % 2):TOK * (kt % 2 + 1)]
                        t_sq[kt % 2] = SC.emit("activation", sqb, xkt, AF.Square,
                                               deps=[t_x] + ([t_ssqmm] if kt >= 2 else []))
                        for h in range(2):
                            TE.emit("matmul", ps_sum[:, 512 * h:512 * (h + 1)],
                                    on_sb[:, 0:1], xkt[:, 512 * h:512 * (h + 1)],
                                    start=(kt == 0), stop=(kt == NKT - 1),
                                    deps=[t_x, t_const])
                        for h in range(2):
                            t_ssqmm = TE.emit("matmul", ps_ssq[:, 512 * h:512 * (h + 1)],
                                              on_sb[:, 0:1], sqb[:, 512 * h:512 * (h + 1)],
                                              start=(kt == 0), stop=(kt == NKT - 1),
                                              deps=[t_sq[kt % 2]])
                    t_summ = TE.tok()
                    VE.emit("tensor_scalar_mul", stat[:, 0:TOK], ps_sum[:], 1.0 / D,
                            deps=[t_summ])
                    VE.self_wait()
                    VE.emit("tensor_tensor", stat[:, TOK:2 * TOK], stat[:, 0:TOK],
                            stat[:, 0:TOK], ALU.mult)
                    VE.emit("tensor_scalar_mul", stat[:, 2 * TOK:3 * TOK], ps_ssq[:],
                            1.0 / D, deps=[t_summ])
                    VE.self_wait()
                    VE.emit("tensor_sub", stat[:, 2 * TOK:3 * TOK],
                            stat[:, 2 * TOK:3 * TOK], stat[:, TOK:2 * TOK])
                    VE.self_wait()
                    t_var = VE.emit("tensor_scalar_add", stat[:, 2 * TOK:3 * TOK],
                                    stat[:, 2 * TOK:3 * TOK], EPS)
                    VE.self_wait()
                    t_ln = SC.emit("activation", stat[:, 3 * TOK:4 * TOK],
                                   stat[:, 2 * TOK:3 * TOK], AF.Ln, deps=[t_var])
                    t_half = VE.emit("tensor_scalar_mul", stat[:, 3 * TOK:4 * TOK],
                                     stat[:, 3 * TOK:4 * TOK], -0.5, deps=[t_ln])
                    VE.self_wait()
                    t_rs = SC.emit("activation", stat[:, 3 * TOK:4 * TOK],
                                   stat[:, 3 * TOK:4 * TOK], AF.Exp, deps=[t_half])
                with ExitStack() as c3:
                    E3 = c3.enter_context
                    ps_mb = E3(nc.psum_tensor(f"psmb{name}", [128, TOK], F32))
                    ps_rb = E3(nc.psum_tensor(f"psrb{name}", [128, TOK], F32))
                    TE.self_wait()
                    for h in range(2):
                        TE.emit("matmul", ps_mb[:, 512 * h:512 * (h + 1)], on_sb[0:1, :],
                                stat[0:1, 512 * h:512 * (h + 1)], start=True, stop=True,
                                deps=[t_rs])
                    for h in range(2):
                        t_bc = TE.emit("matmul", ps_rb[:, 512 * h:512 * (h + 1)],
                                       on_sb[0:1, :],
                                       stat[0:1, 3 * TOK + 512 * h:3 * TOK + 512 * (h + 1)],
                                       start=True, stop=True)
                    VE.emit("tensor_copy", mb_sb[:], ps_mb[:], deps=[t_bc])
                    t_rbc = VE.emit("tensor_copy", rb_sb[:], ps_rb[:])
                    VE.self_wait()
                prev = [None, None]
                for kt in range(NKT):
                    xkt = xT[:, TOK * kt:TOK * (kt + 1)]
                    lbuf = lnf[:, TOK * (kt % 2):TOK * (kt % 2 + 1)]
                    dep_reuse = list(prev[kt % 2] or ())
                    VE.emit("tensor_sub", lbuf, xkt, mb_sb[:],
                            deps=[t_rbc] + [d for d in dep_reuse if d])
                    VE.self_wait()
                    t_n = VE.emit("tensor_tensor", lbuf, lbuf, rb_sb[:], ALU.mult)
                    VE.self_wait()
                    t_cast = SC.emit("activation", dst_bf16[:, TOK * kt:TOK * (kt + 1)],
                                     lbuf, AF.Copy, deps=[t_n])
                    t_cons = fp32_consumer(kt, lbuf, t_n) if fp32_consumer else None
                    prev[kt % 2] = (t_cast, t_cons)
                barrier()

        with ExitStack() as cA:
            EA = cA.enter_context
            ln1o = EA(nc.sbuf_tensor("ln1o", [128, NKT * TOK], BF16))

            layernorm(ln1o, name="l1")
            t_st1 = GP.emit("dma_start",
                            out=ln1_own_d.rearrange("(k p) t -> p k t", p=128),
                            in_=ln1o.rearrange("p (k t) -> p k t", k=NKT))
            GP.wait(t_st1)
            nc.gpsimd.collective_compute("AllGather", ALU.bypass,
                                         [list(range(NCORES))], [ln1_own_d],
                                         [ln1_all]).then_inc(sem_ag1, 1)
            t_ag1 = ("ag1", 1, sem_ag1)
            if _DEBUG:
                GP.emit("dma_start", out=d_ln1, in_=ln1o[:], deps=[t_st1])

            with ExitStack() as cB:
                EB = cB.enter_context
                msk_sb = EB(nc.sbuf_tensor("msk_sb", [128, 4 * HPG * QW], BF16))
                pq_pad = EB(nc.sbuf_tensor("pq_pad", [128, S], BF16))
                pk_pad = EB(nc.sbuf_tensor("pk_pad", [128, S], BF16))
                pv_pad = EB(nc.sbuf_tensor("pv_pad", [128, SKT * HPG * 9], BF16))
                t_mskl = SY.emit("dma_start", out=msk_sb[:], in_=masks)
                zero_sb = EB(nc.sbuf_tensor("zero_sb", [128, D], F32))
                t_zz0 = VE.emit("memset", zero_sb[:], 0.0)
                t_rz = None
                for i in range(4):
                    for tt in range(8):
                        t_rz = GP.emit("dma_start",
                                       out=racc[i][128 * tt:128 * (tt + 1), :],
                                       in_=zero_sb[:], deps=[t_zz0])

                if _PHASES <= 0:
                    return nc
                with ExitStack() as c2:
                    E2 = c2.enter_context
                    lns = E2(nc.sbuf_tensor("lns", [128, NKT * 512], BF16))
                    wq_sb = E2(nc.sbuf_tensor("wq_sb", [128, NKT * 128], BF16))
                    wk_sb = E2(nc.sbuf_tensor("wk_sb", [128, NKT * 128], BF16))
                    wv_sb = E2(nc.sbuf_tensor("wv_sb", [128, NKT * 32], BF16))
                    prq = E2(nc.sbuf_tensor("prq", [128, 512], BF16))
                    esk = E2(nc.sbuf_tensor("esk", [128, HPG], F32))
                    ps_q = [E2(nc.psum_tensor(f"ps_q{i}", [128, 512], F32)) for i in range(2)]
                    ps_k = [E2(nc.psum_tensor(f"ps_k{i}", [128, 512], F32)) for i in range(2)]
                    ps_v = [E2(nc.psum_tensor(f"ps_v{i}", [128, 32], F32)) for i in range(2)]
                    ps_sk = E2(nc.psum_tensor("ps_sk", [128, HPG], F32))
                    bq = [Buf(), Buf()]
                    bk = [Buf(), Buf()]
                    bv = [Buf(), Buf()]
                    bsk, blns, bprq, besk = Buf(), Buf(), Buf(), Buf()

                    t_wq = GP.emit("dma_start", out=wq_sb[:], in_=wqp)
                    t_wk = GP.emit("dma_start", out=wk_sb[:], in_=wkp)
                    t_wv = GP.emit("dma_start", out=wv_sb[:], in_=wv)
                    t_pv1 = VE.emit(
                        "memset",
                        pv_pad.rearrange("p (k h n) -> p k h n", k=SKT, h=HPG)[:, :, :, 8:9],
                        1.0)
                    ln1v = ln1_all.rearrange("(r k p) t -> r k p t", r=NCORES, k=NKT)
                    for ch in range(16):
                        r, s0 = ch // 2, 512 * (ch % 2)
                        t_ld = SY.emit("dma_start",
                                       out=lns.rearrange("p (k t) -> p k t", k=NKT),
                                       in_=ln1v[r, :, :, s0:s0 + 512].rearrange(
                                           "k p t -> p k t"),
                                       deps=[t_ag1] + blns.wdeps())
                        blns.wr(t_ld)
                        pq_, bq_ = ps_q[ch % 2], bq[ch % 2]
                        for kt in range(NKT):
                            t_mm = TE.emit("matmul", pq_[:],
                                           wq_sb[:, 128 * kt:128 * (kt + 1)],
                                           lns[:, 512 * kt:512 * (kt + 1)],
                                           start=(kt == 0), stop=(kt == NKT - 1),
                                           deps=[t_ld, t_wq] + (bq_.wdeps() if kt == 0 else []))
                        bq_.wr(t_mm)
                        blns.rd(t_mm)
                        t_sg = SC.emit("activation", prq[:], pq_[:], AF.Sigmoid,
                                       deps=[t_mm] + bprq.wdeps())
                        bq_.rd(t_sg)
                        bprq.wr(t_sg)
                        t_pq = VE.emit("tensor_scalar_mul",
                                       pq_pad[:, 512 * ch:512 * (ch + 1)], prq[:],
                                       pm_sb[:, 0:1], deps=[t_sg, t_const])
                        bprq.rd(t_pq)
                        pk_, bk_ = ps_k[ch % 2], bk[ch % 2]
                        for kt in range(NKT):
                            t_mm = TE.emit("matmul", pk_[:],
                                           wk_sb[:, 128 * kt:128 * (kt + 1)],
                                           lns[:, 512 * kt:512 * (kt + 1)],
                                           start=(kt == 0), stop=(kt == NKT - 1),
                                           deps=[t_wk] + (bk_.wdeps() if kt == 0 else []))
                        bk_.wr(t_mm)
                        blns.rd(t_mm)
                        t_pk = SC.emit("activation", pk_pad[:, 512 * ch:512 * (ch + 1)],
                                       pk_[:], AF.Sigmoid, deps=[t_mm])
                        bk_.rd(t_pk)
                        for tt in range(4):
                            kk = 4 * ch + tt
                            pv_, bv_ = ps_v[tt % 2], bv[tt % 2]
                            for kt in range(NKT):
                                t_mm = TE.emit(
                                    "matmul", pv_[:],
                                    lns[:, 512 * kt + 128 * tt:512 * kt + 128 * (tt + 1)],
                                    wv_sb[:, 32 * kt:32 * (kt + 1)],
                                    start=(kt == 0), stop=(kt == NKT - 1),
                                    deps=[t_wv] + (bv_.wdeps() if kt == 0 else []))
                            bv_.wr(t_mm)
                            blns.rd(t_mm)
                            pvv = pv_pad.rearrange("p (k h n) -> p k h n", k=SKT, h=HPG)
                            t_ev = SC.emit("activation", pvv[:, kk, :, 0:8],
                                           pv_.ap().rearrange("p (h n) -> p h n", h=HPG),
                                           AF.Sigmoid, deps=[t_mm, t_pv1])
                            bv_.rd(t_ev)
                            TE.self_wait()
                            t_skm = TE.emit("matmul", ps_sk[:],
                                            pk_pad[:, 128 * kk:128 * (kk + 1)], bd_sb[:],
                                            start=True, stop=True,
                                            deps=[t_pk] + bsk.wdeps())
                            bsk.wr(t_skm)
                            t_esk = SC.emit("activation", esk[:], ps_sk[:], AF.Exp,
                                            scale=-1.0, deps=[t_skm] + besk.wdeps())
                            bsk.rd(t_esk)
                            besk.wr(t_esk)
                            for h in range(HPG):
                                pvh = pvv[:, kk, h, 0:9]
                                t_sc9 = VE.emit("tensor_scalar_mul", pvh, pvh,
                                                esk[:, h:h + 1], deps=[t_esk, t_ev])
                                besk.rd(t_sc9)
                            VE.self_wait()
                barrier()

                if _PHASES <= 1:
                    return nc
                with ExitStack() as c2:
                    E2 = c2.enter_context
                    ps_dot = E2(nc.psum_tensor("ps_dot", [128, HPG * QW], F32))
                    ps_nd = E2(nc.psum_tensor("ps_nd", [128, QW], F32))
                    E_sb = E2(nc.sbuf_tensor("E_sb", [128, 2 * HPG * QW], BF16))
                    num_sb = E2(nc.sbuf_tensor("num_sb", [128, QW], F32))
                    rec_sb = E2(nc.sbuf_tensor("rec_sb", [128, QW], F32))
                    att_sb = E2(nc.sbuf_tensor("att_sb", [128, QW], F32))
                    atb_sb = E2(nc.sbuf_tensor("atb_sb", [128, QW], BF16))
                    bdh = [Buf(), Buf()]
                    bE = [Buf(), Buf()]
                    bnd, bnsb, brec, batt, batb = (Buf() for _ in range(5))

                    t_z = VE.emit("memset", ps_nd[:], 0.0)
                    bnd.wr(t_z)
                    t_stc = None
                    git = 0
                    for j in range(NQC):
                        q0 = QW * j
                        Kj = 4 * (j + 1)
                        pend = None

                        def emit_av(pkk, pp, pw, last):
                            TE.self_wait()
                            for h in range(HPG):
                                t_av = TE.emit(
                                    "matmul", ps_nd[32 * h:32 * h + 9, :],
                                    pv_pad[:, (HPG * pkk + h) * 9:(HPG * pkk + h) * 9 + 9],
                                    E_sb[:, pp * HPG * QW + QW * h:pp * HPG * QW + QW * h + QW],
                                    start=(pkk == 0), stop=last,
                                    tile_position=(0, 32 * h), skip_group_check=True,
                                    deps=([pw] + (bnd.wdeps() if pkk == 0 else []))
                                    if h == 0 else [])
                            bnd.wr(t_av)
                            bE[pp].rd(t_av)
                            TE.self_wait()
                            return t_av

                        for kk in range(Kj):
                            par = git % 2
                            for half in range(2):
                                for h in (2 * half, 2 * half + 1):
                                    t_dot = TE.emit(
                                        "matmul", ps_dot[:, QW * h:QW * (h + 1)],
                                        pk_pad[32 * h:32 * h + 32, 128 * kk:128 * (kk + 1)],
                                        pq_pad[32 * h:32 * h + 32, q0:q0 + QW],
                                        start=True, stop=True, tile_position=(32 * h, 0),
                                        deps=(bdh[half].wdeps() if h == 2 * half else []))
                                bdh[half].wr(t_dot)
                            t_exp = None
                            for half in range(2):
                                Ebh = E_sb[:, par * HPG * QW + 2 * QW * half:
                                           par * HPG * QW + 2 * QW * (half + 1)]
                                t_exp = SC.emit(
                                    "activation", Ebh,
                                    ps_dot[:, 2 * QW * half:2 * QW * (half + 1)],
                                    AF.Exp, scale=2.0,
                                    deps=[bdh[half].w] + (bE[par].wdeps() if half == 0 else []))
                                bdh[half].rd(t_exp)
                                if half == 0:
                                    bE[par].wr(t_exp)
                            bE[par].wr(t_exp)
                            dpat = kk - 4 * j
                            if 0 <= dpat <= 3:
                                moff = dpat * HPG * QW
                                t_m = VE.emit("tensor_tensor",
                                              E_sb[:, par * HPG * QW:(par + 1) * HPG * QW],
                                              E_sb[:, par * HPG * QW:(par + 1) * HPG * QW],
                                              msk_sb[:, moff:moff + HPG * QW], ALU.mult,
                                              deps=[t_exp, t_mskl])
                                bE[par].wr(t_m)
                            if pend is not None:
                                emit_av(pend[0], pend[1], pend[2], False)
                            pend = (kk, par, bE[par].w)
                            git += 1
                        t_av = emit_av(pend[0], pend[1], pend[2], True)
                        t_evn = VE.emit("tensor_copy", num_sb[:], ps_nd[:],
                                        deps=[t_av] + bnsb.wdeps())
                        bnd.rd(t_evn)
                        bnsb.wr(t_evn)
                        VE.self_wait()
                        TE.self_wait()
                        t_den = TE.emit("matmul", ps_nd[:], sel_sb[:], num_sb[:],
                                        start=True, stop=True,
                                        deps=[t_evn] + bnd.wdeps())
                        bnsb.rd(t_den)
                        bnd.wr(t_den)
                        TE.self_wait()
                        t_rc = VE.emit("reciprocal", rec_sb[:], ps_nd[:],
                                       deps=[t_den] + brec.wdeps())
                        bnd.rd(t_rc)
                        brec.wr(t_rc)
                        VE.self_wait()
                        t_at = VE.emit("tensor_tensor", att_sb[:], num_sb[:],
                                       rec_sb[:], ALU.mult, deps=batt.wdeps())
                        bnsb.rd(t_at)
                        brec.rd(t_at)
                        batt.wr(t_at)
                        VE.self_wait()
                        t_ab = VE.emit("tensor_scalar", atb_sb[:], att_sb[:],
                                       vd_sb[:, 0:1], ve0_sb[:, 0:1], ALU.mult, ALU.add,
                                       deps=[t_vd] + batb.wdeps())
                        batt.rd(t_ab)
                        batb.wr(t_ab)
                        VE.self_wait()
                        t_stc = GP.emit("dma_start", out=attn_own_d[:, q0:q0 + QW],
                                        in_=atb_sb[:], deps=[t_ab])
                        batb.rd(t_stc)
                    GP.wait(t_stc)
                    nc.gpsimd.collective_compute("AllGather", ALU.bypass,
                                                 [list(range(NCORES))], [attn_own_d],
                                                 [attn_all]).then_inc(sem_ag2, 1)
                    t_ag2 = ("ag2", 1, sem_ag2)
                    if _DEBUG:
                        GP.emit("dma_start", out=d_attn, in_=attn_own_d, deps=[t_stc])
                barrier()
            barrier()

            if _PHASES <= 2:
                return nc
            with ExitStack() as c2:
                E2 = c2.enter_context
                atf = E2(nc.sbuf_tensor("atf", [128, S], BF16))
                atn = E2(nc.sbuf_tensor("atn", [128, 8 * TOK], BF16))
                wo_sb = E2(nc.sbuf_tensor("wo_sb", [128, 2 * 8 * 128], BF16))
                wgt_sb = E2(nc.sbuf_tensor("wgt_sb", [128, NKT * 128], BF16))
                gat_sb = E2(nc.sbuf_tensor("gat_sb", [128, TOK], BF16))
                tmp_sb = E2(nc.sbuf_tensor("tmp_sb", [128, TOK], F32))
                ps_wo = [E2(nc.psum_tensor(f"ps_wo{i}", [128, TOK], F32)) for i in range(2)]
                ps_gt = [E2(nc.psum_tensor(f"ps_gt{i}", [128, TOK], F32)) for i in range(2)]
                bwo = [Buf(), Buf()]
                bgt = [Buf(), Buf()]
                bwos = [Buf(), Buf()]
                bwgs, bgat, batf = Buf(), Buf(), Buf()
                batn = Buf()
                av = attn_all.rearrange("(g p) t -> g p t", p=128)
                for g in range(8):
                    t_af = SY.emit("dma_start", out=atf[:], in_=av[g],
                                   deps=[t_ag2] + batf.wdeps())
                    batf.wr(t_af)
                    dst = atn[:, TOK * g:TOK * (g + 1)]
                    t_a0 = VE.emit("tensor_scalar_mul", dst, atf[:, 0:TOK],
                                   ms_sb[:, 0:1], deps=[t_af, t_msel] + batn.wdeps())
                    for b in range(1, NCORES):
                        VE.self_wait()
                        t_a0 = VE.emit("scalar_tensor_tensor", dst,
                                       atf[:, TOK * b:TOK * (b + 1)], ms_sb[:, b:b + 1],
                                       dst, ALU.mult, ALU.add)
                    VE.self_wait()
                    batn.wr(t_a0)
                    batf.rd(t_a0)
                t_atn = VE.tok()
                for dm in range(NKT):
                    p = dm % 2
                    t_wol = GP.emit("dma_start",
                                    out=wo_sb[:, p * 1024:(p + 1) * 1024],
                                    in_=wopad[dm],
                                    deps=bwos[p].wdeps())
                    bwos[p].wr(t_wol)
                    t_wgl = GP.emit("dma_start",
                                    out=wgt_sb[:],
                                    in_=wgate[dm],
                                    deps=bwgs.wdeps())
                    bwgs.wr(t_wgl)
                    for h in range(2):
                        for g in range(8):
                            t_womm = TE.emit(
                                "matmul", ps_wo[p][:, 512 * h:512 * (h + 1)],
                                wo_sb[:, p * 1024 + 128 * g:p * 1024 + 128 * (g + 1)],
                                atn[:, TOK * g + 512 * h:TOK * g + 512 * (h + 1)],
                                start=(g == 0), stop=(g == 7),
                                deps=[t_wol, t_atn] +
                                     (bwo[p].wdeps() if g == 0 and h == 0 else []))
                    bwo[p].wr(t_womm)
                    bwos[p].rd(t_womm)
                    for h in range(2):
                        for kt in range(NKT):
                            t_gtm = TE.emit(
                                "matmul", ps_gt[p][:, 512 * h:512 * (h + 1)],
                                wgt_sb[:, 128 * kt:128 * (kt + 1)],
                                ln1o[:, TOK * kt + 512 * h:TOK * kt + 512 * (h + 1)],
                                start=(kt == 0), stop=(kt == NKT - 1),
                                deps=[t_wgl] +
                                     (bgt[p].wdeps() if kt == 0 and h == 0 else []))
                    bgt[p].wr(t_gtm)
                    bwgs.rd(t_gtm)
                    t_sg = SC.emit("activation", gat_sb[:], ps_gt[p][:], AF.Sigmoid,
                                   deps=[t_gtm] + bgat.wdeps())
                    bgt[p].rd(t_sg)
                    bgat.wr(t_sg)
                    t_t = VE.emit("tensor_tensor", tmp_sb[:], ps_wo[p][:], gat_sb[:],
                                  ALU.mult, deps=[t_sg, t_womm])
                    bwo[p].rd(t_t)
                    bgat.rd(t_t)
                    VE.self_wait()
                    VE.emit("tensor_add", xT[:, TOK * dm:TOK * (dm + 1)],
                            xT[:, TOK * dm:TOK * (dm + 1)], tmp_sb[:])
                    VE.self_wait()
                if _DEBUG:
                    GP.emit("dma_start", out=d_xmid, in_=xT[:], deps=[VE.tok()])
                    GP.emit("dma_start", out=d_atn, in_=atn[:], deps=[t_atn])
            barrier()
        barrier()

        if _PHASES <= 3:
            return nc
        with ExitStack() as cC:
            EC = cC.enter_context
            idx_sb = EC(nc.sbuf_tensor("idx_sb", [128, NE * TCAP // 128],
                                       mybir.dt.int32))
            idx2_sb = EC(nc.sbuf_tensor("idx2_sb", [128, 2 * NE * TCAP // 128],
                                        mybir.dt.int32))
            cD = EC(ExitStack())
            ln2o = cD.enter_context(nc.sbuf_tensor("ln2o", [128, NKT * TOK], BF16))
            combT = cD.enter_context(nc.sbuf_tensor("combT", [NE, TOK], BF16))

            with ExitStack() as c2:
                E2 = c2.enter_context
                gw_sb = E2(nc.sbuf_tensor("gw_sb", [128, NKT * NE], F32))
                ps_sc = E2(nc.psum_tensor("ps_sc", [NE, TOK], F32))
                sc_sb = E2(nc.sbuf_tensor("sc_sb", [NE, TOK], F32))
                t_gwl = SY.emit("dma_start", out=gw_sb[:], in_=gate_w)
                t_scm = [None]

                def consume(kt, lbuf, t_n):
                    for h in range(2):
                        t_scm[0] = TE.emit("matmul", ps_sc[:, 512 * h:512 * (h + 1)],
                                           gw_sb[:, NE * kt:NE * (kt + 1)],
                                           lbuf[:, 512 * h:512 * (h + 1)],
                                           start=(kt == 0), stop=(kt == NKT - 1),
                                           deps=[t_n, t_gwl])
                    return t_scm[0]

                layernorm(ln2o, fp32_consumer=consume, name="l2")
                t_sig = SC.emit("activation", sc_sb[:], ps_sc[:], AF.Sigmoid,
                                deps=[t_scm[0]])
                sct = E2(nc.sbuf_tensor("sct", [128, NE], F32))
                work = E2(nc.sbuf_tensor("work", [128, NE], F32))
                tmq = E2(nc.sbuf_tensor("tmq", [128, NE], F32))
                msum = E2(nc.sbuf_tensor("msum", [128, 8], F32))
                comb = E2(nc.sbuf_tensor("comb", [128, NE], F32))
                ps_t1 = E2(nc.psum_tensor("ps_t1", [128, NE], F32))
                ps_t2 = E2(nc.psum_tensor("ps_t2", [NE, 128], F32))
                bt1, bt2 = Buf(), Buf()
                t_cstore = None
                for tt in range(8):
                    TE.self_wait()
                    t_tr = TE.emit("transpose", ps_t1[:],
                                   sc_sb[:, 128 * tt:128 * (tt + 1)],
                                   id_sb[0:NE, 0:NE], deps=[t_sig] + bt1.wdeps())
                    bt1.wr(t_tr)
                    t_sct = VE.emit("tensor_copy", sct[:], ps_t1[:], deps=[t_tr])
                    bt1.rd(t_sct)
                    VE.self_wait()
                    VE.emit("tensor_copy", work[:], sct[:])
                    VE.emit("memset", msum[:, 0:1], 0.0)
                    VE.self_wait()
                    for it in range(TOPK):
                        VE.emit("reduce_max", msum[:, 1:2], work[:], AX.X)
                        VE.self_wait()
                        VE.emit("tensor_add", msum[:, 0:1], msum[:, 0:1], msum[:, 1:2])
                        if it < TOPK - 1:
                            VE.emit("tensor_scalar", tmq[:], work[:], msum[:, 1:2],
                                    None, ALU.is_equal)
                            VE.self_wait()
                            VE.emit("scalar_tensor_tensor", work[:], tmq[:], -100.0,
                                    work[:], ALU.mult, ALU.add)
                            VE.self_wait()
                    VE.self_wait()
                    VE.emit("tensor_scalar", tmq[:], sct[:], msum[:, 1:2], None,
                            ALU.is_ge)
                    VE.emit("tensor_scalar_add", msum[:, 2:3], msum[:, 0:1], 1e-20)
                    VE.self_wait()
                    VE.emit("reciprocal", msum[:, 3:4], msum[:, 2:3])
                    VE.self_wait()
                    VE.emit("tensor_scalar", comb[:], sct[:], msum[:, 3:4], None,
                            ALU.mult, deps=([t_cstore] if tt > 0 else []))
                    VE.self_wait()
                    t_cb = VE.emit("tensor_tensor", comb[:], comb[:], tmq[:], ALU.mult)
                    VE.self_wait()
                    t_cstore = GP.emit("dma_start",
                                       out=comb_tok_d[128 * tt:128 * (tt + 1), :],
                                       in_=comb[:], deps=[t_cb])
                    bcombrow.rd(t_cstore) if False else None
                    TE.self_wait()
                    t_tr2 = TE.emit("transpose", ps_t2[:], comb[:], id_sb[:],
                                    deps=[t_cb] + bt2.wdeps())
                    bt2.wr(t_tr2)
                    t_cT = VE.emit("tensor_copy", combT[0:NE, 128 * tt:128 * (tt + 1)],
                                   ps_t2[:], deps=[t_tr2])
                    bt2.rd(t_cT)
                    VE.self_wait()
                if _DEBUG:
                    GP.emit("dma_start", out=d_ln2, in_=ln2o[:], deps=[VE.tok()])
                    GP.emit("dma_start", out=d_comb, in_=combT[:], deps=[VE.tok()])
                ps_tb = E2(nc.psum_tensor("ps_tb", [128, 128], BF16))
                lntt = E2(nc.sbuf_tensor("lntt", [128, D], BF16))
                idb2 = E2(nc.sbuf_tensor("idb2", [128, 128], BF16))
                t_idb = VE.emit("tensor_copy", idb2[:], id_sb[:])
                VE.self_wait()
                btb, blntt = Buf(), Buf()
                for tt in range(8):
                    for kt in range(NKT):
                        TE.self_wait()
                        t_tr = TE.emit("transpose", ps_tb[:],
                                       ln2o[:, TOK * kt + 128 * tt:TOK * kt + 128 * (tt + 1)],
                                       idb2[:], deps=[t_idb] + btb.wdeps())
                        btb.wr(t_tr)
                        t_ev = VE.emit("tensor_copy", lntt[:, 128 * kt:128 * (kt + 1)],
                                       ps_tb[:], deps=[t_tr] + (blntt.wdeps() if kt == 0 else []))
                        btb.rd(t_ev)
                        VE.self_wait()
                    t_lnst = GP.emit("dma_start", out=ln2tok_d[128 * tt:128 * (tt + 1), :],
                                     in_=lntt[:], deps=[VE.tok()])
                    blntt.wr(t_lnst) if False else None
                    blntt.rd(t_lnst)
                t_ln2tok = t_lnst
                msk = E2(nc.sbuf_tensor("msk", [NE, TOK], F32))
                zsc = E2(nc.sbuf_tensor("zsc", [NE, TOK], F32))
                posx = E2(nc.sbuf_tensor("posx", [NE, TOK], F32))
                post_i = E2(nc.sbuf_tensor("post_i", [128, 8 * NE], mybir.dt.int32))
                tokid_sb = E2(nc.sbuf_tensor("tokid_sb", [128, 8], mybir.dt.int32))
                idxf = E2(nc.sbuf_tensor("idxf", [NA, 128], mybir.dt.int32))
                t_tid = SY.emit("dma_start", out=tokid_sb[:], in_=tokid)
                eoff_sb = E2(nc.sbuf_tensor("eoff_sb", [NE, 1], F32))
                t_eo = SY.emit("dma_start", out=eoff_sb[:], in_=eoff)
                VE.emit("tensor_scalar", msk[:], combT[:], 0.0, None, ALU.is_gt,
                        deps=[t_cT])
                VE.emit("memset", zsc[:], 0.0)
                VE.self_wait()
                VE.emit("tensor_tensor_scan", posx[:], msk[:], zsc[:], 0.0,
                        ALU.add, ALU.add)
                VE.self_wait()
                VE.emit("tensor_scalar_add", posx[:], posx[:], -1.0)
                VE.self_wait()
                VE.emit("tensor_scalar", zsc[:], msk[:], -1.0, 1.0, ALU.mult, ALU.add)
                VE.self_wait()
                VE.emit("scalar_tensor_tensor", posx[:], zsc[:], 1.0e6,
                        posx[:], ALU.mult, ALU.add)
                VE.self_wait()
                VE.emit("tensor_scalar", zsc[:], posx[:], float(TCAP) - 0.5, None,
                        ALU.is_ge)
                VE.self_wait()
                VE.emit("scalar_tensor_tensor", posx[:], zsc[:], 1.0e6,
                        posx[:], ALU.mult, ALU.add)
                VE.self_wait()
                t_posx = VE.emit("tensor_scalar_add", posx[:], posx[:],
                                 eoff_sb[:, 0:1], deps=[t_eo])
                VE.self_wait()
                t_if = VE.emit("memset", idxf[:], 100000)
                VE.self_wait()
                t_ifd = GP.emit("dma_start",
                                out=idx_d.rearrange("e (a p) -> (e a) p", p=128),
                                in_=idxf[:], deps=[t_if])
                t_scat = None
                for tt in range(8):
                    TE.self_wait()
                    t_trp = TE.emit("transpose", ps_t1[:],
                                    posx[:, 128 * tt:128 * (tt + 1)],
                                    id_sb[0:NE, 0:NE], deps=[t_posx] + bt1.wdeps())
                    bt1.wr(t_trp)
                    t_pi = VE.emit("tensor_copy", post_i[:, NE * tt:NE * (tt + 1)],
                                   ps_t1[:], deps=[t_trp])
                    bt1.rd(t_pi)
                    VE.self_wait()
                    for e in range(NE):
                        t_scat = GP.emit("indirect_dma_start",
                                         idx_d.rearrange("e (t o) -> (e t) o", o=1),
                                         IndirectOffsetOnAxis(
                                             post_i[:, NE * tt + e:NE * tt + e + 1], 0),
                                         tokid_sb[:, tt:tt + 1], None,
                                         bounds_check=r_btcap, oob_is_err=False,
                                         deps=[t_pi, t_tid, t_ifd])
                for e in range(NE):
                    for a in range(A):
                        t_idxload = GP.emit(
                            "dma_start", out=idx_sb[:, A * e + a:A * e + a + 1],
                            in_=idx_d[e:e + 1, 128 * a:128 * (a + 1)].rearrange(
                                "e p -> p e"),
                            deps=[t_scat])
                if _DEBUG:
                    GP.emit("dma_start", out=d_idx, in_=idx_d, deps=[t_scat])
                for dmh in range(2):
                    VE.emit("tensor_scalar", idx2_sb[:, NA * dmh:NA * (dmh + 1)],
                            idx_sb[:], 2, dmh, ALU.mult, ALU.add, deps=[t_idxload])
                VE.self_wait()
                t_idx2 = VE.tok()
            barrier()

            if _PHASES <= 4:
                return nc
            if _PHASES <= 5:
                return nc
            with ExitStack() as c2:
                E2 = c2.enter_context
                shact = E2(nc.sbuf_tensor("shact", [128, 12 * TOK], BF16))
                wsh_sb = E2(nc.sbuf_tensor("wsh_sb", [128, 4 * NKT * 128], BF16))
                wsd_sb = E2(nc.sbuf_tensor("wsd_sb", [128, 2 * 12 * 128], BF16))
                silu2 = E2(nc.sbuf_tensor("silu2", [128, TOK], BF16))
                ps_a = E2(nc.psum_tensor("ps_a2", [128, TOK], F32))
                ps_b = E2(nc.psum_tensor("ps_b2", [128, TOK], F32))
                ps_c = E2(nc.psum_tensor("ps_c2", [128, TOK], F32))
                ps_d = E2(nc.psum_tensor("ps_d2", [128, TOK], F32))
                bpa, bpb, bpc, bpd = Buf(), Buf(), Buf(), Buf()
                bwsh = [Buf(), Buf()]
                bwsd = [Buf(), Buf()]
                bsilu, bsh = Buf(), Buf()
                bxt = [Buf() for _ in range(NKT)]
                for m in range(12):
                    pbuf = m % 2
                    woff = pbuf * 2 * NKT * 128
                    t_lg = GP.emit("dma_start",
                                   out=wsh_sb[:, woff:woff + NKT * 128],
                                   in_=sgm[m],
                                   deps=bwsh[pbuf].wdeps())
                    t_lu = GP.emit("dma_start",
                                   out=wsh_sb[:, woff + NKT * 128:woff + 2 * NKT * 128],
                                   in_=sum_m[m])
                    bwsh[pbuf].wr(t_lu)
                    pg, bg = (ps_a, bpa) if m % 2 == 0 else (ps_b, bpb)
                    pu, bu = (ps_c, bpc) if m % 2 == 0 else (ps_d, bpd)
                    for h in range(2):
                        for kt in range(NKT):
                            t_g = TE.emit(
                                "matmul", pg[:, 512 * h:512 * (h + 1)],
                                wsh_sb[:, woff + 128 * kt:woff + 128 * (kt + 1)],
                                ln2o[:, TOK * kt + 512 * h:TOK * kt + 512 * (h + 1)],
                                start=(kt == 0), stop=(kt == NKT - 1),
                                deps=[t_lg] + (bg.wdeps() if kt == 0 and h == 0 else []))
                    bg.wr(t_g)
                    bwsh[pbuf].rd(t_g)
                    for h in range(2):
                        for kt in range(NKT):
                            t_u = TE.emit(
                                "matmul", pu[:, 512 * h:512 * (h + 1)],
                                wsh_sb[:, woff + NKT * 128 + 128 * kt:
                                       woff + NKT * 128 + 128 * (kt + 1)],
                                ln2o[:, TOK * kt + 512 * h:TOK * kt + 512 * (h + 1)],
                                start=(kt == 0), stop=(kt == NKT - 1),
                                deps=[t_lu] + (bu.wdeps() if kt == 0 and h == 0 else []))
                    bu.wr(t_u)
                    bwsh[pbuf].rd(t_u)
                    t_si = SC.emit("activation", silu2[:], pg[:], AF.Silu,
                                   deps=[t_g] + bsilu.wdeps())
                    bg.rd(t_si)
                    bsilu.wr(t_si)
                    a = shact[:, TOK * m:TOK * (m + 1)]
                    t_a1 = VE.emit("tensor_tensor", a, silu2[:], pu[:], ALU.mult,
                                   deps=[t_si, t_u])
                    bu.rd(t_a1)
                    bsilu.rd(t_a1)
                    bsh.wr(t_a1)
                    VE.self_wait()
                for dm in range(NKT):
                    pbuf = dm % 2
                    woff = pbuf * 12 * 128
                    t_lw = GP.emit("dma_start",
                                   out=wsd_sb[:, woff:woff + 12 * 128],
                                   in_=sdm[dm],
                                   deps=bwsd[pbuf].wdeps())
                    bwsd[pbuf].wr(t_lw)
                    pr, br = (ps_a, bpa) if dm % 2 == 0 else (ps_b, bpb)
                    for h in range(2):
                        for f in range(12):
                            t_r = TE.emit(
                                "matmul", pr[:, 512 * h:512 * (h + 1)],
                                wsd_sb[:, woff + 128 * f:woff + 128 * (f + 1)],
                                shact[:, TOK * f + 512 * h:TOK * f + 512 * (h + 1)],
                                start=(f == 0), stop=(f == 11),
                                deps=[t_lw, bsh.w] +
                                     (br.wdeps() if f == 0 and h == 0 else []))
                        br.wr(t_r)
                    bwsd[pbuf].rd(t_r)
                    bsh.rd(t_r)
                    t_add = VE.emit("tensor_add", xT[:, TOK * dm:TOK * (dm + 1)],
                                    xT[:, TOK * dm:TOK * (dm + 1)], pr[:],
                                    deps=[t_r] + bxt[dm].wdeps())
                    br.rd(t_add)
                    bxt[dm].wr(t_add)
                    VE.self_wait()
            barrier()

            cD.close()
            barrier()
            if _PHASES <= 6:
                return nc
            with ExitStack() as c2:
                E2 = c2.enter_context
                xg = [E2(nc.sbuf_tensor(f"xg{i}", [128, NKT * TCAP], BF16))
                      for i in range(2)]
                actT = [E2(nc.sbuf_tensor(f"actT{i}", [128, 6 * TCAP], BF16))
                        for i in range(2)]
                gst = [E2(nc.sbuf_tensor(f"gst{i}", [128, A * D], BF16))
                       for i in range(2)]
                cbg = [E2(nc.sbuf_tensor(f"cbg{i}", [128, A * NE], F32))
                       for i in range(2)]
                wgu_sb = E2(nc.sbuf_tensor("wgu_sb", [128, 4 * NKT * 128], BF16))
                wdd_sb = E2(nc.sbuf_tensor("wdd_sb", [128, 6 * D], BF16))
                silu_sb = E2(nc.sbuf_tensor("silu_sb", [128, TCAP], BF16))
                rout = [E2(nc.sbuf_tensor(f"rout{i}", [128, 1024], F32))
                        for i in range(4)]
                ps_g = [E2(nc.psum_tensor(f"ps_g{i}", [128, TCAP], F32)) for i in range(2)]
                ps_u = [E2(nc.psum_tensor(f"ps_u{i}", [128, TCAP], F32)) for i in range(2)]
                ps_r = [E2(nc.psum_tensor(f"ps_r{i}", [128, 1024], F32)) for i in range(2)]
                bxg = [Buf(), Buf()]
                bact = [Buf(), Buf()]
                bgst = [Buf(), Buf()]
                bstg = [Buf(), Buf()]
                bcbg = [Buf(), Buf()]
                bwgu = [Buf(), Buf()]
                bwdd = [Buf(), Buf()]
                bpg = [Buf(), Buf()]
                bpu = [Buf(), Buf()]
                bpr = [Buf(), Buf()]
                brt = [Buf(), Buf(), Buf(), Buf()]
                bracc = [Buf(), Buf(), Buf(), Buf()]
                bsilu = Buf()
                for i in range(4):
                    bracc[i].wr(t_rz)
                def prefetch(e):
                    buf = e % 2
                    t_stg = None
                    for a in range(A):
                        ga = gst[buf][:, D * a:D * (a + 1)]
                        t_ga = GP.emit("indirect_dma_start", ga, None, ln2tok_d,
                                       IndirectOffsetOnAxis(
                                           idx_sb[:, A * e + a:A * e + a + 1], 0),
                                       bounds_check=r_btok, oob_is_err=False,
                                       deps=[t_idxload, t_ln2tok] +
                                       (bgst[buf].wdeps() if a == 0 else []))
                        bgst[buf].wr(t_ga)
                        t_stg = GP.emit("dma_start",
                                        out=stage_d[buf, 128 * a:128 * (a + 1), :],
                                        in_=ga, deps=[t_ga] +
                                        (bstg[buf].wdeps() if a == 0 else []))
                        bgst[buf].rd(t_stg)
                    bstg[buf].wr(t_stg)
                    t_cg = None
                    for a in range(A):
                        t_cg = GP.emit("indirect_dma_start",
                                       cbg[buf][:, NE * a:NE * (a + 1)], None,
                                       comb_tok_d,
                                       IndirectOffsetOnAxis(
                                           idx_sb[:, A * e + a:A * e + a + 1], 0),
                                       bounds_check=r_btok, oob_is_err=False,
                                       deps=[t_cstore] + (bcbg[buf].wdeps() if a == 0 else []))
                    bcbg[buf].wr(t_cg)
                    t_xg = None
                    for kt in range(NKT):
                        t_xg = SY.emit("dma_start",
                                       out=xg[buf][:, TCAP * kt:TCAP * (kt + 1)],
                                       in_=stage_d[buf, :, 128 * kt:128 * (kt + 1)],
                                       transpose=True, inc=16,
                                       deps=[t_stg] + (bxg[buf].wdeps() if kt == 0 else []))
                    t_xg_sy = SY.tok()
                    bxg[buf].wr(t_xg_sy)
                    bstg[buf].rd(t_xg_sy)
                    return t_xg_sy, t_cg

                pf = {0: prefetch(0)}
                for e in range(NE):
                    buf = e % 2
                    abuf = e % 2
                    if e + 1 < NE:
                        pf[e + 1] = prefetch(e + 1)
                    t_xg_sy, t_cg = pf.pop(e)
                    if _DEBUG and e == 0:
                        GP.emit("dma_start", out=d_xg, in_=xg[0][:],
                                deps=[t_xg_sy])
                    for m in range(6):
                        pb = m % 2
                        woff = pb * 2 * NKT * 128
                        t_lg = GP.emit("dma_start",
                                       out=wgu_sb[:, woff:woff + NKT * 128],
                                       in_=wgm[e, m],
                                       deps=bwgu[pb].wdeps())
                        t_lu = GP.emit("dma_start",
                                       out=wgu_sb[:, woff + NKT * 128:woff + 2 * NKT * 128],
                                       in_=wum[e, m])
                        bwgu[pb].wr(t_lu)
                        for kt in range(NKT):
                            t_g = TE.emit(
                                "matmul", ps_g[pb][:],
                                wgu_sb[:, woff + 128 * kt:woff + 128 * (kt + 1)],
                                xg[buf][:, TCAP * kt:TCAP * (kt + 1)],
                                start=(kt == 0), stop=(kt == NKT - 1),
                                deps=[t_lg, t_xg_sy] +
                                     (bpg[pb].wdeps() if kt == 0 else []))
                        bpg[pb].wr(t_g)
                        bwgu[pb].rd(t_g)
                        bxg[buf].rd(t_g)
                        for kt in range(NKT):
                            t_u = TE.emit(
                                "matmul", ps_u[pb][:],
                                wgu_sb[:, woff + NKT * 128 + 128 * kt:
                                       woff + NKT * 128 + 128 * (kt + 1)],
                                xg[buf][:, TCAP * kt:TCAP * (kt + 1)],
                                start=(kt == 0), stop=(kt == NKT - 1),
                                deps=[t_lu] + (bpu[pb].wdeps() if kt == 0 else []))
                        bpu[pb].wr(t_u)
                        bwgu[pb].rd(t_u)
                        bxg[buf].rd(t_u)
                        t_si = SC.emit("activation", silu_sb[:], ps_g[pb][:], AF.Silu,
                                       deps=[t_g] + bsilu.wdeps())
                        bpg[pb].rd(t_si)
                        bsilu.wr(t_si)
                        a_ = actT[abuf][:, TCAP * m:TCAP * (m + 1)]
                        t_a1 = VE.emit("tensor_tensor", a_, silu_sb[:], ps_u[pb][:],
                                       ALU.mult,
                                       deps=[t_si, t_u] + (bact[abuf].wdeps() if m == 0 else []))
                        bpu[pb].rd(t_a1)
                        bsilu.rd(t_a1)
                        VE.self_wait()
                    bact[abuf].wr(VE.tok())
                    t_lw = None
                    for m in range(6):
                        t_lw = GP.emit("dma_start",
                                       out=wdd_sb[:, D * m:D * (m + 1)],
                                       in_=wdp[e, 128 * m:128 * (m + 1), :],
                                       deps=(bwdd[0].wdeps() if m == 0 else []))
                    bwdd[0].wr(t_lw)
                    for a in range(A):
                        for dmh in range(2):
                            pb = (a * 2 + dmh) % 2
                            pr = ps_r[pb]
                            first = True
                            for m in range(6):
                                for ns in range(2):
                                    t_r = TE.emit(
                                        "matmul", pr[:, 512 * ns:512 * (ns + 1)],
                                        actT[abuf][:, TCAP * m + 128 * a:
                                                   TCAP * m + 128 * (a + 1)],
                                        wdd_sb[:, D * m + 1024 * dmh + 512 * ns:
                                               D * m + 1024 * dmh + 512 * (ns + 1)],
                                        start=(m == 0), stop=(m == 5),
                                        skip_group_check=True,
                                        deps=[t_lw, bact[abuf].w] +
                                             (bpr[pb].wdeps() if first else []))
                                    first = False
                            bpr[pb].wr(t_r)
                            bwdd[0].rd(t_r)
                            bact[abuf].rd(t_r)
                            rb = a * 2 + dmh
                            t_ev = VE.emit("tensor_scalar", rout[rb][:], pr[:],
                                           cbg[buf][:, NE * a + e:NE * a + e + 1],
                                           None, ALU.mult,
                                           deps=[t_r, t_cg] + brt[rb].wdeps())
                            bpr[pb].rd(t_ev)
                            bcbg[buf].rd(t_ev)
                            brt[rb].wr(t_ev)
                            VE.self_wait()
                            acc = e % 4
                            t_sc2 = GP.emit(
                                "indirect_dma_start",
                                racc[acc].rearrange("t (h c) -> (t h) c", h=2),
                                IndirectOffsetOnAxis(
                                    idx2_sb[:, NA * dmh + A * e + a:
                                            NA * dmh + A * e + a + 1], 0),
                                rout[rb][:], None,
                                bounds_check=r_b2tok, oob_is_err=False,
                                compute_op=ALU.add,
                                deps=[t_ev, t_idx2] + bracc[acc].wdeps())
                            bracc[acc].wr(t_sc2)
                            brt[rb].rd(t_sc2)
            barrier()
        barrier()

        if _PHASES <= 6:
            return nc
        with ExitStack() as c2:
            E2 = c2.enter_context
            out_sb = E2(nc.sbuf_tensor("out_sb", [128, D], F32))
            rc_sb = [E2(nc.sbuf_tensor(f"rc_sb{i}", [128, D], F32)) for i in range(4)]
            ps_t = [E2(nc.psum_tensor(f"ps_to{i}", [128, 128], F32)) for i in range(2)]
            bt = [Buf(), Buf()]
            bout = Buf()
            brc = [Buf() for _ in range(4)]
            for tt in range(8):
                t_rl = []
                for i in range(4):
                    t_rl.append(SY.emit("dma_start", out=rc_sb[i][:],
                                        in_=racc[i][128 * tt:128 * (tt + 1), :],
                                        deps=brc[i].wdeps()))
                for kt in range(NKT):
                    p = kt % 2
                    TE.self_wait()
                    t_tr = TE.emit("transpose", ps_t[p][:],
                                   xT[:, TOK * kt + 128 * tt:TOK * kt + 128 * (tt + 1)],
                                   id_sb[:], deps=bt[p].wdeps())
                    bt[p].wr(t_tr)
                    t_cp = VE.emit("tensor_copy", out_sb[:, 128 * kt:128 * (kt + 1)],
                                   ps_t[p][:],
                                   deps=[t_tr] + (bout.wdeps() if kt == 0 else []))
                    bt[p].rd(t_cp)
                    VE.self_wait()
                for i in range(4):
                    VE.emit("tensor_add", out_sb[:], out_sb[:], rc_sb[i][:],
                            deps=[t_rl[i]])
                    VE.self_wait()
                    brc[i].rd(VE.tok())
                t_all = VE.tok()
                t_st = GP.emit("dma_start", out=y[128 * tt:128 * (tt + 1), :],
                               in_=out_sb[:], deps=[t_all])
                bout.wr(t_st)
            GP.self_wait()
        barrier()
    return nc


def _pad_heads(w, hg):
    out = np.zeros((w.shape[0], 128), np.float32)
    for j in range(HPG):
        out[:, 32 * j:32 * j + 8] = w[:, 8 * (HPG * hg + j):8 * (HPG * hg + j) + 8]
    return out


def kernel(L0_attn_hidden_states, L0_attn_Wq, L0_attn_Wk, L0_attn_Wv, L0_attn_Wo,
           L0_attn_Wg, L0_attn_v_emb0, L0_attn_v_emb1, L0_ffn_gate_w, L0_ffn_wg,
           L0_ffn_wu, L0_ffn_wd, L0_ffn_sg, L0_ffn_su, L0_ffn_sd):
    f32 = np.float32
    bf = ml_dtypes.bfloat16
    x = np.asarray(L0_attn_hidden_states, f32)[0]
    Wq = np.asarray(L0_attn_Wq, f32)
    Wk = np.asarray(L0_attn_Wk, f32)
    Wv = np.asarray(L0_attn_Wv, f32)
    Wo = np.asarray(L0_attn_Wo, f32)
    Wg = np.asarray(L0_attn_Wg, f32)
    ve0 = np.asarray(L0_attn_v_emb0, f32)
    ve1 = np.asarray(L0_attn_v_emb1, f32)
    gate_w = np.asarray(L0_ffn_gate_w, f32)
    wg = np.asarray(L0_ffn_wg, f32)
    wu = np.asarray(L0_ffn_wu, f32)
    wd = np.asarray(L0_ffn_wd, f32)
    sg = np.asarray(L0_ffn_sg, f32)
    su = np.asarray(L0_ffn_su, f32)
    sd = np.asarray(L0_ffn_sd, f32)

    def _pmaj(w, nco):
        return np.ascontiguousarray(
            w.reshape(NKT, 128, nco, w.shape[1] // nco).transpose(2, 1, 0, 3)
            .reshape(nco, 128, NKT * (w.shape[1] // nco)))

    xT = np.ascontiguousarray(x.T)
    wopad = np.zeros((16, 128, 8 * 128), f32)
    for g in range(8):
        for j in range(HPG):
            src = Wo[8 * (HPG * g + j):8 * (HPG * g + j) + 8]
            for dm in range(16):
                wopad[dm, 32 * j:32 * j + 8, 128 * g:128 * (g + 1)] = \
                    src[:, 128 * dm:128 * (dm + 1)]
    wopad = wopad.astype(bf)
    wgate = _pmaj(Wg, 16).astype(bf)
    wgm = np.ascontiguousarray(
        wg.reshape(NE, NKT, 128, 6, 128).transpose(0, 3, 2, 1, 4)
        .reshape(NE, 6, 128, NKT * 128)).astype(bf)
    wum = np.ascontiguousarray(
        wu.reshape(NE, NKT, 128, 6, 128).transpose(0, 3, 2, 1, 4)
        .reshape(NE, 6, 128, NKT * 128)).astype(bf)
    wd = wd.astype(bf)
    tokid_h = (np.arange(128)[:, None] + 128 * np.arange(8)[None, :]).astype(np.int32)
    eoff_h = (TCAP * np.arange(NE)[:, None]).astype(np.float32)
    sgm = _pmaj(sg, 12).astype(bf)
    sum_m = _pmaj(su, 12).astype(bf)
    sdm = np.ascontiguousarray(
        sd.reshape(12, 128, 16, 128).transpose(2, 1, 0, 3)
        .reshape(16, 128, 12 * 128)).astype(bf)
    gate_w = np.ascontiguousarray(
        gate_w.reshape(NKT, 128, NE).transpose(1, 0, 2).reshape(128, NKT * NE))
    ident = np.eye(128, dtype=f32)
    ones = np.ones((128, 128), f32)
    sel = np.zeros((128, 128), f32)
    bdm = np.zeros((128, HPG), f32)
    padmask = np.zeros((128, 1), f32)
    for j in range(HPG):
        sel[32 * j + 8, 32 * j:32 * j + 32] = 1.0
        bdm[32 * j:32 * j + 8, j] = 1.0
        padmask[32 * j:32 * j + 8, 0] = 1.0
    cbsel = np.zeros((NE, NE * 128), f32)
    for e in range(NE):
        cbsel[e, 128 * e:128 * (e + 1)] = 1.0
    masks = np.zeros((128, 4 * HPG * QW), f32)
    qr = np.arange(QW)
    kr = np.arange(128)[:, None]
    for p in range(4):
        m = (128 * p + kr <= qr[None, :]).astype(f32)
        for h in range(HPG):
            masks[:, p * HPG * QW + QW * h:p * HPG * QW + QW * (h + 1)] = m

    shared = {
        "wopad": wopad, "wgate": wgate, "gate_w": gate_w, "wgm": wgm, "wum": wum,
        "wdp": wd, "sgm": sgm, "sum_m": sum_m, "sdm": sdm, "ident": ident,
        "tokid": tokid_h, "eoff": eoff_h,
        "ones": ones, "sel": sel, "bd": bdm.astype(bf), "padmask": padmask,
        "cbsel": cbsel.astype(bf), "masks": masks.astype(bf),
    }
    in_maps = []
    for c in range(NCORES):
        ve0p = np.zeros((128, 1), f32)
        ve1p = np.zeros((128, 1), f32)
        for j in range(HPG):
            ve0p[32 * j:32 * j + 8, 0] = ve0[8 * (HPG * c + j):8 * (HPG * c + j) + 8]
            ve1p[32 * j:32 * j + 8, 0] = ve1[8 * (HPG * c + j):8 * (HPG * c + j) + 8]
        mselc = np.zeros((128, NCORES), f32)
        mselc[:, c] = 1.0
        m = dict(shared)
        xc = xT[:, TOK * c:TOK * (c + 1)]
        m.update({
            "xT_own": np.ascontiguousarray(
                xc.reshape(NKT, 128, TOK).transpose(1, 0, 2).reshape(128, NKT * TOK)),
            "wqp": np.ascontiguousarray(
                _pad_heads(Wq, c).reshape(NKT, 128, 128).transpose(1, 0, 2)
                .reshape(128, NKT * 128)).astype(bf),
            "wkp": np.ascontiguousarray(
                _pad_heads(Wk, c).reshape(NKT, 128, 128).transpose(1, 0, 2)
                .reshape(128, NKT * 128)).astype(bf),
            "wv": np.ascontiguousarray(
                Wv[:, 32 * c:32 * (c + 1)].reshape(NKT, 128, 32).transpose(1, 0, 2)
                .reshape(128, NKT * 32)).astype(bf),
            "vemb0p": ve0p, "vemb1p": ve1p, "msel": mselc,
        })
        in_maps.append(m)

    nc = build_program()
    res = run_bass_kernel_spmd(nc, in_maps, core_ids=list(range(NCORES)))
    out = np.concatenate([res.results[c]["y"] for c in range(NCORES)], axis=0)
    return out.reshape(1, S, D).astype(np.float32)
